# version 18
# baseline (speedup 1.0000x reference)
import numpy as np
import ml_dtypes
from contextlib import ExitStack

import concourse.bass as bass
import concourse.mybir as mybir
from concourse.bass_utils import run_bass_kernel_spmd

F32 = mybir.dt.float32
BF16 = mybir.dt.bfloat16
I32 = mybir.dt.int32
AF = mybir.ActivationFunctionType
ALU = mybir.AluOpType
AX = mybir.AxisListType
BFNP = ml_dtypes.bfloat16

D = 1024
TOK = 4096
TILE = 512
NT = TOK // TILE
G = TILE // 128
DC = D // 128
H = 16
DH = 64
FFH = 4096
HC = FFH // 128
NEG = -1.0e30
SEQ_TILES = 2048 // TILE

ENGS = ("sync", "tensor", "vector", "scalar", "gpsimd")


class Prog:

    def __init__(self, cnt):
        self.ops = {e: [] for e in ENGS}
        self.cnt = cnt
        self.last_wait = {e: {} for e in ENGS}
        self.fence_snap = {}

    def emit(self, eng, fn, waits=(), inc=None, n=1):
        own = _OWN.get(eng)
        merged = dict(self.fence_snap)
        if own in ("V", "S"):
            if inc is None:
                inc = own
            merged[own] = self.cnt.get(own, 0)
        for s, v in waits:
            if v > merged.get(s, 0):
                merged[s] = v
        w = []
        for s, v in merged.items():
            if s == "T" and eng == "tensor":
                continue
            if v > self.last_wait[eng].get(s, 0):
                self.last_wait[eng][s] = v
                w.append((s, v))
        self.ops[eng].append((fn, w, inc, n))
        if inc is not None:
            self.cnt[inc] = self.cnt.get(inc, 0) + n
            return self.cnt[inc]
        return None

    def fence(self):
        self.fence_snap = {s: self.cnt.get(s, 0) for s in ("T", "V", "S")}

    def replay(self, block, sems):
        for ename in ENGS:
            ops = self.ops[ename]

            def body(eng, ops=ops):
                for fn, w, inc, n in ops:
                    for s, v in w:
                        eng.wait_ge(sems[s], v)
                    instr = fn(eng)
                    if inc is not None:
                        instr.then_inc(sems[inc], n)
            getattr(block, ename)(body)


_OWN = {"tensor": "T", "vector": "V", "scalar": "S"}


DBG = False


def build_program(dbg=False):
    nc = bass.Bass("TRN2", target_bir_lowering=False, detect_race_conditions=False)

    d_x = nc.dram_tensor("x", [TOK, D], F32, kind="ExternalInput").ap()
    d_gamma = nc.dram_tensor("gamma", [D], F32, kind="ExternalInput").ap()
    d_beta = nc.dram_tensor("beta", [D], F32, kind="ExternalInput").ap()
    d_wqkv = nc.dram_tensor("wqkv", [D, 3 * D], F32, kind="ExternalInput").ap()
    d_wout = nc.dram_tensor("wout", [D, D], F32, kind="ExternalInput").ap()
    d_wfc = nc.dram_tensor("wfc", [FFH, D], F32, kind="ExternalInput").ap()
    d_bfc = nc.dram_tensor("bfc", [FFH], F32, kind="ExternalInput").ap()
    d_afc = nc.dram_tensor("afc", [16, D], F32, kind="ExternalInput").ap()
    d_bfcB = nc.dram_tensor("bfcB", [FFH, 16], F32, kind="ExternalInput").ap()
    d_wpr = nc.dram_tensor("wpr", [D, FFH], F32, kind="ExternalInput").ap()
    d_bpr = nc.dram_tensor("bpr", [D], F32, kind="ExternalInput").ap()
    d_apr = nc.dram_tensor("apr", [16, FFH], F32, kind="ExternalInput").ap()
    d_bprB = nc.dram_tensor("bprB", [D, 16], F32, kind="ExternalInput").ap()
    d_identf = nc.dram_tensor("identf", [128, 128], F32, kind="ExternalInput").ap()
    d_mask = nc.dram_tensor("maskm", [128, 256], BF16, kind="ExternalInput").ap()
    d_onesf = nc.dram_tensor("onesf", [1, 128], F32, kind="ExternalInput").ap()
    d_onesb = nc.dram_tensor("onesb", [1, 128], BF16, kind="ExternalInput").ap()

    d_y = nc.dram_tensor("y", [TOK, D], F32, kind="ExternalOutput").ap()
    d_dbg = {}
    if dbg:
        for name, shape, dt in [("wall", [128, 24576], BF16), ("bw", [128, 16], F32),
                                ("bwv", [1, 1024], F32), ("scl", [128, 8], F32),
                                ("btfc", [16, FFH], BF16), ("a2fc", [16, D], BF16),
                                ("stat", [128, 64], F32), ("xnfm", [128, 4096], BF16),
                                ("qd", [128, 4096], BF16), ("kd", [128, 4096], BF16),
                                ("vd", [128, 4096], BF16), ("ao", [128, 4096], BF16),
                                ("x1", [128, 4096], F32), ("hd", [128, HC * TILE], BF16),
                                ("wfcT0", [128, 1024], BF16), ("wprT0", [128, 4096], BF16)]:
            d_dbg[name] = nc.dram_tensor("dbg_" + name, shape, dt, kind="ExternalOutput").ap()

    d_wfcT = nc.dram_tensor("wfcT_scr", [FFH, D], BF16, kind="Internal").ap()
    d_wprT = nc.dram_tensor("wprT_scr", [D, FFH], BF16, kind="Internal").ap()
    d_woutb = nc.dram_tensor("wout_scr", [D, D], BF16, kind="Internal").ap()
    d_wqs = nc.dram_tensor("wqs_scr", [16 * 128, D], BF16, kind="Internal").ap()
    d_wvs = nc.dram_tensor("wvs_scr", [DC * 128, D], BF16, kind="Internal").ap()

    cnt = {}
    stack = ExitStack()
    e = stack.enter_context

    sems = {}
    for s in ("dc", "da", "db", "dwa", "dwb", "dx", "dy0", "dy1",
              "wo0", "wo1", "wo2", "wf0", "wf1", "wf2",
              "wp0", "wp1", "wp2",
              "wq0", "wq1", "wv0", "wv1", "T", "V", "S", "BAR"):
        sems[s] = e(nc.semaphore(s))

    pb = [e(nc.psum_tensor(f"pb{i}", [128, 512], F32)) for i in range(8)]

    s_identf = e(nc.sbuf_tensor("sbident_f", [128, 128], F32))
    s_mask = e(nc.sbuf_tensor("sb_maskm", [128, 256], BF16))
    s_onesf = e(nc.sbuf_tensor("sb_onesf", [1, 128], F32))
    s_gcol = e(nc.sbuf_tensor("gcol", [128, 8], F32))
    s_g8col = e(nc.sbuf_tensor("g8col", [128, 8], F32))
    s_betab = e(nc.sbuf_tensor("betab", [128, 8], BF16))
    s_bw = e(nc.sbuf_tensor("bw", [128, 16], F32))
    s_bwv = e(nc.sbuf_tensor("bwv", [1, 1024], BF16))
    s_onesb = e(nc.sbuf_tensor("sb_onesb", [1, 128], BF16))
    s_bfc = e(nc.sbuf_tensor("bfccol", [128, 32], F32))
    s_bpr = e(nc.sbuf_tensor("bprcol", [128, 8], F32))
    s_cm05 = e(nc.sbuf_tensor("cm05", [128, 1], F32))
    s_cp05 = e(nc.sbuf_tensor("cp05", [128, 1], F32))
    s_scfc = e(nc.sbuf_tensor("scfc", [128, 1], F32))
    s_rsfc = e(nc.sbuf_tensor("rsfc", [128, 1], F32))
    s_scpr = e(nc.sbuf_tensor("scpr", [128, 1], F32))
    s_rspr = e(nc.sbuf_tensor("rspr", [128, 1], F32))
    s_am = e(nc.sbuf_tensor("amcols", [128, 32], F32))
    s_hsfc = e(nc.sbuf_tensor("hsfc", [128, 1], F32))
    s_hspr = e(nc.sbuf_tensor("hspr", [128, 1], F32))
    s_amc = e(nc.sbuf_tensor("amc", [128, 1], F32))
    s_amr = e(nc.sbuf_tensor("amr", [1, 128], F32))
    s_sc1 = e(nc.sbuf_tensor("sc1", [1, 1], F32))


    s_a2fc = e(nc.sbuf_tensor("a2fc", [16, D], BF16))
    s_btfc = e(nc.sbuf_tensor("btfc", [16, FFH], BF16))
    s_a2pr = e(nc.sbuf_tensor("a2pr", [16, FFH], BF16))
    s_btpr = e(nc.sbuf_tensor("btpr", [16, D], BF16))

    P = Prog(cnt)

    def dma(prog, eng, out_ap, in_ap, sem, slow=False):
        if slow:
            return prog.emit(eng, lambda E, o=out_ap, i=in_ap: E.dma_start(out=o, in_=i, allow_slow_non_contiguous=True), inc=sem, n=16)
        return prog.emit(eng, lambda E, o=out_ap, i=in_ap: E.dma_start(out=o, in_=i), inc=sem, n=16)

    def mm(prog, out_ap, lhsT, rhs, start, stop, tp=None, waits=()):
        def f(E, o=out_ap, l=lhsT, r=rhs):
            return E.matmul(o, l, r, start=start, stop=stop, tile_position=tp,
                            skip_group_check=True)
        return prog.emit("tensor", f, waits=waits, inc="T")

    def trans(prog, out_ap, in_ap, ident, start=True, stop=True, waits=()):
        def f(E, o=out_ap, i=in_ap, idn=ident):
            return E.matmul(o, i, idn, start=start, stop=stop, is_transpose=True,
                            skip_group_check=True)
        return prog.emit("tensor", f, waits=waits, inc="T")

    with ExitStack() as arena:
        a = arena.enter_context
        t_stage = a(nc.sbuf_tensor("stage", [128, 2 * FFH], F32))
        t_q = a(nc.sbuf_tensor("tq", [128, 2048], F32))
        t_k0 = a(nc.sbuf_tensor("tk0", [128, 2048], I32))
        t_d = a(nc.sbuf_tensor("td", [128, 2048], F32))
        t_s1 = a(nc.sbuf_tensor("ts1", [128, 2048], F32))
        t_s2 = a(nc.sbuf_tensor("ts2", [128, 2048], F32))
        t_wqf = a(nc.sbuf_tensor("twqf", [128, 2 * FFH], F32))
        t_panel = a(nc.sbuf_tensor("tpanel", [128, 2 * FFH], BF16))
        t_small = a(nc.sbuf_tensor("tsmall", [128, 16], F32))

        tok_dx = 0
        tok_dx = dma(P, "sync", s_identf[:], d_identf, "dc")
        tok_dx = dma(P, "sync", s_mask[:], d_mask, "dc")
        tok_dx = dma(P, "sync", s_onesf[:], d_onesf, "dc")
        tok_dx = dma(P, "sync", s_onesb[:], d_onesb, "dc")
        tok_dx = dma(P, "sync", s_gcol[:], d_gamma.rearrange("(c p) -> p c", p=128), "dc", slow=True)
        tok_dx = dma(P, "sync", t_small[:, 0:8], d_beta.rearrange("(c p) -> p c", p=128), "dc", slow=True)
        tok_dx = dma(P, "sync", s_bfc[:], d_bfc.rearrange("(c p) -> p c", p=128), "dc", slow=True)
        tok_dx = dma(P, "sync", s_bpr[:], d_bpr.rearrange("(c p) -> p c", p=128), "dc", slow=True)
        tok_consts = tok_dx
        P.emit("vector", lambda E: E.memset(s_cm05[:], -0.5))
        P.emit("vector", lambda E: E.memset(s_cp05[:], 0.5))
        P.emit("vector", lambda E: E.tensor_scalar_mul(s_g8col[:], s_gcol[:], 0.125),
               waits=[("dc", tok_consts)])
        tv = P.emit("vector", lambda E: E.tensor_copy(out=s_betab[:], in_=t_small[:, 0:8]),
                    waits=[("dc", tok_consts)], inc="V")

        t_bwv32 = a(nc.sbuf_tensor("tbwv32", [1, 1024], F32))
        cast_tk2 = [0, 0]
        pstore_tk = [0, 0]
        bw_read = 0
        bwv_read = 0
        for dc in range(DC):
            sem = "da" if dc % 2 == 0 else "db"
            ssem = "dwa" if dc % 2 == 0 else "dwb"
            slot = (dc % 2) * 4096
            tok_l = P.emit("sync",
                           lambda E, dc=dc, slot=slot:
                           E.dma_start(out=t_stage[:, slot:slot + 3072],
                                       in_=d_wqkv[dc * 128:(dc + 1) * 128, :]),
                           waits=[("S", cast_tk2[dc % 2])], inc=sem, n=16)
            cast = P.emit("scalar",
                          lambda E, slot=slot: E.copy(out=t_panel[:, slot:slot + 3072],
                                                      in_=t_stage[:, slot:slot + 3072]),
                          waits=[(sem, tok_l), (ssem, pstore_tk[dc % 2]), ("dc", tok_consts)], inc="S")
            cast_tk2[dc % 2] = cast
            tT = None
            for oc in range(16):
                tT = mm(P, pb[0][:, oc:oc + 1],
                        t_panel[:, slot + oc * 128: slot + (oc + 1) * 128],
                        s_betab[:, dc:dc + 1], start=True, stop=True,
                        waits=[("S", cast), ("V", bw_read)])
            for half in range(2):
                tT = mm(P, pb[1 + half][0:1, 0:512], s_betab[:, dc:dc + 1],
                        t_panel[:, slot + 2048 + half * 512: slot + 2048 + (half + 1) * 512],
                        start=True, stop=True,
                        waits=[("V", bwv_read)])
            tTn = P.emit("tensor", lambda E: E.nop(), inc="T")
            if dc == 0:
                bw_read = P.emit("vector", lambda E: E.tensor_copy(out=s_bw[:], in_=pb[0][:, 0:16]),
                                 waits=[("T", tTn)], inc="V")
                P.emit("vector", lambda E: E.tensor_copy(out=t_bwv32[0:1, 0:512], in_=pb[1][0:1, 0:512]),
                       inc="V")
                bwv_read = P.emit("vector", lambda E: E.tensor_copy(out=t_bwv32[0:1, 512:1024],
                                                                    in_=pb[2][0:1, 0:512]), inc="V")
            else:
                bw_read = P.emit("vector", lambda E: E.tensor_tensor(out=s_bw[:], in0=s_bw[:],
                                                                     in1=pb[0][:, 0:16], op=ALU.add),
                                 waits=[("T", tTn)], inc="V")
                P.emit("vector", lambda E: E.tensor_tensor(out=t_bwv32[0:1, 0:512], in0=t_bwv32[0:1, 0:512],
                                                           in1=pb[1][0:1, 0:512], op=ALU.add), inc="V")
                bwv_read = P.emit("vector", lambda E: E.tensor_tensor(out=t_bwv32[0:1, 512:1024],
                                                                      in0=t_bwv32[0:1, 512:1024],
                                                                      in1=pb[2][0:1, 0:512], op=ALU.add),
                                  inc="V")
            P.emit("scalar",
                   lambda E, dc=dc, slot=slot: E.activation(out=t_panel[:, slot:slot + 1024],
                                                            in_=t_panel[:, slot:slot + 1024],
                                                            func=AF.Copy, scale=s_g8col[:, dc:dc + 1]),
                   waits=[("T", tTn)])
            fold = P.emit("scalar",
                          lambda E, dc=dc, slot=slot: E.activation(out=t_panel[:, slot + 1024:slot + 3072],
                                                                   in_=t_panel[:, slot + 1024:slot + 3072],
                                                                   func=AF.Copy, scale=s_gcol[:, dc:dc + 1]),
                          inc="S")
            P.emit("gpsimd",
                   lambda E, dc=dc, slot=slot: E.dma_start(
                       out=bass.AP(d_wqs.tensor, dc * 128, [[1024, 128], [128 * 1024, 16], [1, 128]]),
                       in_=bass.AP(t_panel, slot, [[8192, 128], [128, 16], [1, 128]])),
                   waits=[("S", fold)], inc=ssem, n=16)
            pstore_tk[dc % 2] = P.emit("gpsimd",
                                       lambda E, dc=dc, slot=slot:
                                       E.dma_start(out=d_wvs[dc * 128:(dc + 1) * 128, :],
                                                   in_=t_panel[:, slot + 2048:slot + 3072]),
                                       inc=ssem, n=16)
        tok_bw = P.emit("vector", lambda E: E.tensor_scalar_mul(s_bw[:, 0:8], s_bw[:, 0:8], 0.125),
                        inc="V")
        tok_bwv = P.emit("vector", lambda E: E.tensor_copy(out=s_bwv[:], in_=t_bwv32[:]), inc="V")

        P.fence()
        st_toks = list(pstore_tk)
        cast_tk = [0, 0]
        for ic in range(DC):
            sem = "da" if ic % 2 == 0 else "db"
            ssem = "dwa" if ic % 2 == 0 else "dwb"
            tok_l = P.emit("sync",
                           lambda E, ic=ic: E.dma_start(out=t_stage[:, (ic % 2) * 4096:(ic % 2) * 4096 + 1024],
                                                        in_=d_wout[ic * 128:(ic + 1) * 128, :]),
                           waits=[("V", cast_tk[ic % 2])], inc=sem, n=16)
            tok_c = P.emit("vector",
                           lambda E, ic=ic: E.tensor_copy(out=t_panel[:, (ic % 2) * 1024:(ic % 2) * 1024 + 1024],
                                                          in_=t_stage[:, (ic % 2) * 4096:(ic % 2) * 4096 + 1024]),
                           waits=[(sem, tok_l), (ssem, st_toks[ic % 2])], inc="V")
            cast_tk[ic % 2] = tok_c
            st_toks[ic % 2] = P.emit(
                "gpsimd",
                lambda E, ic=ic: E.dma_start(out=d_woutb[ic * 128:(ic + 1) * 128, :],
                                             in_=t_panel[:, (ic % 2) * 1024:(ic % 2) * 1024 + 1024]),
                waits=[("V", tok_c)], inc=ssem, n=16)
        P.fence()

        tok_a = dma(P, "sync", t_stage[:16, 0:1024], d_afc, "da")
        tva = P.emit("vector", lambda E: E.tensor_scalar(out=s_a2fc[:], in0=t_stage[:16, 0:1024],
                                                         scalar1=2.0, scalar2=None, op0=ALU.mult),
                     waits=[("da", tok_a)], inc="V")
        tok_a = P.emit("sync", lambda E: E.dma_start(out=t_stage[:16, 0:4096], in_=d_apr),
                       waits=[("V", tva)], inc="da", n=16)
        tva = P.emit("vector", lambda E: E.tensor_scalar(out=s_a2pr[:], in0=t_stage[:16, 0:4096],
                                                         scalar1=2.0, scalar2=None, op0=ALU.mult),
                     waits=[("da", tok_a)], inc="V")
        P.fence()
        btr = [0, 0]
        bev = [0, 0]
        for c in range(HC):
            sem = "da" if c % 2 == 0 else "db"
            tok_l = P.emit("sync",
                           lambda E, c=c: E.dma_start(out=t_stage[:, (c % 2) * 4096:(c % 2) * 4096 + 16],
                                                      in_=d_bfcB[c * 128:(c + 1) * 128, :]),
                           waits=[("T", btr[c % 2])], inc=sem, n=16)
            tt = trans(P, pb[2 + c % 2][0:16, 0:128],
                       t_stage[:, (c % 2) * 4096:(c % 2) * 4096 + 16], s_identf[:],
                       waits=[(sem, tok_l), ("S", bev[c % 2])])
            btr[c % 2] = tt
            bev[c % 2] = P.emit("scalar",
                                lambda E, c=c: E.copy(out=s_btfc[:16, c * 128:(c + 1) * 128],
                                                      in_=pb[2 + c % 2][0:16, 0:128]),
                                waits=[("T", tt)], inc="S")
        P.fence()
        btr = [0, 0]
        bev = [0, 0]
        for c in range(DC):
            sem = "da" if c % 2 == 0 else "db"
            tok_l = P.emit("sync",
                           lambda E, c=c: E.dma_start(out=t_stage[:, (c % 2) * 4096:(c % 2) * 4096 + 16],
                                                      in_=d_bprB[c * 128:(c + 1) * 128, :]),
                           waits=[("T", btr[c % 2])], inc=sem, n=16)
            tt = trans(P, pb[2 + c % 2][0:16, 0:128],
                       t_stage[:, (c % 2) * 4096:(c % 2) * 4096 + 16], s_identf[:],
                       waits=[(sem, tok_l), ("S", bev[c % 2])])
            btr[c % 2] = tt
            bev[c % 2] = P.emit("scalar",
                                lambda E, c=c: E.copy(out=s_btpr[:16, c * 128:(c + 1) * 128],
                                                      in_=pb[2 + c % 2][0:16, 0:128]),
                                waits=[("T", tt)], inc="S")
        P.fence()

        def quant_weight(d_w, nchunk, free, s_scale, s_rscale, s_hscale, am_ncols, d_scrT, a2, bt,
                         panel_cols, tpos_cols):
            P.fence()
            red_tk = [0, 0]
            for c in range(nchunk):
                sl = (c % 2) * 4096
                sem = "da" if c % 2 == 0 else "db"
                tok_l = P.emit("sync",
                               lambda E, c=c, sl=sl: E.dma_start(out=t_stage[:, sl:sl + free],
                                                                 in_=d_w[c * 128:(c + 1) * 128, :]),
                               waits=[("V", red_tk[c % 2])], inc=sem, n=16)
                red_tk[c % 2] = P.emit("vector",
                                       lambda E, c=c, sl=sl: E.reduce_max(out=s_am[:, c:c + 1],
                                                                          in_=t_stage[:, sl:sl + free],
                                                                          axis=AX.X, apply_absolute_value=True),
                                       waits=[(sem, tok_l)], inc="V")
            P.fence()
            tok = P.emit("vector", lambda E: E.reduce_max(out=s_amc[:], in_=s_am[:, 0:am_ncols], axis=AX.X),
                         inc="V")
            tt = trans(P, pb[3][0:1, 0:128], s_amc[:], s_identf[:], waits=[("V", tok)])
            tok = P.emit("vector", lambda E: E.tensor_copy(out=s_amr[:], in_=pb[3][0:1, 0:128]),
                         waits=[("T", tt)], inc="V")
            tok = P.emit("vector", lambda E: E.reduce_max(out=s_sc1[:], in_=s_amr[:], axis=AX.X), inc="V")
            tok = P.emit("vector", lambda E: E.tensor_scalar_mul(s_sc1[:], s_sc1[:], float(np.float32(1.0) / np.float32(127.0))),
                         inc="V")
            tt = mm(P, pb[3][:, 0:1], s_onesf[:], s_sc1[:], start=True, stop=True,
                    waits=[("V", tok)])
            tok = P.emit("vector", lambda E: E.tensor_copy(out=s_scale[:], in_=pb[3][:, 0:1]),
                         waits=[("T", tt)], inc="V")
            tok = P.emit("vector", lambda E: E.reciprocal(out=s_rscale[:], in_=s_scale[:]), inc="V")
            tok = P.emit("vector", lambda E: E.tensor_scalar_mul(s_hscale[:], s_scale[:], 0.5), inc="V")
            P.fence()
            nsl = free // 1024
            tok_store = 0
            ev_bank = {}
            trans_last = 0
            tq_tk = [0, 0]
            gsl = [0]
            stt_tk = [0, 0]
            pend_tw = [None]
            twc = [0] * nchunk
            trans_c = [0] * nchunk
            ev_bank = {}
            store_tk = [0, 0]

            def flush_tw(prog):
                if pend_tw[0] is not None:
                    pf, pc, ptk, ptl = pend_tw[0]
                    tw = prog.emit("scalar", pf, waits=[("V", ptk), ("T", ptl)], inc="S")
                    twc[pc] = tw
                    pend_tw[0] = None

            def quant_slices(c):
                sl = (c % 2) * 4096
                wsl = (c % 2) * 4096
                sem = "da" if c % 2 == 0 else "db"
                tok_l = P.emit("sync",
                               lambda E, c=c, sl=sl: E.dma_start(out=t_stage[:, sl:sl + free],
                                                                 in_=d_w[c * 128:(c + 1) * 128, :]),
                               waits=[("S", tq_tk[c % 2])], inc=sem, n=16)
                tq = 0
                for s in range(nsl):
                    slot = gsl[0] % 2
                    q = slot * 1024
                    gsl[0] += 1
                    src = t_stage[:, sl + s * 1024: sl + (s + 1) * 1024]
                    tq = P.emit("scalar",
                                lambda E, src=src, q=q: E.activation(out=t_q[:, q:q + 1024], in_=src,
                                                                     func=AF.Copy, scale=s_rscale[:]),
                                waits=[(sem, tok_l), ("V", stt_tk[slot])], inc="S")
                    flush_tw(P)
                    tk = P.emit("vector",
                                lambda E, q=q: E.tensor_copy(out=t_k0[:, q:q + 1024], in_=t_q[:, q:q + 1024]),
                                waits=[("S", tq)], inc="V")
                    tk = P.emit("vector",
                                lambda E, q=q: E.tensor_tensor(out=t_d[:, q:q + 1024], in0=t_q[:, q:q + 1024],
                                                               in1=t_k0[:, q:q + 1024], op=ALU.subtract),
                                inc="V")
                    t1 = P.emit("scalar",
                                lambda E, q=q: E.activation(out=t_s1[:, q:q + 1024], in_=t_d[:, q:q + 1024],
                                                            func=AF.Sign, bias=s_cm05[:]),
                                waits=[("V", tk)], inc="S")
                    t1 = P.emit("scalar",
                                lambda E, q=q: E.activation(out=t_s2[:, q:q + 1024], in_=t_d[:, q:q + 1024],
                                                            func=AF.Sign, bias=s_cp05[:]), inc="S")
                    tk = P.emit("vector",
                                lambda E, q=q: E.scalar_tensor_tensor(out=t_s1[:, q:q + 1024],
                                                                      in0=t_k0[:, q:q + 1024], scalar=2.0,
                                                                      in1=t_s1[:, q:q + 1024],
                                                                      op0=ALU.mult, op1=ALU.add),
                                waits=[("S", t1)], inc="V")
                    tk = P.emit("vector",
                                lambda E, q=q: E.tensor_tensor(out=t_d[:, q:q + 1024], in0=t_s1[:, q:q + 1024],
                                                               in1=t_s2[:, q:q + 1024], op=ALU.add),
                                inc="V")
                    stt_tk[slot] = tk
                    def mk_tw(s=s, q=q, wsl=wsl):
                        return lambda E: E.activation(out=t_wqf[:, wsl + s * 1024: wsl + (s + 1) * 1024],
                                                      in_=t_d[:, q:q + 1024],
                                                      func=AF.Copy, scale=s_hscale[:])
                    pend_tw[0] = (mk_tw(), c, tk, trans_c[c - 2] if c >= 2 else 0)
                tq_tk[c % 2] = tq

            def trans_store(c):
                wsl = (c % 2) * 4096
                psl = (c % 2) * free
                nblk = free // 128
                ev_tok = None
                tt = None
                for b in range(nblk):
                    bi = 4 + ((b // 4) % 2)
                    bank = pb[bi]
                    tt = trans(P, bank[:, (b % 4) * 128:(b % 4) * 128 + 128],
                               t_wqf[:, wsl + b * 128: wsl + (b + 1) * 128], s_identf[:],
                               start=True, stop=False,
                               waits=[("S", twc[c]), ("S", ev_bank.get(bi, 0))])
                    tt = mm(P, bank[:, (b % 4) * 128:(b % 4) * 128 + 128],
                            a2[:16, b * 128:(b + 1) * 128],
                            bt[:16, c * 128:(c + 1) * 128],
                            start=False, stop=True)
                    if b % 4 == 3:
                        ev_tok = P.emit("scalar",
                                        lambda E, b=b, bank=bank, psl=psl:
                                        E.copy(out=t_panel[:, psl + (b - 3) * 128: psl + (b + 1) * 128],
                                               in_=bank[:]),
                                        waits=[("T", tt), ("dwa" if c % 2 == 0 else "dwb", store_tk[c % 2])],
                                        inc="S")
                        ev_bank[bi] = ev_tok
                trans_c[c] = tt
                store_tk[c % 2] = P.emit("gpsimd",
                                         lambda E, c=c, psl=psl:
                                         E.dma_start(out=d_scrT[c * 128:(c + 1) * 128, :],
                                                     in_=t_panel[:, psl:psl + free]),
                                         waits=[("S", ev_tok)],
                                         inc="dwa" if c % 2 == 0 else "dwb", n=16)

            for c in range(nchunk):
                quant_slices(c)
                flush_tw(P)
                if c >= 1:
                    trans_store(c - 1)
            trans_store(nchunk - 1)
            P.fence()

        quant_weight(d_wfc, HC, 1024, s_scfc, s_rsfc, s_hsfc, HC, d_wfcT, s_a2fc, s_btfc, 1024, None)
        quant_weight(d_wpr, DC, 4096, s_scpr, s_rspr, s_hspr, DC, d_wprT, s_a2pr, s_btpr, 4096, None)

        if dbg:
            P.fence()
            P.emit("sync", lambda E: E.nop(),
                   waits=[(k, cnt.get(k, 0)) for k in ("T", "V", "S")])
            pass
            dma(P, "sync", d_dbg["bw"], s_bw[:], "dc")
            dma(P, "sync", d_dbg["bwv"], s_bwv[:], "dc")
            P.emit("vector", lambda E: E.tensor_copy(out=s_am[:, 0:1], in_=s_scfc[:]))
            P.emit("vector", lambda E: E.tensor_copy(out=s_am[:, 1:2], in_=s_rsfc[:]))
            P.emit("vector", lambda E: E.tensor_copy(out=s_am[:, 2:3], in_=s_scpr[:]))
            tvd = P.emit("vector", lambda E: E.tensor_copy(out=s_am[:, 3:4], in_=s_rspr[:]), inc="V")
            dma(P, "sync", d_dbg["scl"], s_am[:, 0:8], "dc")
            dma(P, "sync", d_dbg["btfc"], s_btfc[:], "dc")
            dma(P, "sync", d_dbg["a2fc"], s_a2fc[:], "dc")
            dma(P, "sync", d_dbg["wfcT0"], d_wfcT[0:128, :], "dc")
            dma(P, "sync", d_dbg["wprT0"], d_wprT[0:128, :], "dc")
        for eng in ENGS:
            P.emit(eng, lambda E: E.nop(), inc="BAR")

        with nc.Block("setup") as b1:
            P.replay(b1, sems)

    s_x = e(nc.sbuf_tensor("xsb", [128, 2 * G * D], F32))
    s_xnb = e(nc.sbuf_tensor("xnb", [128, D], F32))
    s_xn1 = e(nc.sbuf_tensor("xn1", [128, DC * TILE], BF16))
    s_xn2 = e(nc.sbuf_tensor("xn2", [128, DC * TILE], BF16))
    s_q = e(nc.sbuf_tensor("qsb", [128, 8 * TILE], BF16))
    s_k = e(nc.sbuf_tensor("ksb", [128, 8 * TILE], BF16))
    s_v = e(nc.sbuf_tensor("vsb", [128, G * D], BF16))
    s_kprev = e(nc.sbuf_tensor("kprev", [128, 8 * 128], BF16))
    s_vprev = e(nc.sbuf_tensor("vprev", [128, D], BF16))
    s_ao = e(nc.sbuf_tensor("aosb", [128, 8 * TILE], BF16))
    s_h = e(nc.sbuf_tensor("hsb", [128, HC * TILE], BF16))
    s_sm = e(nc.sbuf_tensor("smbuf", [128, 7 * 256], F32))
    s_pT = e(nc.sbuf_tensor("pT", [128, 3 * 256], BF16))
    s_stat = e(nc.sbuf_tensor("stat", [128, 80], F32))
    s_ofm = e(nc.sbuf_tensor("ofm", [128, 512], F32))
    s_wfcb = e(nc.sbuf_tensor("wfcb", [128, 3 * 1024], BF16))
    s_wqb = e(nc.sbuf_tensor("wqb", [128, 2 * 2048], BF16))
    s_wvb = e(nc.sbuf_tensor("wvb", [128, 2 * 2048], BF16))
    s_wprb = e(nc.sbuf_tensor("wprb", [128, 3 * 4096], BF16))
    s_woutb = e(nc.sbuf_tensor("woutb", [128, 3 * 1024], BF16))

    NEGMAX, ROWSUM, RECIP = 0, 52, 64
    SUM, SSQ, MU, MSQ, VARP, R1, V2P, ASC, NMA, T0 = 8, 12, 16, 20, 24, 28, 32, 36, 40, 44

    M = Prog(cnt)
    M.fence_snap = {}
    for eng in ENGS:
        M.emit(eng, lambda E: E.nop(), waits=[("BAR", cnt["BAR"])])

    bank_free = [[] for _ in range(8)]

    def bmm(out_bank, cols, lhsT, rhs, start, stop, tp=None, waits=(), prow=None):
        w = list(waits)
        if start:
            w += bank_free[out_bank]
        ap = pb[out_bank][:, cols[0]:cols[1]] if prow is None else pb[out_bank][prow[0]:prow[1], cols[0]:cols[1]]
        tok = mm(M, ap, lhsT, rhs, start=start, stop=stop, tp=tp, waits=w)
        if start:
            bank_free[out_bank] = []
        return tok

    def btrans(out_bank, cols, in_ap, start=True, stop=True, waits=()):
        w = list(waits) + (bank_free[out_bank] if start else [])
        tok = trans(M, pb[out_bank][:, cols[0]:cols[1]], in_ap, s_identf[:],
                    start=start, stop=stop, waits=w)
        if start:
            bank_free[out_bank] = []
        return tok

    def bread(out_bank, sem, tok):
        bank_free[out_bank] = [(sem, tok)]

    wout_cons = [0, 0, 0]
    wfc_cons = [0, 0, 0]
    wpr_cons = [0, 0, 0]
    wq_cons = [0, 0]
    wv_cons = [0, 0]
    xnb_cons = [0, 0]
    xn1_cons = 0
    xn2_cons = 0
    qk_write_guard = [0, 0]
    v_write_guard = 0
    ao_guard = 0
    h_guard = 0
    ofm_trans_last = [0]
    ofm_cons = [0, 0]
    prev_expC = [0] * 7
    prev_trE = [0] * 7
    prev_H = [("V", 0), ("S", 0)]
    store_tok = [0, 0]
    kp_tok = 0
    vp_tok = 0
    st = {}

    def emit_stats(wait_sem, wait_tok):
        tv = ts = None
        for g in range(G):
            xb = st["xbase"]
            tv = M.emit("vector",
                        lambda E, g=g, xb=xb: E.tensor_scalar(out=s_xn1[:, 0:1024],
                                                              in0=s_x[:, xb + g * D: xb + (g + 1) * D],
                                                              scalar1=1.0, scalar2=None,
                                                              op0=ALU.mult, op1=ALU.add,
                                                              accum_out=s_stat[:, SUM + g:SUM + g + 1]),
                        waits=[(wait_sem, wait_tok), ("T", xn1_cons)], inc="V")
            ts = M.emit("scalar",
                        lambda E, g=g, xb=xb: E.activation(out=s_xn1[:, 1024:2048],
                                                           in_=s_x[:, xb + g * D: xb + (g + 1) * D],
                                                           func=AF.Square,
                                                           accum_out=s_stat[:, SSQ + g:SSQ + g + 1]),
                        waits=[(wait_sem, wait_tok), ("T", xn1_cons)], inc="S")
        c = 1.0 / 1024.0
        tv = M.emit("vector", lambda E: E.tensor_scalar_mul(s_stat[:, MU:MU + 4], s_stat[:, SUM:SUM + 4], c),
                    waits=[("V", tv)], inc="V")
        tv = M.emit("vector", lambda E: E.tensor_scalar_mul(s_stat[:, MSQ:MSQ + 4], s_stat[:, SSQ:SSQ + 4], c),
                    waits=[("S", ts)], inc="V")
        tv = M.emit("vector", lambda E: E.tensor_tensor(out=s_stat[:, T0:T0 + 4], in0=s_stat[:, MU:MU + 4],
                                                        in1=s_stat[:, MU:MU + 4], op=ALU.mult), inc="V")
        tv = M.emit("vector", lambda E: E.tensor_tensor(out=s_stat[:, VARP:VARP + 4], in0=s_stat[:, MSQ:MSQ + 4],
                                                        in1=s_stat[:, T0:T0 + 4], op=ALU.subtract), inc="V")
        tv = M.emit("vector", lambda E: E.tensor_scalar_add(s_stat[:, VARP:VARP + 4], s_stat[:, VARP:VARP + 4],
                                                            1e-5), inc="V")
        ts2 = M.emit("scalar", lambda E: E.activation(out=s_stat[:, T0:T0 + 4], in_=s_stat[:, VARP:VARP + 4],
                                                      func=AF.Ln),
                     waits=[("V", tv)], inc="S")
        ts2 = M.emit("scalar", lambda E: E.activation(out=s_stat[:, R1:R1 + 4], in_=s_stat[:, T0:T0 + 4],
                                                      func=AF.Exp, scale=s_cm05[:]), inc="S")
        return tv, ts2

    def emit_norm_transpose(dst, a_col, nma_col, wait_s_tok):
        ev_last = None
        xb = st["xbase"]
        for g in range(G):
            tok_x = M.emit("vector",
                           lambda E, g=g, xb=xb, a_col=a_col, nma_col=nma_col:
                           E.tensor_scalar(out=s_xnb[:, 0:D],
                                           in0=s_x[:, xb + g * D: xb + (g + 1) * D],
                                           scalar1=s_stat[:, a_col + g:a_col + g + 1],
                                           scalar2=s_stat[:, nma_col + g:nma_col + g + 1],
                                           op0=ALU.mult, op1=ALU.add),
                           waits=[("T", xnb_cons[0]), ("S", wait_s_tok)], inc="V")
            tt = None
            for half in range(2):
                bank = half
                for i4 in range(4):
                    dc = half * 4 + i4
                    tt = btrans(bank, (i4 * 128, (i4 + 1) * 128),
                                s_xnb[:, dc * 128:(dc + 1) * 128],
                                start=True, stop=True,
                                waits=[("V", tok_x)])
                dst_ap = bass.AP(dst, half * 4 * TILE + g * 128,
                                 [[DC * TILE, 128], [TILE, 4], [1, 128]])
                tok = M.emit("scalar",
                             lambda E, bank=bank, dst_ap=dst_ap:
                             E.copy(out=dst_ap, in_=pb[bank][:].rearrange("p (a b) -> p a b", b=128)),
                             waits=[("T", tt)], inc="S")
                bread(bank, "S", tok)
                ev_last = tok
            xnb_cons[0] = tt
        return ev_last

    def emit_qkv(t, xn1_ev):
        qk_ev = []
        for op in range(8):
            sl = op % 2
            tok_w = M.emit("gpsimd",
                           lambda E, op=op, sl=sl: E.dma_start(
                               out=s_wqb[:, sl * 2048:(sl + 1) * 2048].rearrange("p (a d) -> p a d", d=1024),
                               in_=d_wqs[op * 256:(op + 1) * 256, :].rearrange("(a p) d -> p a d", p=128)),
                           waits=[("T", wq_cons[sl])], inc="wq%d" % sl, n=16)
            for j in range(2):
                oc = 2 * op + j
                bank = 2 + (oc % 2)
                tt = None
                for dc in range(DC):
                    tt = bmm(bank, (0, 512),
                             s_wqb[:, sl * 2048 + j * 1024 + dc * 128: sl * 2048 + j * 1024 + (dc + 1) * 128],
                             s_xn1[:, dc * TILE:(dc + 1) * TILE],
                             start=(dc == 0), stop=(dc == DC - 1),
                             waits=[("wq%d" % sl, tok_w), ("S", xn1_ev),
                                    ("T", qk_write_guard[0]), ("V", kp_tok)])
                dst = s_q if oc < 8 else s_k
                fc = oc % 8
                tok = M.emit("scalar",
                             lambda E, dst=dst, fc=fc, oc=oc, bank=bank:
                             E.activation(out=dst[:, fc * TILE:(fc + 1) * TILE], in_=pb[bank][:],
                                          func=AF.Identity, bias=s_bw[:, oc:oc + 1]),
                             waits=[("T", tt)], inc="S")
                bread(bank, "S", tok)
                qk_ev.append(tok)
            wq_cons[sl] = tt
        v_ev = [0] * G
        ttl = None
        for dp in range(4):
            sl = dp % 2
            tok_w = M.emit("gpsimd",
                           lambda E, dp=dp, sl=sl: E.dma_start(
                               out=s_wvb[:, sl * 2048:(sl + 1) * 2048].rearrange("p (a d) -> p a d", d=1024),
                               in_=d_wvs[dp * 256:(dp + 1) * 256, :].rearrange("(a p) d -> p a d", p=128)),
                           waits=[("T", wv_cons[sl])], inc="wv%d" % sl, n=16)
            for j in range(2):
                dc = 2 * dp + j
                for g in range(G):
                    for half in range(2):
                        bi = g * 2 + half
                        ttl = bmm(bi, (0, 512),
                                  s_xn1[:, dc * TILE + g * 128: dc * TILE + (g + 1) * 128],
                                  s_wvb[:, sl * 2048 + j * 1024 + half * 512: sl * 2048 + j * 1024 + (half + 1) * 512],
                                  start=(dc == 0), stop=False,
                                  waits=[("wv%d" % sl, tok_w), ("S", xn1_ev),
                                         ("T", v_write_guard), ("V", vp_tok)])
            wv_cons[sl] = ttl
        for g in range(G):
            for half in range(2):
                bi = g * 2 + half
                tt = bmm(bi, (0, 512), s_onesb[:], s_bwv[0:1, half * 512:(half + 1) * 512],
                         start=False, stop=True)
                tok = M.emit("vector",
                             lambda E, g=g, half=half, bi=bi:
                             E.tensor_copy(out=s_v[:, g * D + half * 512: g * D + (half + 1) * 512],
                                           in_=pb[bi][:]),
                             waits=[("T", tt)], inc="V")
                bread(bi, "V", tok)
                v_ev[g] = tok
        return qk_ev, v_ev

    def make_attention_steps(t, qk_ev, v_ev):
        seq_start = (t % SEQ_TILES == 0)
        NI = G * H
        ctx = []
        for w in range(G):
            fw = (seq_start and w == 0)
            for h in range(H):
                ctx.append((w, h, fw))
        tokA = [0] * NI; tokB = [0] * NI; tokC = [0] * NI; tokD = [0] * NI
        tokE = [0] * NI; tokF = [0] * NI; tokG = [0] * NI; tokH = [0] * NI
        engH = [0] * NI
        res = {"tokG": tokG, "tokH": tokH, "engH": engH, "tokA": tokA, "tokB": tokB,
               "tokC": tokC, "tokE": tokE}

        def stA(i):
            w, h, fw = ctx[i]
            po = 64 * (h % 2); fc = h // 2
            qs = s_q[po:po + 64, fc * TILE + w * 128: fc * TILE + (w + 1) * 128]
            bank = i % 2
            wts = [("S", qk_ev[8 + fc]), ("V", tokB[i - 2] if i >= 2 else 0)]
            if fw:
                tokA[i] = bmm(bank, (0, 128), qs, s_k[po:po + 64, fc * TILE: fc * TILE + 128],
                              start=True, stop=True, tp=(po, 0), waits=wts)
            elif w == 0:
                bmm(bank, (0, 128), qs, s_kprev[po:po + 64, fc * 128:(fc + 1) * 128],
                    start=True, stop=True, tp=(po, 0), waits=wts)
                tokA[i] = bmm(bank, (128, 256), qs, s_k[po:po + 64, fc * TILE: fc * TILE + 128],
                              start=True, stop=True, tp=(po, 0))
            else:
                tokA[i] = bmm(bank, (0, 256), qs,
                              s_k[po:po + 64, fc * TILE + (w - 1) * 128: fc * TILE + (w + 1) * 128],
                              start=True, stop=True, tp=(po, 0), waits=wts)

        def stB(i):
            w, h, fw = ctx[i]
            nk = 128 if fw else 256
            wh = i % 7
            bank = i % 2
            mof = 128 if fw else 0
            tok = M.emit("vector",
                         lambda E, wh=wh, nk=nk, mof=mof, bank=bank:
                         E.tensor_tensor(out=s_sm[:, wh * 256: wh * 256 + nk], in0=pb[bank][:, 0:nk],
                                         in1=s_mask[:, mof:mof + nk], op=ALU.add),
                         waits=[("T", tokA[i]),
                                ("S", tokC[i - 7] if i >= 7 else prev_expC[i]),
                                ("T", tokE[i - 7] if i >= 7 else prev_trE[i])], inc="V")
            bread(bank, "V", tok)
            tokB[i] = M.emit("vector",
                             lambda E, wh=wh, nk=nk: E.reduce_max(out=s_stat[:, NEGMAX + wh:NEGMAX + wh + 1],
                                                                  in_=s_sm[:, wh * 256: wh * 256 + nk],
                                                                  axis=AX.X, negate=True),
                             inc="V")

        def stC(i):
            w, h, fw = ctx[i]
            nk = 128 if fw else 256
            wh = i % 7
            tokC[i] = M.emit("scalar",
                             lambda E, wh=wh, nk=nk: E.activation(out=s_sm[:, wh * 256: wh * 256 + nk],
                                                                  in_=s_sm[:, wh * 256: wh * 256 + nk],
                                                                  func=AF.Exp,
                                                                  bias=s_stat[:, NEGMAX + wh:NEGMAX + wh + 1],
                                                                  accum_out=s_stat[:, ROWSUM + wh:ROWSUM + wh + 1]),
                             waits=[("V", tokB[i]),
                                    ("T", tokE[i - 7] if i >= 7 else prev_trE[i])], inc="S")

        def stD(i):
            w, h, fw = ctx[i]
            nk = 128 if fw else 256
            wh = i % 7
            M.emit("vector",
                   lambda E, wh=wh: E.reciprocal(out=s_stat[:, RECIP + wh:RECIP + wh + 1],
                                                 in_=s_stat[:, ROWSUM + wh:ROWSUM + wh + 1]),
                   waits=[("S", tokC[i])], inc="V")
            tokD[i] = M.emit("vector",
                             lambda E, wh=wh, nk=nk: E.tensor_scalar_mul(s_sm[:, wh * 256: wh * 256 + nk],
                                                                         s_sm[:, wh * 256: wh * 256 + nk],
                                                                         s_stat[:, RECIP + wh:RECIP + wh + 1]),
                             inc="V")

        def stE(i):
            w, h, fw = ctx[i]
            nk = 128 if fw else 256
            wh = i % 7
            tt2 = None
            for kc in range(nk // 128):
                tt2 = btrans(2, (kc * 128, (kc + 1) * 128),
                             s_sm[:, wh * 256 + kc * 128: wh * 256 + (kc + 1) * 128],
                             start=True, stop=True,
                             waits=[("V", tokD[i])])
            tokE[i] = tt2

        def stF(i):
            w, h, fw = ctx[i]
            nk = 128 if fw else 256
            wh = i % 7
            pslot = i % 3
            tokF[i] = M.emit("scalar",
                             lambda E, pslot=pslot, nk=nk:
                             E.copy(out=s_pT[:, pslot * 256: pslot * 256 + nk], in_=pb[2][:, 0:nk]),
                             waits=[("T", tokE[i])], inc="S")
            bread(2, "S", tokF[i])

        def stG(i):
            w, h, fw = ctx[i]
            wh = i % 3
            po = 64 * (h % 2)
            pvw = [("S", tokF[i]), ("V", v_ev[w])]
            if fw:
                tokG[i] = bmm(3, (0, 128), s_v[:, w * D + h * DH: w * D + (h + 1) * DH],
                              s_pT[:, wh * 256: wh * 256 + 128],
                              start=True, stop=True, tp=(0, po), waits=pvw, prow=(po, po + 64))
            else:
                vprev_ap = (s_vprev[:, h * DH:(h + 1) * DH] if w == 0
                            else s_v[:, (w - 1) * D + h * DH: (w - 1) * D + h * DH + DH])
                bmm(3, (0, 128), vprev_ap, s_pT[:, wh * 256: wh * 256 + 128],
                    start=True, stop=False, tp=(0, po), waits=pvw, prow=(po, po + 64))
                tokG[i] = bmm(3, (0, 128), s_v[:, w * D + h * DH: w * D + (h + 1) * DH],
                              s_pT[:, wh * 256 + 128: wh * 256 + 256],
                              start=False, stop=True, tp=(0, po), prow=(po, po + 64))

        def stH(i):
            w, h, fw = ctx[i]
            po = 64 * (h % 2); fc = h // 2
            if i % 2 == 0:
                tokH[i] = M.emit("vector",
                                 lambda E, po=po, fc=fc, w=w:
                                 E.tensor_copy(out=s_ao[po:po + 64, fc * TILE + w * 128: fc * TILE + (w + 1) * 128],
                                               in_=pb[3][po:po + 64, 0:128]),
                                 waits=[("T", tokG[i]), ("T", ao_guard)], inc="V")
                engH[i] = 0
                bread(3, "V", tokH[i])
            else:
                tokH[i] = M.emit("scalar",
                                 lambda E, po=po, fc=fc, w=w:
                                 E.copy(out=s_ao[po:po + 64, fc * TILE + w * 128: fc * TILE + (w + 1) * 128],
                                        in_=pb[3][po:po + 64, 0:128]),
                                 waits=[("T", tokG[i]), ("T", ao_guard)], inc="S")
                engH[i] = 1
                bread(3, "S", tokH[i])

        stages = (stA, stB, stC, stD, stE, stF, stG, stH)
        offs = (0, 2, 4, 5, 7, 8, 10, 11)
        steps = []
        for j in range(NI + offs[-1]):
            def step(j=j):
                for k in range(len(stages) - 1, -1, -1):
                    i = j - offs[k]
                    if 0 <= i < NI:
                        stages[k](i)
            steps.append(step)
        return steps, res

    def emit_ffn_groups(tp):
        groups = []
        xb = (tp % 2) * G * D
        h_toks = st.setdefault("h_toks", {})
        h_toks.clear()

        def mk_fc(c):
            def f():
                sl = c % 3
                tok_w = M.emit("gpsimd",
                               lambda E, c=c, sl=sl: E.dma_start(out=s_wfcb[:, sl * 1024:(sl + 1) * 1024],
                                                                 in_=d_wfcT[c * 128:(c + 1) * 128, :]),
                               waits=[("T", wfc_cons[sl])], inc="wf%d" % sl, n=16)
                bank = 4 + sl
                tt = None
                for dc in range(DC):
                    tt = bmm(bank, (0, 512),
                             s_wfcb[:, sl * 1024 + dc * 128: sl * 1024 + (dc + 1) * 128],
                             s_xn2[:, dc * TILE:(dc + 1) * TILE],
                             start=(dc == 0), stop=(dc == DC - 1),
                             waits=[("wf%d" % sl, tok_w), ("S", st["xn2_ev"]),
                                    ("T", h_guard_g[0])])
                wfc_cons[sl] = tt
                tok = M.emit("scalar",
                             lambda E, c=c, bank=bank: E.activation(out=s_h[:, c * TILE:(c + 1) * TILE],
                                                                    in_=pb[bank][:], func=AF.Gelu,
                                                                    bias=s_bfc[:, c:c + 1]),
                             waits=[("T", tt)], inc="S")
                bread(bank, "S", tok)
                h_toks[c] = tok
            return f

        def mk_proj(do):
            def f():
                sl = do % 3
                tok_w = M.emit("gpsimd",
                               lambda E, do=do, sl=sl: E.dma_start(out=s_wprb[:, sl * 4096:(sl + 1) * 4096],
                                                                   in_=d_wprT[do * 128:(do + 1) * 128, :]),
                               waits=[("T", wpr_cons[sl])], inc="wp%d" % sl, n=16)
                tt = None
                for c in range(HC):
                    tt = bmm(6, (0, 512),
                             s_wprb[:, sl * 4096 + c * 128: sl * 4096 + (c + 1) * 128],
                             s_h[:, c * TILE:(c + 1) * TILE],
                             start=(c == 0), stop=(c == HC - 1),
                             waits=[("wp%d" % sl, tok_w), ("S", h_toks[HC - 1])])
                wpr_cons[sl] = tt
                st["pass2_last"] = tt
                tok_ev = M.emit("scalar",
                                lambda E, do=do: E.activation(out=s_ofm[:, 0:512], in_=pb[6][:],
                                                              func=AF.Identity, bias=s_bpr[:, do:do + 1]),
                                waits=[("T", tt), ("T", ofm_trans_last[0])], inc="S")
                bread(6, "S", tok_ev)
                tt2 = None
                for g in range(G):
                    tt2 = btrans(7, (g * 128, (g + 1) * 128),
                                 s_ofm[:, g * 128:(g + 1) * 128],
                                 start=True, stop=True, waits=[("S", tok_ev)])
                ofm_trans_last[0] = tt2
                tok_res = M.emit("vector",
                                 lambda E, do=do, xb=xb:
                                 E.scalar_tensor_tensor(
                                     out=bass.AP(s_x, xb + do * 128, [[2 * G * D, 128], [D, 4], [1, 128]]),
                                     in0=pb[7][:].rearrange("p (g c) -> p g c", c=128), scalar=1.0,
                                     in1=bass.AP(s_x, xb + do * 128, [[2 * G * D, 128], [D, 4], [1, 128]]),
                                     op0=ALU.mult, op1=ALU.add),
                                 waits=[("T", tt2)], inc="V")
                bread(7, "V", tok_res)
                st["last_res"] = tok_res
            return f

        for c in range(HC):
            groups.append(mk_fc(c))
        for do in range(DC):
            groups.append(mk_proj(do))

        def mk_store():
            def f():
                par = tp % 2
                store_tok[par] = M.emit("sync",
                                        lambda E, tp=tp, xb=xb: E.dma_start(
                                            out=d_y[tp * TILE:(tp + 1) * TILE, :].rearrange("(g p) d -> p g d", p=128),
                                            in_=s_x[:, xb:xb + G * D].rearrange("p (g d) -> p g d", d=D)),
                                        waits=[("V", st["last_res"])], inc="dy%d" % par, n=16)
            return f
        groups.append(mk_store())
        return groups

    h_guard_g = [0]

    def emit_outproj(t, pv_last_waits):
        tt = None
        for ic in range(DC):
            sl = ic % 3
            tok_w = M.emit("gpsimd",
                           lambda E, ic=ic, sl=sl: E.dma_start(out=s_woutb[:, sl * 1024:(sl + 1) * 1024],
                                                               in_=d_woutb[ic * 128:(ic + 1) * 128, :]),
                           waits=[("T", wout_cons[sl])], inc="wo%d" % sl, n=16)
            for g in range(G):
                for half in range(2):
                    tt = bmm(2 * g + half, (0, 512),
                             s_ao[:, ic * TILE + g * 128: ic * TILE + (g + 1) * 128],
                             s_woutb[:, sl * 1024 + half * 512: sl * 1024 + (half + 1) * 512],
                             start=(ic == 0), stop=(ic == DC - 1),
                             waits=[("wo%d" % sl, tok_w)] + (pv_last_waits if ic == 0 else []))
            wout_cons[sl] = tt
        tok_T = M.emit("tensor", lambda E: E.nop(), inc="T")
        tvlast = None
        xb = st["xbase"]
        for g in range(G):
            for half in range(2):
                bi = 2 * g + half
                tvlast = M.emit("vector",
                                lambda E, g=g, half=half, bi=bi, xb=xb:
                                E.scalar_tensor_tensor(
                                    out=s_x[:, xb + g * D + half * 512: xb + g * D + (half + 1) * 512],
                                    in0=pb[bi][:], scalar=1.0,
                                    in1=s_x[:, xb + g * D + half * 512: xb + g * D + (half + 1) * 512],
                                    op0=ALU.mult, op1=ALU.add),
                                waits=[("T", tok_T)], inc="V")
                bread(bi, "V", tvlast)
        return tok_T, tvlast

    pending_ffn = None
    for t in range(NT):
        st["xbase"] = (t % 2) * G * D
        par = t % 2
        tok_x = M.emit("sync",
                       lambda E, t=t, xb=st["xbase"]: E.dma_start(
                           out=s_x[:, xb:xb + G * D].rearrange("p (g d) -> p g d", d=D),
                           in_=d_x[t * TILE:(t + 1) * TILE, :].rearrange("(g p) d -> p g d", p=128)),
                       waits=[("dy%d" % par, store_tok[par])], inc="dx", n=16)

        tv, ts = emit_stats("dx", tok_x)
        T1 = 48
        tv = M.emit("vector", lambda E: E.reciprocal(out=s_stat[:, T1:T1 + 4], in_=s_stat[:, VARP:VARP + 4]),
                    inc="V")
        tv = M.emit("vector", lambda E: E.tensor_scalar_add(s_stat[:, V2P:V2P + 4], s_stat[:, VARP:VARP + 4],
                                                            -1e-5), inc="V")
        tv = M.emit("vector", lambda E: E.tensor_tensor(out=s_stat[:, V2P:V2P + 4], in0=s_stat[:, V2P:V2P + 4],
                                                        in1=s_stat[:, T1:T1 + 4], op=ALU.mult), inc="V")
        tv = M.emit("vector", lambda E: E.tensor_scalar_add(s_stat[:, V2P:V2P + 4], s_stat[:, V2P:V2P + 4],
                                                            1e-5), inc="V")
        ts = M.emit("scalar", lambda E: E.activation(out=s_stat[:, T1:T1 + 4], in_=s_stat[:, V2P:V2P + 4],
                                                     func=AF.Ln),
                    waits=[("V", tv), ("S", ts)], inc="S")
        ts = M.emit("scalar", lambda E: E.activation(out=s_stat[:, V2P:V2P + 4], in_=s_stat[:, T1:T1 + 4],
                                                     func=AF.Exp, scale=s_cm05[:]), inc="S")
        tv = M.emit("vector", lambda E: E.tensor_tensor(out=s_stat[:, ASC:ASC + 4], in0=s_stat[:, R1:R1 + 4],
                                                        in1=s_stat[:, V2P:V2P + 4], op=ALU.mult),
                    waits=[("S", ts)], inc="V")
        tv = M.emit("vector", lambda E: E.scalar_tensor_tensor(out=s_stat[:, NMA:NMA + 4],
                                                               in0=s_stat[:, MU:MU + 4],
                                                               scalar=-1.0, in1=s_stat[:, ASC:ASC + 4],
                                                               op0=ALU.mult, op1=ALU.mult), inc="V")
        xn1_ev = emit_norm_transpose(s_xn1, ASC, NMA, 0)

        qk_ev, v_ev = emit_qkv(t, xn1_ev)

        att_steps, att = make_attention_steps(t, qk_ev, v_ev)
        if pending_ffn is not None:
            ng, ns = len(pending_ffn), len(att_steps)
            si = 0
            lead = min(ns, 10)
            while si < lead:
                att_steps[si]()
                si += 1
            for gi, grp in enumerate(pending_ffn):
                grp()
                want = min(ns, lead + (gi + 1) * (ns - lead) // ng)
                while si < want:
                    att_steps[si]()
                    si += 1
            while si < ns:
                att_steps[si]()
                si += 1
        else:
            for stp in att_steps:
                stp()
        h_guard_g[0] = st.get("pass2_last", 0)

        NI = G * H
        qk_write_guard[0] = att["tokA"][NI - 1]
        v_write_guard = att["tokG"][NI - 1]
        for k in range(7):
            prev_expC[(NI - 7 + k) % 7] = att["tokC"][NI - 7 + k]
            prev_trE[(NI - 7 + k) % 7] = att["tokE"][NI - 7 + k]

        if (t + 1) % SEQ_TILES != 0:
            kp_tok = M.emit("vector",
                            lambda E: E.tensor_copy(
                                out=s_kprev[:].rearrange("p (f c) -> p f c", c=128),
                                in_=s_k[:].rearrange("p (f t) -> p f t", t=TILE)[:, :, TILE - 128:TILE]),
                            waits=[("T", att["tokG"][15])], inc="V")
            vp_tok = M.emit("vector",
                            lambda E: E.tensor_copy(out=s_vprev[:], in_=s_v[:, (G - 1) * D: G * D]),
                            waits=[("T", att["tokG"][15])], inc="V")

        li = NI - 1
        pvw = [(("V", "S")[att["engH"][li]], att["tokH"][li]),
               (("V", "S")[att["engH"][li - 1]], att["tokH"][li - 1])]
        opj_T, tvlast = emit_outproj(t, pvw)
        ao_guard = opj_T

        tv2, ts2 = emit_stats("V", tvlast)
        tsn = None
        for g in range(G):
            tsn = M.emit("vector",
                         lambda E, g=g: E.scalar_tensor_tensor(out=s_stat[:, NMA + g:NMA + g + 1],
                                                               in0=s_stat[:, MU + g:MU + g + 1], scalar=-1.0,
                                                               in1=s_stat[:, R1 + g:R1 + g + 1],
                                                               op0=ALU.mult, op1=ALU.mult),
                         waits=[("S", ts2)], inc="V")
        st["xn2_ev"] = emit_norm_transpose(s_xn2, R1, NMA, 0)
        xn2_cons = 0
        pending_ffn = emit_ffn_groups(t)

    for grp in pending_ffn:
        grp()

    M.emit("sync", lambda E: E.nop(), waits=[("dy0", cnt.get("dy0", 0)), ("dy1", cnt.get("dy1", 0))])

    with nc.Block("main") as b2:
        M.replay(b2, sems)

    stack.close()
    return nc


_NC_CACHE = None


def kernel(L0_attn_x, L0_attn_ln_gamma, L0_attn_ln_beta, L0_attn_w_qkv, L0_attn_w_out,
           L0_ffn_W_fc, L0_ffn_b_fc, L0_ffn_A_fc, L0_ffn_B_fc,
           L0_ffn_W_proj, L0_ffn_b_proj, L0_ffn_A_proj, L0_ffn_B_proj):
    global _NC_CACHE
    x = np.ascontiguousarray(np.asarray(L0_attn_x, dtype=np.float32))
    B, N, Dm = x.shape
    ncores = 8
    per = B // ncores

    if _NC_CACHE is None:
        _NC_CACHE = build_program(dbg=DBG)
    nc = _NC_CACHE

    ident_f = np.eye(128, dtype=np.float32)
    ii = np.arange(128)[:, None]
    jj = np.arange(256)[None, :]
    prev = jj < 128
    jjm = np.where(prev, jj, jj - 128)
    bad = np.where(prev, jjm < ii, jjm > ii)
    mask_m = np.where(bad, np.float32(NEG), np.float32(0.0)).astype(BFNP)
    ones_f = np.ones((1, 128), dtype=np.float32)

    common = dict(
        gamma=np.asarray(L0_attn_ln_gamma, np.float32),
        beta=np.asarray(L0_attn_ln_beta, np.float32),
        wqkv=np.asarray(L0_attn_w_qkv, np.float32),
        wout=np.asarray(L0_attn_w_out, np.float32),
        wfc=np.asarray(L0_ffn_W_fc, np.float32),
        bfc=np.asarray(L0_ffn_b_fc, np.float32),
        afc=np.asarray(L0_ffn_A_fc, np.float32),
        bfcB=np.asarray(L0_ffn_B_fc, np.float32),
        wpr=np.asarray(L0_ffn_W_proj, np.float32),
        bpr=np.asarray(L0_ffn_b_proj, np.float32),
        apr=np.asarray(L0_ffn_A_proj, np.float32),
        bprB=np.asarray(L0_ffn_B_proj, np.float32),
        identf=ident_f, maskm=mask_m, onesf=ones_f,
        onesb=np.ones((1, 128), dtype=BFNP),
    )
    in_maps = []
    for c in range(ncores):
        m = dict(common)
        m["x"] = np.ascontiguousarray(x[c * per:(c + 1) * per].reshape(TOK, Dm))
        in_maps.append(m)

    res = run_bass_kernel_spmd(nc, in_maps, core_ids=list(range(ncores)))
    out = np.empty((B, N, Dm), np.float32)
    for c in range(ncores):
        out[c * per:(c + 1) * per] = res.results[c]["y"].reshape(per, N, Dm)
    if DBG:
        global DBG_RES
        DBG_RES = res.results
    return out


# version 19
# speedup vs baseline: 1.0524x; 1.0524x over previous
import numpy as np
import ml_dtypes
from contextlib import ExitStack

import concourse.bass as bass
import concourse.mybir as mybir
from concourse.bass_utils import run_bass_kernel_spmd

F32 = mybir.dt.float32
BF16 = mybir.dt.bfloat16
I32 = mybir.dt.int32
AF = mybir.ActivationFunctionType
ALU = mybir.AluOpType
AX = mybir.AxisListType
BFNP = ml_dtypes.bfloat16

D = 1024
TOK = 4096
TILE = 512
NT = TOK // TILE
G = TILE // 128
DC = D // 128
H = 16
DH = 64
FFH = 4096
HC = FFH // 128
NEG = -1.0e30
SEQ_TILES = 2048 // TILE

ENGS = ("sync", "tensor", "vector", "scalar", "gpsimd")


class Prog:

    def __init__(self, cnt):
        self.ops = {e: [] for e in ENGS}
        self.cnt = cnt
        self.last_wait = {e: {} for e in ENGS}
        self.fence_snap = {}

    def emit(self, eng, fn, waits=(), inc=None, n=1):
        own = _OWN.get(eng)
        merged = dict(self.fence_snap)
        if own in ("V", "S"):
            if inc is None:
                inc = own
            merged[own] = self.cnt.get(own, 0)
        for s, v in waits:
            if v > merged.get(s, 0):
                merged[s] = v
        w = []
        for s, v in merged.items():
            if s == "T" and eng == "tensor":
                continue
            if v > self.last_wait[eng].get(s, 0):
                self.last_wait[eng][s] = v
                w.append((s, v))
        self.ops[eng].append((fn, w, inc, n))
        if inc is not None:
            self.cnt[inc] = self.cnt.get(inc, 0) + n
            return self.cnt[inc]
        return None

    def fence(self):
        self.fence_snap = {s: self.cnt.get(s, 0) for s in ("T", "V", "S")}

    def replay(self, block, sems):
        for ename in ENGS:
            ops = self.ops[ename]

            def body(eng, ops=ops):
                for fn, w, inc, n in ops:
                    for s, v in w:
                        eng.wait_ge(sems[s], v)
                    instr = fn(eng)
                    if inc is not None:
                        instr.then_inc(sems[inc], n)
            getattr(block, ename)(body)


_OWN = {"tensor": "T", "vector": "V", "scalar": "S"}


DBG = False


def build_program(dbg=False):
    nc = bass.Bass("TRN2", target_bir_lowering=False, detect_race_conditions=False)

    d_x = nc.dram_tensor("x", [TOK, D], F32, kind="ExternalInput").ap()
    d_gamma = nc.dram_tensor("gamma", [D], F32, kind="ExternalInput").ap()
    d_beta = nc.dram_tensor("beta", [D], F32, kind="ExternalInput").ap()
    d_wqkv = nc.dram_tensor("wqkv", [D, 3 * D], F32, kind="ExternalInput").ap()
    d_wout = nc.dram_tensor("wout", [D, D], F32, kind="ExternalInput").ap()
    d_wfc = nc.dram_tensor("wfc", [FFH, D], F32, kind="ExternalInput").ap()
    d_bfc = nc.dram_tensor("bfc", [FFH], F32, kind="ExternalInput").ap()
    d_afc = nc.dram_tensor("afc", [16, D], F32, kind="ExternalInput").ap()
    d_bfcB = nc.dram_tensor("bfcB", [FFH, 16], F32, kind="ExternalInput").ap()
    d_wpr = nc.dram_tensor("wpr", [D, FFH], F32, kind="ExternalInput").ap()
    d_bpr = nc.dram_tensor("bpr", [D], F32, kind="ExternalInput").ap()
    d_apr = nc.dram_tensor("apr", [16, FFH], F32, kind="ExternalInput").ap()
    d_bprB = nc.dram_tensor("bprB", [D, 16], F32, kind="ExternalInput").ap()
    d_identf = nc.dram_tensor("identf", [128, 128], F32, kind="ExternalInput").ap()
    d_mask = nc.dram_tensor("maskm", [128, 256], BF16, kind="ExternalInput").ap()
    d_onesf = nc.dram_tensor("onesf", [1, 128], F32, kind="ExternalInput").ap()
    d_onesb = nc.dram_tensor("onesb", [1, 128], BF16, kind="ExternalInput").ap()

    d_y = nc.dram_tensor("y", [TOK, D], F32, kind="ExternalOutput").ap()
    d_dbg = {}
    if dbg:
        for name, shape, dt in [("wall", [128, 24576], BF16), ("bw", [128, 16], F32),
                                ("bwv", [1, 1024], F32), ("scl", [128, 8], F32),
                                ("btfc", [16, FFH], BF16), ("a2fc", [16, D], BF16),
                                ("stat", [128, 64], F32), ("xnfm", [128, 4096], BF16),
                                ("qd", [128, 4096], BF16), ("kd", [128, 4096], BF16),
                                ("vd", [128, 4096], BF16), ("ao", [128, 4096], BF16),
                                ("x1", [128, 4096], F32), ("hd", [128, HC * TILE], BF16),
                                ("wfcT0", [128, 1024], BF16), ("wprT0", [128, 4096], BF16)]:
            d_dbg[name] = nc.dram_tensor("dbg_" + name, shape, dt, kind="ExternalOutput").ap()

    d_wfcT = nc.dram_tensor("wfcT_scr", [FFH, D], BF16, kind="Internal").ap()
    d_wprT = nc.dram_tensor("wprT_scr", [D, FFH], BF16, kind="Internal").ap()
    d_woutb = nc.dram_tensor("wout_scr", [D, D], BF16, kind="Internal").ap()
    d_wqs = nc.dram_tensor("wqs_scr", [16 * 128, D], BF16, kind="Internal").ap()
    d_wvs = nc.dram_tensor("wvs_scr", [DC * 128, D], BF16, kind="Internal").ap()

    cnt = {}
    stack = ExitStack()
    e = stack.enter_context

    sems = {}
    for s in ("dc", "da", "db", "dwa", "dwb", "dx", "dy0", "dy1",
              "wo0", "wo1", "wo2", "wf0", "wf1", "wf2",
              "wp0", "wp1", "wp2",
              "wq0", "wq1", "wv0", "wv1", "T", "V", "S", "BAR"):
        sems[s] = e(nc.semaphore(s))

    pb = [e(nc.psum_tensor(f"pb{i}", [128, 512], F32)) for i in range(8)]

    s_identf = e(nc.sbuf_tensor("sbident_f", [128, 128], F32))
    s_mask = e(nc.sbuf_tensor("sb_maskm", [128, 256], BF16))
    s_onesf = e(nc.sbuf_tensor("sb_onesf", [1, 128], F32))
    s_gcol = e(nc.sbuf_tensor("gcol", [128, 8], F32))
    s_g8col = e(nc.sbuf_tensor("g8col", [128, 8], F32))
    s_betab = e(nc.sbuf_tensor("betab", [128, 8], BF16))
    s_bw = e(nc.sbuf_tensor("bw", [128, 16], F32))
    s_bwv = e(nc.sbuf_tensor("bwv", [1, 1024], BF16))
    s_onesb = e(nc.sbuf_tensor("sb_onesb", [1, 128], BF16))
    s_bfc = e(nc.sbuf_tensor("bfccol", [128, 32], F32))
    s_bpr = e(nc.sbuf_tensor("bprcol", [128, 8], F32))
    s_cm05 = e(nc.sbuf_tensor("cm05", [128, 1], F32))
    s_cp05 = e(nc.sbuf_tensor("cp05", [128, 1], F32))
    s_scfc = e(nc.sbuf_tensor("scfc", [128, 1], F32))
    s_rsfc = e(nc.sbuf_tensor("rsfc", [128, 1], F32))
    s_scpr = e(nc.sbuf_tensor("scpr", [128, 1], F32))
    s_rspr = e(nc.sbuf_tensor("rspr", [128, 1], F32))
    s_am = e(nc.sbuf_tensor("amcols", [128, 32], F32))
    s_hsfc = e(nc.sbuf_tensor("hsfc", [128, 1], F32))
    s_hspr = e(nc.sbuf_tensor("hspr", [128, 1], F32))
    s_amc = e(nc.sbuf_tensor("amc", [128, 1], F32))
    s_amr = e(nc.sbuf_tensor("amr", [1, 128], F32))
    s_sc1 = e(nc.sbuf_tensor("sc1", [1, 1], F32))


    s_a2fc = e(nc.sbuf_tensor("a2fc", [16, D], BF16))
    s_btfc = e(nc.sbuf_tensor("btfc", [16, FFH], BF16))
    s_a2pr = e(nc.sbuf_tensor("a2pr", [16, FFH], BF16))
    s_btpr = e(nc.sbuf_tensor("btpr", [16, D], BF16))

    P = Prog(cnt)

    def dma(prog, eng, out_ap, in_ap, sem, slow=False):
        if slow:
            return prog.emit(eng, lambda E, o=out_ap, i=in_ap: E.dma_start(out=o, in_=i, allow_slow_non_contiguous=True), inc=sem, n=16)
        return prog.emit(eng, lambda E, o=out_ap, i=in_ap: E.dma_start(out=o, in_=i), inc=sem, n=16)

    def mm(prog, out_ap, lhsT, rhs, start, stop, tp=None, waits=()):
        def f(E, o=out_ap, l=lhsT, r=rhs):
            return E.matmul(o, l, r, start=start, stop=stop, tile_position=tp,
                            skip_group_check=True)
        return prog.emit("tensor", f, waits=waits, inc="T")

    def trans(prog, out_ap, in_ap, ident, start=True, stop=True, waits=()):
        def f(E, o=out_ap, i=in_ap, idn=ident):
            return E.matmul(o, i, idn, start=start, stop=stop, is_transpose=True,
                            skip_group_check=True)
        return prog.emit("tensor", f, waits=waits, inc="T")

    with ExitStack() as arena:
        a = arena.enter_context
        t_stage = a(nc.sbuf_tensor("stage", [128, 2 * FFH], F32))
        t_q = a(nc.sbuf_tensor("tq", [128, 2048], F32))
        t_k0 = a(nc.sbuf_tensor("tk0", [128, 2048], I32))
        t_d = a(nc.sbuf_tensor("td", [128, 2048], F32))
        t_s1 = a(nc.sbuf_tensor("ts1", [128, 2048], F32))
        t_s2 = a(nc.sbuf_tensor("ts2", [128, 2048], F32))
        t_wqf = a(nc.sbuf_tensor("twqf", [128, 2 * FFH], F32))
        t_panel = a(nc.sbuf_tensor("tpanel", [128, 2 * FFH], BF16))
        t_small = a(nc.sbuf_tensor("tsmall", [128, 16], F32))

        tok_dx = 0
        tok_dx = dma(P, "sync", s_identf[:], d_identf, "dc")
        tok_dx = dma(P, "sync", s_mask[:], d_mask, "dc")
        tok_dx = dma(P, "sync", s_onesf[:], d_onesf, "dc")
        tok_dx = dma(P, "sync", s_onesb[:], d_onesb, "dc")
        tok_dx = dma(P, "sync", s_gcol[:], d_gamma.rearrange("(c p) -> p c", p=128), "dc", slow=True)
        tok_dx = dma(P, "sync", t_small[:, 0:8], d_beta.rearrange("(c p) -> p c", p=128), "dc", slow=True)
        tok_dx = dma(P, "sync", s_bfc[:], d_bfc.rearrange("(c p) -> p c", p=128), "dc", slow=True)
        tok_dx = dma(P, "sync", s_bpr[:], d_bpr.rearrange("(c p) -> p c", p=128), "dc", slow=True)
        tok_consts = tok_dx
        P.emit("vector", lambda E: E.memset(s_cm05[:], -0.5))
        P.emit("vector", lambda E: E.memset(s_cp05[:], 0.5))
        P.emit("vector", lambda E: E.tensor_scalar_mul(s_g8col[:], s_gcol[:], 0.125),
               waits=[("dc", tok_consts)])
        tv = P.emit("vector", lambda E: E.tensor_copy(out=s_betab[:], in_=t_small[:, 0:8]),
                    waits=[("dc", tok_consts)], inc="V")

        t_bwv32 = a(nc.sbuf_tensor("tbwv32", [1, 1024], F32))
        cast_tk2 = [0, 0]
        pstore_tk = [0, 0]
        bw_read = 0
        bwv_read = 0
        for dc in range(DC):
            sem = "da" if dc % 2 == 0 else "db"
            ssem = "dwa" if dc % 2 == 0 else "dwb"
            slot = (dc % 2) * 4096
            tok_l = P.emit("sync",
                           lambda E, dc=dc, slot=slot:
                           E.dma_start(out=t_stage[:, slot:slot + 3072],
                                       in_=d_wqkv[dc * 128:(dc + 1) * 128, :]),
                           waits=[("S", cast_tk2[dc % 2])], inc=sem, n=16)
            cast = P.emit("scalar",
                          lambda E, slot=slot: E.copy(out=t_panel[:, slot:slot + 3072],
                                                      in_=t_stage[:, slot:slot + 3072]),
                          waits=[(sem, tok_l), (ssem, pstore_tk[dc % 2]), ("dc", tok_consts)], inc="S")
            cast_tk2[dc % 2] = cast
            tT = None
            for oc in range(16):
                tT = mm(P, pb[0][:, oc:oc + 1],
                        t_panel[:, slot + oc * 128: slot + (oc + 1) * 128],
                        s_betab[:, dc:dc + 1], start=True, stop=True,
                        waits=[("S", cast), ("V", bw_read)])
            for half in range(2):
                tT = mm(P, pb[1 + half][0:1, 0:512], s_betab[:, dc:dc + 1],
                        t_panel[:, slot + 2048 + half * 512: slot + 2048 + (half + 1) * 512],
                        start=True, stop=True,
                        waits=[("V", bwv_read)])
            tTn = P.emit("tensor", lambda E: E.nop(), inc="T")
            if dc == 0:
                bw_read = P.emit("vector", lambda E: E.tensor_copy(out=s_bw[:], in_=pb[0][:, 0:16]),
                                 waits=[("T", tTn)], inc="V")
                P.emit("vector", lambda E: E.tensor_copy(out=t_bwv32[0:1, 0:512], in_=pb[1][0:1, 0:512]),
                       inc="V")
                bwv_read = P.emit("vector", lambda E: E.tensor_copy(out=t_bwv32[0:1, 512:1024],
                                                                    in_=pb[2][0:1, 0:512]), inc="V")
            else:
                bw_read = P.emit("vector", lambda E: E.tensor_tensor(out=s_bw[:], in0=s_bw[:],
                                                                     in1=pb[0][:, 0:16], op=ALU.add),
                                 waits=[("T", tTn)], inc="V")
                P.emit("vector", lambda E: E.tensor_tensor(out=t_bwv32[0:1, 0:512], in0=t_bwv32[0:1, 0:512],
                                                           in1=pb[1][0:1, 0:512], op=ALU.add), inc="V")
                bwv_read = P.emit("vector", lambda E: E.tensor_tensor(out=t_bwv32[0:1, 512:1024],
                                                                      in0=t_bwv32[0:1, 512:1024],
                                                                      in1=pb[2][0:1, 0:512], op=ALU.add),
                                  inc="V")
            P.emit("scalar",
                   lambda E, dc=dc, slot=slot: E.activation(out=t_panel[:, slot:slot + 1024],
                                                            in_=t_panel[:, slot:slot + 1024],
                                                            func=AF.Copy, scale=s_g8col[:, dc:dc + 1]),
                   waits=[("T", tTn)])
            fold = P.emit("scalar",
                          lambda E, dc=dc, slot=slot: E.activation(out=t_panel[:, slot + 1024:slot + 3072],
                                                                   in_=t_panel[:, slot + 1024:slot + 3072],
                                                                   func=AF.Copy, scale=s_gcol[:, dc:dc + 1]),
                          inc="S")
            P.emit("gpsimd",
                   lambda E, dc=dc, slot=slot: E.dma_start(
                       out=bass.AP(d_wqs.tensor, dc * 128, [[1024, 128], [128 * 1024, 16], [1, 128]]),
                       in_=bass.AP(t_panel, slot, [[8192, 128], [128, 16], [1, 128]])),
                   waits=[("S", fold)], inc=ssem, n=16)
            pstore_tk[dc % 2] = P.emit("gpsimd",
                                       lambda E, dc=dc, slot=slot:
                                       E.dma_start(out=d_wvs[dc * 128:(dc + 1) * 128, :],
                                                   in_=t_panel[:, slot + 2048:slot + 3072]),
                                       inc=ssem, n=16)
        tok_bw = P.emit("vector", lambda E: E.tensor_scalar_mul(s_bw[:, 0:8], s_bw[:, 0:8], 0.125),
                        inc="V")
        tok_bwv = P.emit("vector", lambda E: E.tensor_copy(out=s_bwv[:], in_=t_bwv32[:]), inc="V")

        P.fence()
        st_toks = list(pstore_tk)
        cast_tk = [0, 0]
        for ic in range(DC):
            sem = "da" if ic % 2 == 0 else "db"
            ssem = "dwa" if ic % 2 == 0 else "dwb"
            tok_l = P.emit("sync",
                           lambda E, ic=ic: E.dma_start(out=t_stage[:, (ic % 2) * 4096:(ic % 2) * 4096 + 1024],
                                                        in_=d_wout[ic * 128:(ic + 1) * 128, :]),
                           waits=[("V", cast_tk[ic % 2])], inc=sem, n=16)
            tok_c = P.emit("vector",
                           lambda E, ic=ic: E.tensor_copy(out=t_panel[:, (ic % 2) * 1024:(ic % 2) * 1024 + 1024],
                                                          in_=t_stage[:, (ic % 2) * 4096:(ic % 2) * 4096 + 1024]),
                           waits=[(sem, tok_l), (ssem, st_toks[ic % 2])], inc="V")
            cast_tk[ic % 2] = tok_c
            st_toks[ic % 2] = P.emit(
                "gpsimd",
                lambda E, ic=ic: E.dma_start(out=d_woutb[ic * 128:(ic + 1) * 128, :],
                                             in_=t_panel[:, (ic % 2) * 1024:(ic % 2) * 1024 + 1024]),
                waits=[("V", tok_c)], inc=ssem, n=16)
        P.fence()

        tok_a = dma(P, "sync", t_stage[:16, 0:1024], d_afc, "da")
        tva = P.emit("vector", lambda E: E.tensor_scalar(out=s_a2fc[:], in0=t_stage[:16, 0:1024],
                                                         scalar1=2.0, scalar2=None, op0=ALU.mult),
                     waits=[("da", tok_a)], inc="V")
        tok_a = P.emit("sync", lambda E: E.dma_start(out=t_stage[:16, 0:4096], in_=d_apr),
                       waits=[("V", tva)], inc="da", n=16)
        tva = P.emit("vector", lambda E: E.tensor_scalar(out=s_a2pr[:], in0=t_stage[:16, 0:4096],
                                                         scalar1=2.0, scalar2=None, op0=ALU.mult),
                     waits=[("da", tok_a)], inc="V")
        P.fence()
        btr = [0, 0]
        bev = [0, 0]
        for c in range(HC):
            sem = "da" if c % 2 == 0 else "db"
            tok_l = P.emit("sync",
                           lambda E, c=c: E.dma_start(out=t_stage[:, (c % 2) * 4096:(c % 2) * 4096 + 16],
                                                      in_=d_bfcB[c * 128:(c + 1) * 128, :]),
                           waits=[("T", btr[c % 2])], inc=sem, n=16)
            tt = trans(P, pb[2 + c % 2][0:16, 0:128],
                       t_stage[:, (c % 2) * 4096:(c % 2) * 4096 + 16], s_identf[:],
                       waits=[(sem, tok_l), ("S", bev[c % 2])])
            btr[c % 2] = tt
            bev[c % 2] = P.emit("scalar",
                                lambda E, c=c: E.copy(out=s_btfc[:16, c * 128:(c + 1) * 128],
                                                      in_=pb[2 + c % 2][0:16, 0:128]),
                                waits=[("T", tt)], inc="S")
        P.fence()
        btr = [0, 0]
        bev = [0, 0]
        for c in range(DC):
            sem = "da" if c % 2 == 0 else "db"
            tok_l = P.emit("sync",
                           lambda E, c=c: E.dma_start(out=t_stage[:, (c % 2) * 4096:(c % 2) * 4096 + 16],
                                                      in_=d_bprB[c * 128:(c + 1) * 128, :]),
                           waits=[("T", btr[c % 2])], inc=sem, n=16)
            tt = trans(P, pb[2 + c % 2][0:16, 0:128],
                       t_stage[:, (c % 2) * 4096:(c % 2) * 4096 + 16], s_identf[:],
                       waits=[(sem, tok_l), ("S", bev[c % 2])])
            btr[c % 2] = tt
            bev[c % 2] = P.emit("scalar",
                                lambda E, c=c: E.copy(out=s_btpr[:16, c * 128:(c + 1) * 128],
                                                      in_=pb[2 + c % 2][0:16, 0:128]),
                                waits=[("T", tt)], inc="S")
        P.fence()

        def quant_weight(d_w, nchunk, free, s_scale, s_rscale, s_hscale, am_ncols, d_scrT, a2, bt,
                         panel_cols, tpos_cols):
            P.fence()
            red_tk = [0, 0]
            for c in range(nchunk):
                sl = (c % 2) * 4096
                sem = "da" if c % 2 == 0 else "db"
                tok_l = P.emit("sync",
                               lambda E, c=c, sl=sl: E.dma_start(out=t_stage[:, sl:sl + free],
                                                                 in_=d_w[c * 128:(c + 1) * 128, :]),
                               waits=[("V", red_tk[c % 2])], inc=sem, n=16)
                red_tk[c % 2] = P.emit("vector",
                                       lambda E, c=c, sl=sl: E.reduce_max(out=s_am[:, c:c + 1],
                                                                          in_=t_stage[:, sl:sl + free],
                                                                          axis=AX.X, apply_absolute_value=True),
                                       waits=[(sem, tok_l)], inc="V")
            P.fence()
            tok = P.emit("vector", lambda E: E.reduce_max(out=s_amc[:], in_=s_am[:, 0:am_ncols], axis=AX.X),
                         inc="V")
            tt = trans(P, pb[3][0:1, 0:128], s_amc[:], s_identf[:], waits=[("V", tok)])
            tok = P.emit("vector", lambda E: E.tensor_copy(out=s_amr[:], in_=pb[3][0:1, 0:128]),
                         waits=[("T", tt)], inc="V")
            tok = P.emit("vector", lambda E: E.reduce_max(out=s_sc1[:], in_=s_amr[:], axis=AX.X), inc="V")
            tok = P.emit("vector", lambda E: E.tensor_scalar_mul(s_sc1[:], s_sc1[:], float(np.float32(1.0) / np.float32(127.0))),
                         inc="V")
            tt = mm(P, pb[3][:, 0:1], s_onesf[:], s_sc1[:], start=True, stop=True,
                    waits=[("V", tok)])
            tok = P.emit("vector", lambda E: E.tensor_copy(out=s_scale[:], in_=pb[3][:, 0:1]),
                         waits=[("T", tt)], inc="V")
            tok = P.emit("vector", lambda E: E.reciprocal(out=s_rscale[:], in_=s_scale[:]), inc="V")
            tok = P.emit("vector", lambda E: E.tensor_scalar_mul(s_hscale[:], s_scale[:], 0.5), inc="V")
            P.fence()
            nsl = free // 1024
            tok_store = 0
            ev_bank = {}
            trans_last = 0
            tq_tk = [0, 0]
            gsl = [0]
            stt_tk = [0, 0]
            pend_tw = [None]
            twc = [0] * nchunk
            trans_c = [0] * nchunk
            ev_bank = {}
            store_tk = [0, 0]

            def flush_tw(prog):
                if pend_tw[0] is not None:
                    pf, pc, ptk, ptl = pend_tw[0]
                    tw = prog.emit("scalar", pf, waits=[("V", ptk), ("T", ptl)], inc="S")
                    twc[pc] = tw
                    pend_tw[0] = None

            def quant_slices(c):
                sl = (c % 2) * 4096
                wsl = (c % 2) * 4096
                sem = "da" if c % 2 == 0 else "db"
                tok_l = P.emit("sync",
                               lambda E, c=c, sl=sl: E.dma_start(out=t_stage[:, sl:sl + free],
                                                                 in_=d_w[c * 128:(c + 1) * 128, :]),
                               waits=[("S", tq_tk[c % 2])], inc=sem, n=16)
                tq = 0
                for s in range(nsl):
                    slot = gsl[0] % 2
                    q = slot * 1024
                    gsl[0] += 1
                    src = t_stage[:, sl + s * 1024: sl + (s + 1) * 1024]
                    tq = P.emit("scalar",
                                lambda E, src=src, q=q: E.activation(out=t_q[:, q:q + 1024], in_=src,
                                                                     func=AF.Copy, scale=s_rscale[:]),
                                waits=[(sem, tok_l), ("V", stt_tk[slot])], inc="S")
                    flush_tw(P)
                    tk = P.emit("vector",
                                lambda E, q=q: E.tensor_copy(out=t_k0[:, q:q + 1024], in_=t_q[:, q:q + 1024]),
                                waits=[("S", tq)], inc="V")
                    tk = P.emit("vector",
                                lambda E, q=q: E.tensor_tensor(out=t_d[:, q:q + 1024], in0=t_q[:, q:q + 1024],
                                                               in1=t_k0[:, q:q + 1024], op=ALU.subtract),
                                inc="V")
                    t1 = P.emit("scalar",
                                lambda E, q=q: E.activation(out=t_s1[:, q:q + 1024], in_=t_d[:, q:q + 1024],
                                                            func=AF.Sign, bias=s_cm05[:]),
                                waits=[("V", tk)], inc="S")
                    t1 = P.emit("scalar",
                                lambda E, q=q: E.activation(out=t_s2[:, q:q + 1024], in_=t_d[:, q:q + 1024],
                                                            func=AF.Sign, bias=s_cp05[:]), inc="S")
                    tk = P.emit("vector",
                                lambda E, q=q: E.scalar_tensor_tensor(out=t_s1[:, q:q + 1024],
                                                                      in0=t_k0[:, q:q + 1024], scalar=2.0,
                                                                      in1=t_s1[:, q:q + 1024],
                                                                      op0=ALU.mult, op1=ALU.add),
                                waits=[("S", t1)], inc="V")
                    tk = P.emit("vector",
                                lambda E, q=q: E.tensor_tensor(out=t_d[:, q:q + 1024], in0=t_s1[:, q:q + 1024],
                                                               in1=t_s2[:, q:q + 1024], op=ALU.add),
                                inc="V")
                    stt_tk[slot] = tk
                    def mk_tw(s=s, q=q, wsl=wsl):
                        return lambda E: E.activation(out=t_wqf[:, wsl + s * 1024: wsl + (s + 1) * 1024],
                                                      in_=t_d[:, q:q + 1024],
                                                      func=AF.Copy, scale=s_hscale[:])
                    pend_tw[0] = (mk_tw(), c, tk, trans_c[c - 2] if c >= 2 else 0)
                tq_tk[c % 2] = tq

            def trans_store(c):
                wsl = (c % 2) * 4096
                psl = (c % 2) * free
                nblk = free // 128
                ev_tok = None
                tt = None
                for b in range(nblk):
                    bi = 4 + ((b // 4) % 2)
                    bank = pb[bi]
                    tt = trans(P, bank[:, (b % 4) * 128:(b % 4) * 128 + 128],
                               t_wqf[:, wsl + b * 128: wsl + (b + 1) * 128], s_identf[:],
                               start=True, stop=False,
                               waits=[("S", twc[c]), ("S", ev_bank.get(bi, 0))])
                    tt = mm(P, bank[:, (b % 4) * 128:(b % 4) * 128 + 128],
                            a2[:16, b * 128:(b + 1) * 128],
                            bt[:16, c * 128:(c + 1) * 128],
                            start=False, stop=True)
                    if b % 4 == 3:
                        ev_tok = P.emit("scalar",
                                        lambda E, b=b, bank=bank, psl=psl:
                                        E.copy(out=t_panel[:, psl + (b - 3) * 128: psl + (b + 1) * 128],
                                               in_=bank[:]),
                                        waits=[("T", tt), ("dwa" if c % 2 == 0 else "dwb", store_tk[c % 2])],
                                        inc="S")
                        ev_bank[bi] = ev_tok
                trans_c[c] = tt
                store_tk[c % 2] = P.emit("gpsimd",
                                         lambda E, c=c, psl=psl:
                                         E.dma_start(out=d_scrT[c * 128:(c + 1) * 128, :],
                                                     in_=t_panel[:, psl:psl + free]),
                                         waits=[("S", ev_tok)],
                                         inc="dwa" if c % 2 == 0 else "dwb", n=16)

            for c in range(nchunk):
                quant_slices(c)
                flush_tw(P)
                if c >= 1:
                    trans_store(c - 1)
            trans_store(nchunk - 1)
            P.fence()

        quant_weight(d_wfc, HC, 1024, s_scfc, s_rsfc, s_hsfc, HC, d_wfcT, s_a2fc, s_btfc, 1024, None)
        quant_weight(d_wpr, DC, 4096, s_scpr, s_rspr, s_hspr, DC, d_wprT, s_a2pr, s_btpr, 4096, None)

        if dbg:
            P.fence()
            P.emit("sync", lambda E: E.nop(),
                   waits=[(k, cnt.get(k, 0)) for k in ("T", "V", "S")])
            pass
            dma(P, "sync", d_dbg["bw"], s_bw[:], "dc")
            dma(P, "sync", d_dbg["bwv"], s_bwv[:], "dc")
            P.emit("vector", lambda E: E.tensor_copy(out=s_am[:, 0:1], in_=s_scfc[:]))
            P.emit("vector", lambda E: E.tensor_copy(out=s_am[:, 1:2], in_=s_rsfc[:]))
            P.emit("vector", lambda E: E.tensor_copy(out=s_am[:, 2:3], in_=s_scpr[:]))
            tvd = P.emit("vector", lambda E: E.tensor_copy(out=s_am[:, 3:4], in_=s_rspr[:]), inc="V")
            dma(P, "sync", d_dbg["scl"], s_am[:, 0:8], "dc")
            dma(P, "sync", d_dbg["btfc"], s_btfc[:], "dc")
            dma(P, "sync", d_dbg["a2fc"], s_a2fc[:], "dc")
            dma(P, "sync", d_dbg["wfcT0"], d_wfcT[0:128, :], "dc")
            dma(P, "sync", d_dbg["wprT0"], d_wprT[0:128, :], "dc")
        for eng in ENGS:
            P.emit(eng, lambda E: E.nop(), inc="BAR")

        with nc.Block("setup") as b1:
            P.replay(b1, sems)

    s_x = e(nc.sbuf_tensor("xsb", [128, 2 * G * D], F32))
    s_xnb = e(nc.sbuf_tensor("xnb", [128, D], F32))
    s_xn1 = e(nc.sbuf_tensor("xn1", [128, DC * TILE], BF16))
    s_xn2 = e(nc.sbuf_tensor("xn2", [128, DC * TILE], BF16))
    s_q = e(nc.sbuf_tensor("qsb", [128, 8 * TILE], BF16))
    s_k = e(nc.sbuf_tensor("ksb", [128, 8 * TILE], BF16))
    s_v = e(nc.sbuf_tensor("vsb", [128, G * D], BF16))
    s_kprev = e(nc.sbuf_tensor("kprev", [128, 8 * 128], BF16))
    s_vprev = e(nc.sbuf_tensor("vprev", [128, D], BF16))
    s_ao = e(nc.sbuf_tensor("aosb", [128, 8 * TILE], BF16))
    s_h = e(nc.sbuf_tensor("hsb", [128, HC * TILE], BF16))
    s_sm = e(nc.sbuf_tensor("smbuf", [128, 7 * 256], F32))
    s_pT = e(nc.sbuf_tensor("pT", [128, 3 * 256], BF16))
    s_stat = e(nc.sbuf_tensor("stat", [128, 80], F32))
    s_ofm = e(nc.sbuf_tensor("ofm", [128, 512], F32))
    s_wfcb = e(nc.sbuf_tensor("wfcb", [128, 3 * 1024], BF16))
    s_wqb = e(nc.sbuf_tensor("wqb", [128, 2 * 2048], BF16))
    s_wvb = e(nc.sbuf_tensor("wvb", [128, 2 * 2048], BF16))
    s_wprb = e(nc.sbuf_tensor("wprb", [128, 3 * 4096], BF16))
    s_woutb = e(nc.sbuf_tensor("woutb", [128, 3 * 1024], BF16))

    NEGMAX, ROWSUM, RECIP = 0, 52, 64
    SUM, SSQ, MU, MSQ, VARP, R1, V2P, ASC, NMA, T0 = 8, 12, 16, 20, 24, 28, 32, 36, 40, 44

    M = Prog(cnt)
    M.fence_snap = {}
    for eng in ENGS:
        M.emit(eng, lambda E: E.nop(), waits=[("BAR", cnt["BAR"])])

    bank_free = [[] for _ in range(8)]

    def bmm(out_bank, cols, lhsT, rhs, start, stop, tp=None, waits=(), prow=None):
        w = list(waits)
        if start:
            w += bank_free[out_bank]
        ap = pb[out_bank][:, cols[0]:cols[1]] if prow is None else pb[out_bank][prow[0]:prow[1], cols[0]:cols[1]]
        tok = mm(M, ap, lhsT, rhs, start=start, stop=stop, tp=tp, waits=w)
        if start:
            bank_free[out_bank] = []
        return tok

    def btrans(out_bank, cols, in_ap, start=True, stop=True, waits=()):
        w = list(waits) + (bank_free[out_bank] if start else [])
        tok = trans(M, pb[out_bank][:, cols[0]:cols[1]], in_ap, s_identf[:],
                    start=start, stop=stop, waits=w)
        if start:
            bank_free[out_bank] = []
        return tok

    def bread(out_bank, sem, tok):
        bank_free[out_bank] = [(sem, tok)]

    wout_cons = [0, 0, 0]
    wfc_cons = [0, 0, 0]
    wpr_cons = [0, 0, 0]
    wq_cons = [0, 0]
    wv_cons = [0, 0]
    xnb_cons = [0, 0]
    xn1_cons = 0
    xn2_cons = 0
    qk_write_guard = [0, 0]
    v_write_guard = 0
    ao_guard = 0
    h_guard = 0
    ofm_trans_last = [0]
    ofm_cons = [0, 0]
    prev_expC = [0] * 7
    prev_trE = [0] * 7
    prev_H = [("V", 0), ("S", 0)]
    store_tok = [0, 0]
    kp_tok = 0
    vp_tok = 0
    st = {}

    def emit_stats(wait_sem, wait_tok):
        tv = ts = None
        for g in range(G):
            xb = st["xbase"]
            tv = M.emit("vector",
                        lambda E, g=g, xb=xb: E.tensor_scalar(out=s_xn1[:, 0:1024],
                                                              in0=s_x[:, xb + g * D: xb + (g + 1) * D],
                                                              scalar1=1.0, scalar2=None,
                                                              op0=ALU.mult, op1=ALU.add,
                                                              accum_out=s_stat[:, SUM + g:SUM + g + 1]),
                        waits=[(wait_sem, wait_tok), ("T", xn1_cons)], inc="V")
            ts = M.emit("scalar",
                        lambda E, g=g, xb=xb: E.activation(out=s_xn1[:, 1024:2048],
                                                           in_=s_x[:, xb + g * D: xb + (g + 1) * D],
                                                           func=AF.Square,
                                                           accum_out=s_stat[:, SSQ + g:SSQ + g + 1]),
                        waits=[(wait_sem, wait_tok), ("T", xn1_cons)], inc="S")
        c = 1.0 / 1024.0
        tv = M.emit("vector", lambda E: E.tensor_scalar_mul(s_stat[:, MU:MU + 4], s_stat[:, SUM:SUM + 4], c),
                    waits=[("V", tv)], inc="V")
        tv = M.emit("vector", lambda E: E.tensor_scalar_mul(s_stat[:, MSQ:MSQ + 4], s_stat[:, SSQ:SSQ + 4], c),
                    waits=[("S", ts)], inc="V")
        tv = M.emit("vector", lambda E: E.tensor_tensor(out=s_stat[:, T0:T0 + 4], in0=s_stat[:, MU:MU + 4],
                                                        in1=s_stat[:, MU:MU + 4], op=ALU.mult), inc="V")
        tv = M.emit("vector", lambda E: E.tensor_tensor(out=s_stat[:, VARP:VARP + 4], in0=s_stat[:, MSQ:MSQ + 4],
                                                        in1=s_stat[:, T0:T0 + 4], op=ALU.subtract), inc="V")
        tv = M.emit("vector", lambda E: E.tensor_scalar_add(s_stat[:, VARP:VARP + 4], s_stat[:, VARP:VARP + 4],
                                                            1e-5), inc="V")
        ts2 = M.emit("scalar", lambda E: E.activation(out=s_stat[:, T0:T0 + 4], in_=s_stat[:, VARP:VARP + 4],
                                                      func=AF.Ln),
                     waits=[("V", tv)], inc="S")
        ts2 = M.emit("scalar", lambda E: E.activation(out=s_stat[:, R1:R1 + 4], in_=s_stat[:, T0:T0 + 4],
                                                      func=AF.Exp, scale=s_cm05[:]), inc="S")
        return tv, ts2

    def emit_norm_transpose(dst, a_col, nma_col, wait_s_tok):
        ev_last = None
        xb = st["xbase"]
        for g in range(G):
            tok_x = M.emit("vector",
                           lambda E, g=g, xb=xb, a_col=a_col, nma_col=nma_col:
                           E.tensor_scalar(out=s_xnb[:, 0:D],
                                           in0=s_x[:, xb + g * D: xb + (g + 1) * D],
                                           scalar1=s_stat[:, a_col + g:a_col + g + 1],
                                           scalar2=s_stat[:, nma_col + g:nma_col + g + 1],
                                           op0=ALU.mult, op1=ALU.add),
                           waits=[("T", xnb_cons[0]), ("S", wait_s_tok)], inc="V")
            tt = None
            for half in range(2):
                bank = half
                for i4 in range(4):
                    dc = half * 4 + i4
                    tt = btrans(bank, (i4 * 128, (i4 + 1) * 128),
                                s_xnb[:, dc * 128:(dc + 1) * 128],
                                start=True, stop=True,
                                waits=[("V", tok_x)])
                dst_ap = bass.AP(dst, half * 4 * TILE + g * 128,
                                 [[DC * TILE, 128], [TILE, 4], [1, 128]])
                tok = M.emit("scalar",
                             lambda E, bank=bank, dst_ap=dst_ap:
                             E.copy(out=dst_ap, in_=pb[bank][:].rearrange("p (a b) -> p a b", b=128)),
                             waits=[("T", tt)], inc="S")
                bread(bank, "S", tok)
                ev_last = tok
            xnb_cons[0] = tt
        return ev_last

    def emit_qkv(t, xn1_ev):
        qk_ev = []
        for op in range(8):
            sl = op % 2
            tok_w = M.emit("gpsimd",
                           lambda E, op=op, sl=sl: E.dma_start(
                               out=s_wqb[:, sl * 2048:(sl + 1) * 2048].rearrange("p (a d) -> p a d", d=1024),
                               in_=d_wqs[op * 256:(op + 1) * 256, :].rearrange("(a p) d -> p a d", p=128)),
                           waits=[("T", wq_cons[sl])], inc="wq%d" % sl, n=16)
            for j in range(2):
                oc = 2 * op + j
                bank = 2 + (oc % 2)
                tt = None
                for dc in range(DC):
                    tt = bmm(bank, (0, 512),
                             s_wqb[:, sl * 2048 + j * 1024 + dc * 128: sl * 2048 + j * 1024 + (dc + 1) * 128],
                             s_xn1[:, dc * TILE:(dc + 1) * TILE],
                             start=(dc == 0), stop=(dc == DC - 1),
                             waits=[("wq%d" % sl, tok_w), ("S", xn1_ev),
                                    ("T", qk_write_guard[0]), ("V", kp_tok)])
                dst = s_q if oc < 8 else s_k
                fc = oc % 8
                tok = M.emit("scalar",
                             lambda E, dst=dst, fc=fc, oc=oc, bank=bank:
                             E.activation(out=dst[:, fc * TILE:(fc + 1) * TILE], in_=pb[bank][:],
                                          func=AF.Identity, bias=s_bw[:, oc:oc + 1]),
                             waits=[("T", tt)], inc="S")
                bread(bank, "S", tok)
                qk_ev.append(tok)
            wq_cons[sl] = tt
        v_ev = [0] * G
        ttl = None
        for dp in range(4):
            sl = dp % 2
            tok_w = M.emit("gpsimd",
                           lambda E, dp=dp, sl=sl: E.dma_start(
                               out=s_wvb[:, sl * 2048:(sl + 1) * 2048].rearrange("p (a d) -> p a d", d=1024),
                               in_=d_wvs[dp * 256:(dp + 1) * 256, :].rearrange("(a p) d -> p a d", p=128)),
                           waits=[("T", wv_cons[sl])], inc="wv%d" % sl, n=16)
            for j in range(2):
                dc = 2 * dp + j
                for g in range(G):
                    for half in range(2):
                        bi = g * 2 + half
                        ttl = bmm(bi, (0, 512),
                                  s_xn1[:, dc * TILE + g * 128: dc * TILE + (g + 1) * 128],
                                  s_wvb[:, sl * 2048 + j * 1024 + half * 512: sl * 2048 + j * 1024 + (half + 1) * 512],
                                  start=(dc == 0), stop=False,
                                  waits=[("wv%d" % sl, tok_w), ("S", xn1_ev),
                                         ("T", v_write_guard), ("V", vp_tok)])
            wv_cons[sl] = ttl
        for g in range(G):
            for half in range(2):
                bi = g * 2 + half
                tt = bmm(bi, (0, 512), s_onesb[:], s_bwv[0:1, half * 512:(half + 1) * 512],
                         start=False, stop=True)
                tok = M.emit("vector",
                             lambda E, g=g, half=half, bi=bi:
                             E.tensor_copy(out=s_v[:, g * D + half * 512: g * D + (half + 1) * 512],
                                           in_=pb[bi][:]),
                             waits=[("T", tt)], inc="V")
                bread(bi, "V", tok)
                v_ev[g] = tok
        return qk_ev, v_ev

    def make_attention_steps(t, qk_ev, v_ev):
        seq_start = (t % SEQ_TILES == 0)
        NI = G * H
        ctx = []
        for w in range(G):
            fw = (seq_start and w == 0)
            for h in range(H):
                ctx.append((w, h, fw))
        tokA = [0] * NI; tokB = [0] * NI; tokC = [0] * NI; tokD = [0] * NI
        tokE = [0] * NI; tokF = [0] * NI; tokG = [0] * NI; tokH = [0] * NI
        engH = [0] * NI
        res = {"tokG": tokG, "tokH": tokH, "engH": engH, "tokA": tokA, "tokB": tokB,
               "tokC": tokC, "tokE": tokE}

        def stA(i):
            w, h, fw = ctx[i]
            po = 64 * (h % 2); fc = h // 2
            qs = s_q[po:po + 64, fc * TILE + w * 128: fc * TILE + (w + 1) * 128]
            bank = i % 2
            wts = [("S", qk_ev[8 + fc]), ("V", tokB[i - 2] if i >= 2 else 0)]
            if fw:
                tokA[i] = bmm(bank, (0, 128), qs, s_k[po:po + 64, fc * TILE: fc * TILE + 128],
                              start=True, stop=True, tp=(po, 0), waits=wts)
            elif w == 0:
                bmm(bank, (0, 128), qs, s_kprev[po:po + 64, fc * 128:(fc + 1) * 128],
                    start=True, stop=True, tp=(po, 0), waits=wts)
                tokA[i] = bmm(bank, (128, 256), qs, s_k[po:po + 64, fc * TILE: fc * TILE + 128],
                              start=True, stop=True, tp=(po, 0))
            else:
                tokA[i] = bmm(bank, (0, 256), qs,
                              s_k[po:po + 64, fc * TILE + (w - 1) * 128: fc * TILE + (w + 1) * 128],
                              start=True, stop=True, tp=(po, 0), waits=wts)

        def stB(i):
            w, h, fw = ctx[i]
            nk = 128 if fw else 256
            wh = i % 7
            bank = i % 2
            mof = 128 if fw else 0
            tok = M.emit("vector",
                         lambda E, wh=wh, nk=nk, mof=mof, bank=bank:
                         E.tensor_tensor(out=s_sm[:, wh * 256: wh * 256 + nk], in0=pb[bank][:, 0:nk],
                                         in1=s_mask[:, mof:mof + nk], op=ALU.add),
                         waits=[("T", tokA[i]),
                                ("S", tokC[i - 7] if i >= 7 else prev_expC[i]),
                                ("T", tokE[i - 7] if i >= 7 else prev_trE[i])], inc="V")
            bread(bank, "V", tok)
            tokB[i] = M.emit("vector",
                             lambda E, wh=wh, nk=nk: E.reduce_max(out=s_stat[:, NEGMAX + wh:NEGMAX + wh + 1],
                                                                  in_=s_sm[:, wh * 256: wh * 256 + nk],
                                                                  axis=AX.X, negate=True),
                             inc="V")

        def stC(i):
            w, h, fw = ctx[i]
            nk = 128 if fw else 256
            wh = i % 7
            tokC[i] = M.emit("scalar",
                             lambda E, wh=wh, nk=nk: E.activation(out=s_sm[:, wh * 256: wh * 256 + nk],
                                                                  in_=s_sm[:, wh * 256: wh * 256 + nk],
                                                                  func=AF.Exp,
                                                                  bias=s_stat[:, NEGMAX + wh:NEGMAX + wh + 1],
                                                                  accum_out=s_stat[:, ROWSUM + wh:ROWSUM + wh + 1]),
                             waits=[("V", tokB[i]),
                                    ("T", tokE[i - 7] if i >= 7 else prev_trE[i])], inc="S")

        def stD(i):
            w, h, fw = ctx[i]
            nk = 128 if fw else 256
            wh = i % 7
            M.emit("vector",
                   lambda E, wh=wh: E.reciprocal(out=s_stat[:, RECIP + wh:RECIP + wh + 1],
                                                 in_=s_stat[:, ROWSUM + wh:ROWSUM + wh + 1]),
                   waits=[("S", tokC[i])], inc="V")
            tokD[i] = M.emit("vector",
                             lambda E, wh=wh, nk=nk: E.tensor_scalar_mul(s_sm[:, wh * 256: wh * 256 + nk],
                                                                         s_sm[:, wh * 256: wh * 256 + nk],
                                                                         s_stat[:, RECIP + wh:RECIP + wh + 1]),
                             inc="V")

        def stE(i):
            w, h, fw = ctx[i]
            nk = 128 if fw else 256
            wh = i % 7
            tt2 = None
            tb = 2 + i % 2
            for kc in range(nk // 128):
                tt2 = btrans(tb, (kc * 128, (kc + 1) * 128),
                             s_sm[:, wh * 256 + kc * 128: wh * 256 + (kc + 1) * 128],
                             start=True, stop=True,
                             waits=[("V", tokD[i])])
            tokE[i] = tt2

        def stF(i):
            w, h, fw = ctx[i]
            nk = 128 if fw else 256
            wh = i % 7
            pslot = i % 3
            tb = 2 + i % 2
            tokF[i] = M.emit("scalar",
                             lambda E, pslot=pslot, nk=nk, tb=tb:
                             E.copy(out=s_pT[:, pslot * 256: pslot * 256 + nk], in_=pb[tb][:, 0:nk]),
                             waits=[("T", tokE[i])], inc="S")
            bread(tb, "S", tokF[i])

        def stG(i):
            w, h, fw = ctx[i]
            wh = i % 3
            po = 64 * (h % 2)
            gb = 4 + i % 2
            pvw = [("S", tokF[i]), ("V", v_ev[w])]
            if fw:
                tokG[i] = bmm(gb, (0, 128), s_v[:, w * D + h * DH: w * D + (h + 1) * DH],
                              s_pT[:, wh * 256: wh * 256 + 128],
                              start=True, stop=True, tp=(0, po), waits=pvw, prow=(po, po + 64))
            else:
                vprev_ap = (s_vprev[:, h * DH:(h + 1) * DH] if w == 0
                            else s_v[:, (w - 1) * D + h * DH: (w - 1) * D + h * DH + DH])
                bmm(gb, (0, 128), vprev_ap, s_pT[:, wh * 256: wh * 256 + 128],
                    start=True, stop=False, tp=(0, po), waits=pvw, prow=(po, po + 64))
                tokG[i] = bmm(gb, (0, 128), s_v[:, w * D + h * DH: w * D + (h + 1) * DH],
                              s_pT[:, wh * 256 + 128: wh * 256 + 256],
                              start=False, stop=True, tp=(0, po), prow=(po, po + 64))

        def stH(i):
            w, h, fw = ctx[i]
            po = 64 * (h % 2); fc = h // 2
            gb = 4 + i % 2
            if i % 2 == 0:
                tokH[i] = M.emit("vector",
                                 lambda E, po=po, fc=fc, w=w, gb=gb:
                                 E.tensor_copy(out=s_ao[po:po + 64, fc * TILE + w * 128: fc * TILE + (w + 1) * 128],
                                               in_=pb[gb][po:po + 64, 0:128]),
                                 waits=[("T", tokG[i]), ("T", ao_guard)], inc="V")
                engH[i] = 0
                bread(gb, "V", tokH[i])
            else:
                tokH[i] = M.emit("scalar",
                                 lambda E, po=po, fc=fc, w=w, gb=gb:
                                 E.copy(out=s_ao[po:po + 64, fc * TILE + w * 128: fc * TILE + (w + 1) * 128],
                                        in_=pb[gb][po:po + 64, 0:128]),
                                 waits=[("T", tokG[i]), ("T", ao_guard)], inc="S")
                engH[i] = 1
                bread(gb, "S", tokH[i])

        stages = (stA, stB, stC, stD, stE, stF, stG, stH)
        offs = (0, 2, 4, 5, 7, 8, 10, 11)
        steps = []
        for j in range(NI + offs[-1]):
            def step(j=j):
                for k in range(len(stages) - 1, -1, -1):
                    i = j - offs[k]
                    if 0 <= i < NI:
                        stages[k](i)
            steps.append(step)
        return steps, res

    def emit_ffn_groups(tp):
        groups = []
        xb = (tp % 2) * G * D
        h_toks = st.setdefault("h_toks", {})
        h_toks.clear()

        def mk_fc(c):
            def f():
                sl = c % 3
                tok_w = M.emit("gpsimd",
                               lambda E, c=c, sl=sl: E.dma_start(out=s_wfcb[:, sl * 1024:(sl + 1) * 1024],
                                                                 in_=d_wfcT[c * 128:(c + 1) * 128, :]),
                               waits=[("T", wfc_cons[sl])], inc="wf%d" % sl, n=16)
                bank = 6 + (c % 2)
                tt = None
                for dc in range(DC):
                    tt = bmm(bank, (0, 512),
                             s_wfcb[:, sl * 1024 + dc * 128: sl * 1024 + (dc + 1) * 128],
                             s_xn2[:, dc * TILE:(dc + 1) * TILE],
                             start=(dc == 0), stop=(dc == DC - 1),
                             waits=[("wf%d" % sl, tok_w), ("S", st["xn2_ev"]),
                                    ("T", h_guard_g[0])])
                wfc_cons[sl] = tt
                tok = M.emit("scalar",
                             lambda E, c=c, bank=bank: E.activation(out=s_h[:, c * TILE:(c + 1) * TILE],
                                                                    in_=pb[bank][:], func=AF.Gelu,
                                                                    bias=s_bfc[:, c:c + 1]),
                             waits=[("T", tt)], inc="S")
                bread(bank, "S", tok)
                h_toks[c] = tok
            return f

        def mk_proj(do):
            def f():
                sl = do % 3
                tok_w = M.emit("gpsimd",
                               lambda E, do=do, sl=sl: E.dma_start(out=s_wprb[:, sl * 4096:(sl + 1) * 4096],
                                                                   in_=d_wprT[do * 128:(do + 1) * 128, :]),
                               waits=[("T", wpr_cons[sl])], inc="wp%d" % sl, n=16)
                tt = None
                for c in range(HC):
                    tt = bmm(6, (0, 512),
                             s_wprb[:, sl * 4096 + c * 128: sl * 4096 + (c + 1) * 128],
                             s_h[:, c * TILE:(c + 1) * TILE],
                             start=(c == 0), stop=(c == HC - 1),
                             waits=[("wp%d" % sl, tok_w), ("S", h_toks[HC - 1])])
                wpr_cons[sl] = tt
                st["pass2_last"] = tt
                tok_ev = M.emit("scalar",
                                lambda E, do=do: E.activation(out=s_ofm[:, 0:512], in_=pb[6][:],
                                                              func=AF.Identity, bias=s_bpr[:, do:do + 1]),
                                waits=[("T", tt), ("T", ofm_trans_last[0])], inc="S")
                bread(6, "S", tok_ev)
                tt2 = None
                for g in range(G):
                    tt2 = btrans(7, (g * 128, (g + 1) * 128),
                                 s_ofm[:, g * 128:(g + 1) * 128],
                                 start=True, stop=True, waits=[("S", tok_ev)])
                ofm_trans_last[0] = tt2
                tok_res = M.emit("vector",
                                 lambda E, do=do, xb=xb:
                                 E.scalar_tensor_tensor(
                                     out=bass.AP(s_x, xb + do * 128, [[2 * G * D, 128], [D, 4], [1, 128]]),
                                     in0=pb[7][:].rearrange("p (g c) -> p g c", c=128), scalar=1.0,
                                     in1=bass.AP(s_x, xb + do * 128, [[2 * G * D, 128], [D, 4], [1, 128]]),
                                     op0=ALU.mult, op1=ALU.add),
                                 waits=[("T", tt2)], inc="V")
                bread(7, "V", tok_res)
                st["last_res"] = tok_res
            return f

        for c in range(HC):
            groups.append(mk_fc(c))
        for do in range(DC):
            groups.append(mk_proj(do))

        def mk_store():
            def f():
                par = tp % 2
                store_tok[par] = M.emit("sync",
                                        lambda E, tp=tp, xb=xb: E.dma_start(
                                            out=d_y[tp * TILE:(tp + 1) * TILE, :].rearrange("(g p) d -> p g d", p=128),
                                            in_=s_x[:, xb:xb + G * D].rearrange("p (g d) -> p g d", d=D)),
                                        waits=[("V", st["last_res"])], inc="dy%d" % par, n=16)
            return f
        groups.append(mk_store())
        return groups

    h_guard_g = [0]

    def emit_outproj(t, pv_last_waits):
        tt = None
        for ic in range(DC):
            sl = ic % 3
            tok_w = M.emit("gpsimd",
                           lambda E, ic=ic, sl=sl: E.dma_start(out=s_woutb[:, sl * 1024:(sl + 1) * 1024],
                                                               in_=d_woutb[ic * 128:(ic + 1) * 128, :]),
                           waits=[("T", wout_cons[sl])], inc="wo%d" % sl, n=16)
            for g in range(G):
                for half in range(2):
                    tt = bmm(2 * g + half, (0, 512),
                             s_ao[:, ic * TILE + g * 128: ic * TILE + (g + 1) * 128],
                             s_woutb[:, sl * 1024 + half * 512: sl * 1024 + (half + 1) * 512],
                             start=(ic == 0), stop=(ic == DC - 1),
                             waits=[("wo%d" % sl, tok_w)] + (pv_last_waits if ic == 0 else []))
            wout_cons[sl] = tt
        tok_T = M.emit("tensor", lambda E: E.nop(), inc="T")
        tvlast = None
        xb = st["xbase"]
        for g in range(G):
            for half in range(2):
                bi = 2 * g + half
                tvlast = M.emit("vector",
                                lambda E, g=g, half=half, bi=bi, xb=xb:
                                E.scalar_tensor_tensor(
                                    out=s_x[:, xb + g * D + half * 512: xb + g * D + (half + 1) * 512],
                                    in0=pb[bi][:], scalar=1.0,
                                    in1=s_x[:, xb + g * D + half * 512: xb + g * D + (half + 1) * 512],
                                    op0=ALU.mult, op1=ALU.add),
                                waits=[("T", tok_T)], inc="V")
                bread(bi, "V", tvlast)
        return tok_T, tvlast

    pending_ffn = None
    for t in range(NT):
        st["xbase"] = (t % 2) * G * D
        par = t % 2
        tok_x = M.emit("sync",
                       lambda E, t=t, xb=st["xbase"]: E.dma_start(
                           out=s_x[:, xb:xb + G * D].rearrange("p (g d) -> p g d", d=D),
                           in_=d_x[t * TILE:(t + 1) * TILE, :].rearrange("(g p) d -> p g d", p=128)),
                       waits=[("dy%d" % par, store_tok[par])], inc="dx", n=16)

        tv, ts = emit_stats("dx", tok_x)
        T1 = 48
        tv = M.emit("vector", lambda E: E.reciprocal(out=s_stat[:, T1:T1 + 4], in_=s_stat[:, VARP:VARP + 4]),
                    inc="V")
        tv = M.emit("vector", lambda E: E.tensor_scalar_add(s_stat[:, V2P:V2P + 4], s_stat[:, VARP:VARP + 4],
                                                            -1e-5), inc="V")
        tv = M.emit("vector", lambda E: E.tensor_tensor(out=s_stat[:, V2P:V2P + 4], in0=s_stat[:, V2P:V2P + 4],
                                                        in1=s_stat[:, T1:T1 + 4], op=ALU.mult), inc="V")
        tv = M.emit("vector", lambda E: E.tensor_scalar_add(s_stat[:, V2P:V2P + 4], s_stat[:, V2P:V2P + 4],
                                                            1e-5), inc="V")
        ts = M.emit("scalar", lambda E: E.activation(out=s_stat[:, T1:T1 + 4], in_=s_stat[:, V2P:V2P + 4],
                                                     func=AF.Ln),
                    waits=[("V", tv), ("S", ts)], inc="S")
        ts = M.emit("scalar", lambda E: E.activation(out=s_stat[:, V2P:V2P + 4], in_=s_stat[:, T1:T1 + 4],
                                                     func=AF.Exp, scale=s_cm05[:]), inc="S")
        tv = M.emit("vector", lambda E: E.tensor_tensor(out=s_stat[:, ASC:ASC + 4], in0=s_stat[:, R1:R1 + 4],
                                                        in1=s_stat[:, V2P:V2P + 4], op=ALU.mult),
                    waits=[("S", ts)], inc="V")
        tv = M.emit("vector", lambda E: E.scalar_tensor_tensor(out=s_stat[:, NMA:NMA + 4],
                                                               in0=s_stat[:, MU:MU + 4],
                                                               scalar=-1.0, in1=s_stat[:, ASC:ASC + 4],
                                                               op0=ALU.mult, op1=ALU.mult), inc="V")
        xn1_ev = emit_norm_transpose(s_xn1, ASC, NMA, 0)

        qk_ev, v_ev = emit_qkv(t, xn1_ev)

        att_steps, att = make_attention_steps(t, qk_ev, v_ev)
        if pending_ffn is not None:
            ng, ns = len(pending_ffn), len(att_steps)
            si = 0
            lead = min(ns, 10)
            while si < lead:
                att_steps[si]()
                si += 1
            for gi, grp in enumerate(pending_ffn):
                grp()
                want = min(ns, lead + (gi + 1) * (ns - lead) // ng)
                while si < want:
                    att_steps[si]()
                    si += 1
            while si < ns:
                att_steps[si]()
                si += 1
        else:
            for stp in att_steps:
                stp()
        h_guard_g[0] = st.get("pass2_last", 0)

        NI = G * H
        qk_write_guard[0] = att["tokA"][NI - 1]
        v_write_guard = att["tokG"][NI - 1]
        for k in range(7):
            prev_expC[(NI - 7 + k) % 7] = att["tokC"][NI - 7 + k]
            prev_trE[(NI - 7 + k) % 7] = att["tokE"][NI - 7 + k]

        if (t + 1) % SEQ_TILES != 0:
            kp_tok = M.emit("vector",
                            lambda E: E.tensor_copy(
                                out=s_kprev[:].rearrange("p (f c) -> p f c", c=128),
                                in_=s_k[:].rearrange("p (f t) -> p f t", t=TILE)[:, :, TILE - 128:TILE]),
                            waits=[("T", att["tokG"][15])], inc="V")
            vp_tok = M.emit("vector",
                            lambda E: E.tensor_copy(out=s_vprev[:], in_=s_v[:, (G - 1) * D: G * D]),
                            waits=[("T", att["tokG"][15])], inc="V")

        li = NI - 1
        pvw = [(("V", "S")[att["engH"][li]], att["tokH"][li]),
               (("V", "S")[att["engH"][li - 1]], att["tokH"][li - 1])]
        opj_T, tvlast = emit_outproj(t, pvw)
        ao_guard = opj_T

        tv2, ts2 = emit_stats("V", tvlast)
        tsn = None
        for g in range(G):
            tsn = M.emit("vector",
                         lambda E, g=g: E.scalar_tensor_tensor(out=s_stat[:, NMA + g:NMA + g + 1],
                                                               in0=s_stat[:, MU + g:MU + g + 1], scalar=-1.0,
                                                               in1=s_stat[:, R1 + g:R1 + g + 1],
                                                               op0=ALU.mult, op1=ALU.mult),
                         waits=[("S", ts2)], inc="V")
        st["xn2_ev"] = emit_norm_transpose(s_xn2, R1, NMA, 0)
        xn2_cons = 0
        pending_ffn = emit_ffn_groups(t)

    for grp in pending_ffn:
        grp()

    M.emit("sync", lambda E: E.nop(), waits=[("dy0", cnt.get("dy0", 0)), ("dy1", cnt.get("dy1", 0))])

    with nc.Block("main") as b2:
        M.replay(b2, sems)

    stack.close()
    return nc


_NC_CACHE = None


def kernel(L0_attn_x, L0_attn_ln_gamma, L0_attn_ln_beta, L0_attn_w_qkv, L0_attn_w_out,
           L0_ffn_W_fc, L0_ffn_b_fc, L0_ffn_A_fc, L0_ffn_B_fc,
           L0_ffn_W_proj, L0_ffn_b_proj, L0_ffn_A_proj, L0_ffn_B_proj):
    global _NC_CACHE
    x = np.ascontiguousarray(np.asarray(L0_attn_x, dtype=np.float32))
    B, N, Dm = x.shape
    ncores = 8
    per = B // ncores

    if _NC_CACHE is None:
        _NC_CACHE = build_program(dbg=DBG)
    nc = _NC_CACHE

    ident_f = np.eye(128, dtype=np.float32)
    ii = np.arange(128)[:, None]
    jj = np.arange(256)[None, :]
    prev = jj < 128
    jjm = np.where(prev, jj, jj - 128)
    bad = np.where(prev, jjm < ii, jjm > ii)
    mask_m = np.where(bad, np.float32(NEG), np.float32(0.0)).astype(BFNP)
    ones_f = np.ones((1, 128), dtype=np.float32)

    common = dict(
        gamma=np.asarray(L0_attn_ln_gamma, np.float32),
        beta=np.asarray(L0_attn_ln_beta, np.float32),
        wqkv=np.asarray(L0_attn_w_qkv, np.float32),
        wout=np.asarray(L0_attn_w_out, np.float32),
        wfc=np.asarray(L0_ffn_W_fc, np.float32),
        bfc=np.asarray(L0_ffn_b_fc, np.float32),
        afc=np.asarray(L0_ffn_A_fc, np.float32),
        bfcB=np.asarray(L0_ffn_B_fc, np.float32),
        wpr=np.asarray(L0_ffn_W_proj, np.float32),
        bpr=np.asarray(L0_ffn_b_proj, np.float32),
        apr=np.asarray(L0_ffn_A_proj, np.float32),
        bprB=np.asarray(L0_ffn_B_proj, np.float32),
        identf=ident_f, maskm=mask_m, onesf=ones_f,
        onesb=np.ones((1, 128), dtype=BFNP),
    )
    in_maps = []
    for c in range(ncores):
        m = dict(common)
        m["x"] = np.ascontiguousarray(x[c * per:(c + 1) * per].reshape(TOK, Dm))
        in_maps.append(m)

    res = run_bass_kernel_spmd(nc, in_maps, core_ids=list(range(ncores)))
    out = np.empty((B, N, Dm), np.float32)
    for c in range(ncores):
        out[c * per:(c + 1) * per] = res.results[c]["y"].reshape(per, N, Dm)
    if DBG:
        global DBG_RES
        DBG_RES = res.results
    return out


# version 20
# speedup vs baseline: 1.1347x; 1.0781x over previous
import numpy as np
import ml_dtypes
from contextlib import ExitStack

import concourse.bass as bass
import concourse.mybir as mybir
from concourse.bass_utils import run_bass_kernel_spmd

F32 = mybir.dt.float32
BF16 = mybir.dt.bfloat16
I32 = mybir.dt.int32
AF = mybir.ActivationFunctionType
ALU = mybir.AluOpType
AX = mybir.AxisListType
BFNP = ml_dtypes.bfloat16

D = 1024
TOK = 4096
TILE = 512
NT = TOK // TILE
G = TILE // 128
DC = D // 128
H = 16
DH = 64
FFH = 4096
HC = FFH // 128
NEG = -1.0e30
SEQ_TILES = 2048 // TILE

ENGS = ("sync", "tensor", "vector", "scalar", "gpsimd")


class Prog:

    def __init__(self, cnt):
        self.ops = {e: [] for e in ENGS}
        self.cnt = cnt
        self.last_wait = {e: {} for e in ENGS}
        self.fence_snap = {}

    def emit(self, eng, fn, waits=(), inc=None, n=1):
        own = _OWN.get(eng)
        merged = dict(self.fence_snap)
        if own in ("V", "S"):
            if inc is None:
                inc = own
            merged[own] = self.cnt.get(own, 0)
        for s, v in waits:
            if v > merged.get(s, 0):
                merged[s] = v
        w = []
        for s, v in merged.items():
            if s == "T" and eng == "tensor":
                continue
            if v > self.last_wait[eng].get(s, 0):
                self.last_wait[eng][s] = v
                w.append((s, v))
        self.ops[eng].append((fn, w, inc, n))
        if inc is not None:
            self.cnt[inc] = self.cnt.get(inc, 0) + n
            return self.cnt[inc]
        return None

    def fence(self):
        self.fence_snap = {s: self.cnt.get(s, 0) for s in ("T", "V", "S")}

    def replay(self, block, sems):
        for ename in ENGS:
            ops = self.ops[ename]

            def body(eng, ops=ops):
                for fn, w, inc, n in ops:
                    for s, v in w:
                        eng.wait_ge(sems[s], v)
                    instr = fn(eng)
                    if inc is not None:
                        instr.then_inc(sems[inc], n)
            getattr(block, ename)(body)


_OWN = {"tensor": "T", "vector": "V", "scalar": "S"}


DBG = False


def build_program(dbg=False):
    nc = bass.Bass("TRN2", target_bir_lowering=False, detect_race_conditions=False)

    d_x = nc.dram_tensor("x", [TOK, D], F32, kind="ExternalInput").ap()
    d_gamma = nc.dram_tensor("gamma", [D], F32, kind="ExternalInput").ap()
    d_beta = nc.dram_tensor("beta", [D], F32, kind="ExternalInput").ap()
    d_wqkv = nc.dram_tensor("wqkv", [D, 3 * D], F32, kind="ExternalInput").ap()
    d_wout = nc.dram_tensor("wout", [D, D], F32, kind="ExternalInput").ap()
    d_wfc = nc.dram_tensor("wfc", [FFH, D], F32, kind="ExternalInput").ap()
    d_bfc = nc.dram_tensor("bfc", [FFH], F32, kind="ExternalInput").ap()
    d_afc = nc.dram_tensor("afc", [16, D], F32, kind="ExternalInput").ap()
    d_bfcB = nc.dram_tensor("bfcB", [FFH, 16], F32, kind="ExternalInput").ap()
    d_wpr = nc.dram_tensor("wpr", [D, FFH], F32, kind="ExternalInput").ap()
    d_bpr = nc.dram_tensor("bpr", [D], F32, kind="ExternalInput").ap()
    d_apr = nc.dram_tensor("apr", [16, FFH], F32, kind="ExternalInput").ap()
    d_bprB = nc.dram_tensor("bprB", [D, 16], F32, kind="ExternalInput").ap()
    d_identf = nc.dram_tensor("identf", [128, 128], F32, kind="ExternalInput").ap()
    d_mask = nc.dram_tensor("maskm", [128, 256], BF16, kind="ExternalInput").ap()
    d_onesf = nc.dram_tensor("onesf", [1, 128], F32, kind="ExternalInput").ap()
    d_onesb = nc.dram_tensor("onesb", [1, 128], BF16, kind="ExternalInput").ap()

    d_y = nc.dram_tensor("y", [TOK, D], F32, kind="ExternalOutput").ap()
    d_dbg = {}
    if dbg:
        for name, shape, dt in [("wall", [128, 24576], BF16), ("bw", [128, 16], F32),
                                ("bwv", [1, 1024], F32), ("scl", [128, 8], F32),
                                ("btfc", [16, FFH], BF16), ("a2fc", [16, D], BF16),
                                ("stat", [128, 64], F32), ("xnfm", [128, 4096], BF16),
                                ("qd", [128, 4096], BF16), ("kd", [128, 4096], BF16),
                                ("vd", [128, 4096], BF16), ("ao", [128, 4096], BF16),
                                ("x1", [128, 4096], F32), ("hd", [128, HC * TILE], BF16),
                                ("wfcT0", [128, 1024], BF16), ("wprT0", [128, 4096], BF16)]:
            d_dbg[name] = nc.dram_tensor("dbg_" + name, shape, dt, kind="ExternalOutput").ap()

    d_wfcT = nc.dram_tensor("wfcT_scr", [FFH, D], BF16, kind="Internal").ap()
    d_wprT = nc.dram_tensor("wprT_scr", [D, FFH], BF16, kind="Internal").ap()
    d_woutb = nc.dram_tensor("wout_scr", [D, D], BF16, kind="Internal").ap()
    d_wqs = nc.dram_tensor("wqs_scr", [16 * 128, D], BF16, kind="Internal").ap()
    d_wvs = nc.dram_tensor("wvs_scr", [DC * 128, D], BF16, kind="Internal").ap()

    cnt = {}
    stack = ExitStack()
    e = stack.enter_context

    sems = {}
    for s in ("dc", "da", "db", "dwa", "dwb", "dx", "dy0", "dy1",
              "wo0", "wo1", "wo2", "wf0", "wf1", "wf2",
              "wp0", "wp1", "wp2",
              "wq0", "wq1", "wv0", "wv1", "T", "V", "S", "BAR"):
        sems[s] = e(nc.semaphore(s))

    pb = [e(nc.psum_tensor(f"pb{i}", [128, 512], F32)) for i in range(8)]

    s_identf = e(nc.sbuf_tensor("sbident_f", [128, 128], F32))
    s_mask = e(nc.sbuf_tensor("sb_maskm", [128, 256], BF16))
    s_onesf = e(nc.sbuf_tensor("sb_onesf", [1, 128], F32))
    s_gcol = e(nc.sbuf_tensor("gcol", [128, 8], F32))
    s_g8col = e(nc.sbuf_tensor("g8col", [128, 8], F32))
    s_betab = e(nc.sbuf_tensor("betab", [128, 8], BF16))
    s_bw = e(nc.sbuf_tensor("bw", [128, 16], F32))
    s_bwv = e(nc.sbuf_tensor("bwv", [1, 1024], BF16))
    s_onesb = e(nc.sbuf_tensor("sb_onesb", [1, 128], BF16))
    s_bfc = e(nc.sbuf_tensor("bfccol", [128, 32], F32))
    s_bpr = e(nc.sbuf_tensor("bprcol", [128, 8], F32))
    s_cm05 = e(nc.sbuf_tensor("cm05", [128, 1], F32))
    s_cp05 = e(nc.sbuf_tensor("cp05", [128, 1], F32))
    s_scfc = e(nc.sbuf_tensor("scfc", [128, 1], F32))
    s_rsfc = e(nc.sbuf_tensor("rsfc", [128, 1], F32))
    s_scpr = e(nc.sbuf_tensor("scpr", [128, 1], F32))
    s_rspr = e(nc.sbuf_tensor("rspr", [128, 1], F32))
    s_am = e(nc.sbuf_tensor("amcols", [128, 32], F32))
    s_hsfc = e(nc.sbuf_tensor("hsfc", [128, 1], F32))
    s_hspr = e(nc.sbuf_tensor("hspr", [128, 1], F32))
    s_amc = e(nc.sbuf_tensor("amc", [128, 1], F32))
    s_amr = e(nc.sbuf_tensor("amr", [1, 128], F32))
    s_sc1 = e(nc.sbuf_tensor("sc1", [1, 1], F32))


    s_a2fc = e(nc.sbuf_tensor("a2fc", [16, D], BF16))
    s_btfc = e(nc.sbuf_tensor("btfc", [16, FFH], BF16))
    s_a2pr = e(nc.sbuf_tensor("a2pr", [16, FFH], BF16))
    s_btpr = e(nc.sbuf_tensor("btpr", [16, D], BF16))

    P = Prog(cnt)

    def dma(prog, eng, out_ap, in_ap, sem, slow=False):
        if slow:
            return prog.emit(eng, lambda E, o=out_ap, i=in_ap: E.dma_start(out=o, in_=i, allow_slow_non_contiguous=True), inc=sem, n=16)
        return prog.emit(eng, lambda E, o=out_ap, i=in_ap: E.dma_start(out=o, in_=i), inc=sem, n=16)

    def mm(prog, out_ap, lhsT, rhs, start, stop, tp=None, waits=()):
        def f(E, o=out_ap, l=lhsT, r=rhs):
            return E.matmul(o, l, r, start=start, stop=stop, tile_position=tp,
                            skip_group_check=True)
        return prog.emit("tensor", f, waits=waits, inc="T")

    def trans(prog, out_ap, in_ap, ident, start=True, stop=True, waits=()):
        def f(E, o=out_ap, i=in_ap, idn=ident):
            return E.matmul(o, i, idn, start=start, stop=stop, is_transpose=True,
                            skip_group_check=True)
        return prog.emit("tensor", f, waits=waits, inc="T")

    with ExitStack() as arena:
        a = arena.enter_context
        t_stage = a(nc.sbuf_tensor("stage", [128, 2 * FFH], F32))
        t_q = a(nc.sbuf_tensor("tq", [128, 2048], F32))
        t_k0 = a(nc.sbuf_tensor("tk0", [128, 2048], I32))
        t_d = a(nc.sbuf_tensor("td", [128, 2048], F32))
        t_s1 = a(nc.sbuf_tensor("ts1", [128, 2048], F32))
        t_s2 = a(nc.sbuf_tensor("ts2", [128, 2048], F32))
        t_wqf = a(nc.sbuf_tensor("twqf", [128, 2 * FFH], F32))
        t_panel = a(nc.sbuf_tensor("tpanel", [128, 2 * FFH], BF16))
        t_small = a(nc.sbuf_tensor("tsmall", [128, 16], F32))

        tok_dx = 0
        tok_dx = dma(P, "sync", s_identf[:], d_identf, "dc")
        tok_dx = dma(P, "sync", s_mask[:], d_mask, "dc")
        tok_dx = dma(P, "sync", s_onesf[:], d_onesf, "dc")
        tok_dx = dma(P, "sync", s_onesb[:], d_onesb, "dc")
        tok_dx = dma(P, "sync", s_gcol[:], d_gamma.rearrange("(c p) -> p c", p=128), "dc", slow=True)
        tok_dx = dma(P, "sync", t_small[:, 0:8], d_beta.rearrange("(c p) -> p c", p=128), "dc", slow=True)
        tok_dx = dma(P, "sync", s_bfc[:], d_bfc.rearrange("(c p) -> p c", p=128), "dc", slow=True)
        tok_dx = dma(P, "sync", s_bpr[:], d_bpr.rearrange("(c p) -> p c", p=128), "dc", slow=True)
        tok_consts = tok_dx
        P.emit("vector", lambda E: E.memset(s_cm05[:], -0.5))
        P.emit("vector", lambda E: E.memset(s_cp05[:], 0.5))
        P.emit("vector", lambda E: E.tensor_scalar_mul(s_g8col[:], s_gcol[:], 0.125),
               waits=[("dc", tok_consts)])
        tv = P.emit("vector", lambda E: E.tensor_copy(out=s_betab[:], in_=t_small[:, 0:8]),
                    waits=[("dc", tok_consts)], inc="V")

        t_bwv32 = a(nc.sbuf_tensor("tbwv32", [1, 1024], F32))
        cast_tk2 = [0, 0]
        pstore_tk = [0, 0]
        bw_read = 0
        bwv_read = 0
        for dc in range(DC):
            sem = "da" if dc % 2 == 0 else "db"
            ssem = "dwa" if dc % 2 == 0 else "dwb"
            slot = (dc % 2) * 4096
            tok_l = P.emit("sync",
                           lambda E, dc=dc, slot=slot:
                           E.dma_start(out=t_stage[:, slot:slot + 3072],
                                       in_=d_wqkv[dc * 128:(dc + 1) * 128, :]),
                           waits=[("S", cast_tk2[dc % 2])], inc=sem, n=16)
            cast = P.emit("scalar",
                          lambda E, slot=slot: E.copy(out=t_panel[:, slot:slot + 3072],
                                                      in_=t_stage[:, slot:slot + 3072]),
                          waits=[(sem, tok_l), (ssem, pstore_tk[dc % 2]), ("dc", tok_consts)], inc="S")
            cast_tk2[dc % 2] = cast
            tT = None
            for oc in range(16):
                tT = mm(P, pb[0][:, oc:oc + 1],
                        t_panel[:, slot + oc * 128: slot + (oc + 1) * 128],
                        s_betab[:, dc:dc + 1], start=True, stop=True,
                        waits=[("S", cast), ("V", bw_read)])
            for half in range(2):
                tT = mm(P, pb[1 + half][0:1, 0:512], s_betab[:, dc:dc + 1],
                        t_panel[:, slot + 2048 + half * 512: slot + 2048 + (half + 1) * 512],
                        start=True, stop=True,
                        waits=[("V", bwv_read)])
            tTn = P.emit("tensor", lambda E: E.nop(), inc="T")
            if dc == 0:
                bw_read = P.emit("vector", lambda E: E.tensor_copy(out=s_bw[:], in_=pb[0][:, 0:16]),
                                 waits=[("T", tTn)], inc="V")
                P.emit("vector", lambda E: E.tensor_copy(out=t_bwv32[0:1, 0:512], in_=pb[1][0:1, 0:512]),
                       inc="V")
                bwv_read = P.emit("vector", lambda E: E.tensor_copy(out=t_bwv32[0:1, 512:1024],
                                                                    in_=pb[2][0:1, 0:512]), inc="V")
            else:
                bw_read = P.emit("vector", lambda E: E.tensor_tensor(out=s_bw[:], in0=s_bw[:],
                                                                     in1=pb[0][:, 0:16], op=ALU.add),
                                 waits=[("T", tTn)], inc="V")
                P.emit("vector", lambda E: E.tensor_tensor(out=t_bwv32[0:1, 0:512], in0=t_bwv32[0:1, 0:512],
                                                           in1=pb[1][0:1, 0:512], op=ALU.add), inc="V")
                bwv_read = P.emit("vector", lambda E: E.tensor_tensor(out=t_bwv32[0:1, 512:1024],
                                                                      in0=t_bwv32[0:1, 512:1024],
                                                                      in1=pb[2][0:1, 0:512], op=ALU.add),
                                  inc="V")
            P.emit("scalar",
                   lambda E, dc=dc, slot=slot: E.activation(out=t_panel[:, slot:slot + 1024],
                                                            in_=t_panel[:, slot:slot + 1024],
                                                            func=AF.Copy, scale=s_g8col[:, dc:dc + 1]),
                   waits=[("T", tTn)])
            fold = P.emit("scalar",
                          lambda E, dc=dc, slot=slot: E.activation(out=t_panel[:, slot + 1024:slot + 3072],
                                                                   in_=t_panel[:, slot + 1024:slot + 3072],
                                                                   func=AF.Copy, scale=s_gcol[:, dc:dc + 1]),
                          inc="S")
            P.emit("gpsimd",
                   lambda E, dc=dc, slot=slot: E.dma_start(
                       out=bass.AP(d_wqs.tensor, dc * 128, [[1024, 128], [128 * 1024, 16], [1, 128]]),
                       in_=bass.AP(t_panel, slot, [[8192, 128], [128, 16], [1, 128]])),
                   waits=[("S", fold)], inc=ssem, n=16)
            pstore_tk[dc % 2] = P.emit("gpsimd",
                                       lambda E, dc=dc, slot=slot:
                                       E.dma_start(out=d_wvs[dc * 128:(dc + 1) * 128, :],
                                                   in_=t_panel[:, slot + 2048:slot + 3072]),
                                       inc=ssem, n=16)
        tok_bw = P.emit("vector", lambda E: E.tensor_scalar_mul(s_bw[:, 0:8], s_bw[:, 0:8], 0.125),
                        inc="V")
        tok_bwv = P.emit("vector", lambda E: E.tensor_copy(out=s_bwv[:], in_=t_bwv32[:]), inc="V")

        P.fence()
        st_toks = list(pstore_tk)
        cast_tk = [0, 0]
        for ic in range(DC):
            sem = "da" if ic % 2 == 0 else "db"
            ssem = "dwa" if ic % 2 == 0 else "dwb"
            tok_l = P.emit("sync",
                           lambda E, ic=ic: E.dma_start(out=t_stage[:, (ic % 2) * 4096:(ic % 2) * 4096 + 1024],
                                                        in_=d_wout[ic * 128:(ic + 1) * 128, :]),
                           waits=[("V", cast_tk[ic % 2])], inc=sem, n=16)
            tok_c = P.emit("vector",
                           lambda E, ic=ic: E.tensor_copy(out=t_panel[:, (ic % 2) * 1024:(ic % 2) * 1024 + 1024],
                                                          in_=t_stage[:, (ic % 2) * 4096:(ic % 2) * 4096 + 1024]),
                           waits=[(sem, tok_l), (ssem, st_toks[ic % 2])], inc="V")
            cast_tk[ic % 2] = tok_c
            st_toks[ic % 2] = P.emit(
                "gpsimd",
                lambda E, ic=ic: E.dma_start(out=d_woutb[ic * 128:(ic + 1) * 128, :],
                                             in_=t_panel[:, (ic % 2) * 1024:(ic % 2) * 1024 + 1024]),
                waits=[("V", tok_c)], inc=ssem, n=16)
        P.fence()

        tok_a = dma(P, "sync", t_stage[:16, 0:1024], d_afc, "da")
        tva = P.emit("vector", lambda E: E.tensor_scalar(out=s_a2fc[:], in0=t_stage[:16, 0:1024],
                                                         scalar1=2.0, scalar2=None, op0=ALU.mult),
                     waits=[("da", tok_a)], inc="V")
        tok_a = P.emit("sync", lambda E: E.dma_start(out=t_stage[:16, 0:4096], in_=d_apr),
                       waits=[("V", tva)], inc="da", n=16)
        tva = P.emit("vector", lambda E: E.tensor_scalar(out=s_a2pr[:], in0=t_stage[:16, 0:4096],
                                                         scalar1=2.0, scalar2=None, op0=ALU.mult),
                     waits=[("da", tok_a)], inc="V")
        P.fence()
        btr = [0, 0]
        bev = [0, 0]
        for c in range(HC):
            sem = "da" if c % 2 == 0 else "db"
            tok_l = P.emit("sync",
                           lambda E, c=c: E.dma_start(out=t_stage[:, (c % 2) * 4096:(c % 2) * 4096 + 16],
                                                      in_=d_bfcB[c * 128:(c + 1) * 128, :]),
                           waits=[("T", btr[c % 2])], inc=sem, n=16)
            tt = trans(P, pb[2 + c % 2][0:16, 0:128],
                       t_stage[:, (c % 2) * 4096:(c % 2) * 4096 + 16], s_identf[:],
                       waits=[(sem, tok_l), ("S", bev[c % 2])])
            btr[c % 2] = tt
            bev[c % 2] = P.emit("scalar",
                                lambda E, c=c: E.copy(out=s_btfc[:16, c * 128:(c + 1) * 128],
                                                      in_=pb[2 + c % 2][0:16, 0:128]),
                                waits=[("T", tt)], inc="S")
        P.fence()
        btr = [0, 0]
        bev = [0, 0]
        for c in range(DC):
            sem = "da" if c % 2 == 0 else "db"
            tok_l = P.emit("sync",
                           lambda E, c=c: E.dma_start(out=t_stage[:, (c % 2) * 4096:(c % 2) * 4096 + 16],
                                                      in_=d_bprB[c * 128:(c + 1) * 128, :]),
                           waits=[("T", btr[c % 2])], inc=sem, n=16)
            tt = trans(P, pb[2 + c % 2][0:16, 0:128],
                       t_stage[:, (c % 2) * 4096:(c % 2) * 4096 + 16], s_identf[:],
                       waits=[(sem, tok_l), ("S", bev[c % 2])])
            btr[c % 2] = tt
            bev[c % 2] = P.emit("scalar",
                                lambda E, c=c: E.copy(out=s_btpr[:16, c * 128:(c + 1) * 128],
                                                      in_=pb[2 + c % 2][0:16, 0:128]),
                                waits=[("T", tt)], inc="S")
        P.fence()

        def quant_weight(d_w, nchunk, free, s_scale, s_rscale, s_hscale, am_ncols, d_scrT, a2, bt,
                         panel_cols, tpos_cols):
            P.fence()
            red_tk = [0, 0]
            for c in range(nchunk):
                sl = (c % 2) * 4096
                sem = "da" if c % 2 == 0 else "db"
                tok_l = P.emit("sync",
                               lambda E, c=c, sl=sl: E.dma_start(out=t_stage[:, sl:sl + free],
                                                                 in_=d_w[c * 128:(c + 1) * 128, :]),
                               waits=[("V", red_tk[c % 2])], inc=sem, n=16)
                red_tk[c % 2] = P.emit("vector",
                                       lambda E, c=c, sl=sl: E.reduce_max(out=s_am[:, c:c + 1],
                                                                          in_=t_stage[:, sl:sl + free],
                                                                          axis=AX.X, apply_absolute_value=True),
                                       waits=[(sem, tok_l)], inc="V")
            P.fence()
            tok = P.emit("vector", lambda E: E.reduce_max(out=s_amc[:], in_=s_am[:, 0:am_ncols], axis=AX.X),
                         inc="V")
            tt = trans(P, pb[3][0:1, 0:128], s_amc[:], s_identf[:], waits=[("V", tok)])
            tok = P.emit("vector", lambda E: E.tensor_copy(out=s_amr[:], in_=pb[3][0:1, 0:128]),
                         waits=[("T", tt)], inc="V")
            tok = P.emit("vector", lambda E: E.reduce_max(out=s_sc1[:], in_=s_amr[:], axis=AX.X), inc="V")
            tok = P.emit("vector", lambda E: E.tensor_scalar_mul(s_sc1[:], s_sc1[:], float(np.float32(1.0) / np.float32(127.0))),
                         inc="V")
            tt = mm(P, pb[3][:, 0:1], s_onesf[:], s_sc1[:], start=True, stop=True,
                    waits=[("V", tok)])
            tok = P.emit("vector", lambda E: E.tensor_copy(out=s_scale[:], in_=pb[3][:, 0:1]),
                         waits=[("T", tt)], inc="V")
            tok = P.emit("vector", lambda E: E.reciprocal(out=s_rscale[:], in_=s_scale[:]), inc="V")
            tok = P.emit("vector", lambda E: E.tensor_scalar_mul(s_hscale[:], s_scale[:], 0.5), inc="V")
            P.fence()
            nsl = free // 1024
            tok_store = 0
            ev_bank = {}
            trans_last = 0
            tq_tk = [0, 0]
            gsl = [0]
            stt_tk = [0, 0]
            pend_tw = [None]
            twc = [0] * nchunk
            trans_c = [0] * nchunk
            ev_bank = {}
            store_tk = [0, 0]

            def flush_tw(prog):
                if pend_tw[0] is not None:
                    pf, pc, ptk, ptl = pend_tw[0]
                    tw = prog.emit("scalar", pf, waits=[("V", ptk), ("T", ptl)], inc="S")
                    twc[pc] = tw
                    pend_tw[0] = None

            def quant_slices(c):
                sl = (c % 2) * 4096
                wsl = (c % 2) * 4096
                sem = "da" if c % 2 == 0 else "db"
                tok_l = P.emit("sync",
                               lambda E, c=c, sl=sl: E.dma_start(out=t_stage[:, sl:sl + free],
                                                                 in_=d_w[c * 128:(c + 1) * 128, :]),
                               waits=[("S", tq_tk[c % 2])], inc=sem, n=16)
                tq = 0
                for s in range(nsl):
                    slot = gsl[0] % 2
                    q = slot * 1024
                    gsl[0] += 1
                    src = t_stage[:, sl + s * 1024: sl + (s + 1) * 1024]
                    tq = P.emit("scalar",
                                lambda E, src=src, q=q: E.activation(out=t_q[:, q:q + 1024], in_=src,
                                                                     func=AF.Copy, scale=s_rscale[:]),
                                waits=[(sem, tok_l), ("V", stt_tk[slot])], inc="S")
                    flush_tw(P)
                    tk = P.emit("vector",
                                lambda E, q=q: E.tensor_copy(out=t_k0[:, q:q + 1024], in_=t_q[:, q:q + 1024]),
                                waits=[("S", tq)], inc="V")
                    tk = P.emit("vector",
                                lambda E, q=q: E.tensor_tensor(out=t_d[:, q:q + 1024], in0=t_q[:, q:q + 1024],
                                                               in1=t_k0[:, q:q + 1024], op=ALU.subtract),
                                inc="V")
                    t1 = P.emit("scalar",
                                lambda E, q=q: E.activation(out=t_s1[:, q:q + 1024], in_=t_d[:, q:q + 1024],
                                                            func=AF.Sign, bias=s_cm05[:]),
                                waits=[("V", tk)], inc="S")
                    t1 = P.emit("scalar",
                                lambda E, q=q: E.activation(out=t_s2[:, q:q + 1024], in_=t_d[:, q:q + 1024],
                                                            func=AF.Sign, bias=s_cp05[:]), inc="S")
                    tk = P.emit("vector",
                                lambda E, q=q: E.scalar_tensor_tensor(out=t_s1[:, q:q + 1024],
                                                                      in0=t_k0[:, q:q + 1024], scalar=2.0,
                                                                      in1=t_s1[:, q:q + 1024],
                                                                      op0=ALU.mult, op1=ALU.add),
                                waits=[("S", t1)], inc="V")
                    tk = P.emit("vector",
                                lambda E, q=q: E.tensor_tensor(out=t_d[:, q:q + 1024], in0=t_s1[:, q:q + 1024],
                                                               in1=t_s2[:, q:q + 1024], op=ALU.add),
                                inc="V")
                    stt_tk[slot] = tk
                    def mk_tw(s=s, q=q, wsl=wsl):
                        return lambda E: E.activation(out=t_wqf[:, wsl + s * 1024: wsl + (s + 1) * 1024],
                                                      in_=t_d[:, q:q + 1024],
                                                      func=AF.Copy, scale=s_hscale[:])
                    pend_tw[0] = (mk_tw(), c, tk, trans_c[c - 2] if c >= 2 else 0)
                tq_tk[c % 2] = tq

            def trans_store(c):
                wsl = (c % 2) * 4096
                psl = (c % 2) * free
                nblk = free // 128
                ev_tok = None
                tt = None
                for b in range(nblk):
                    bi = 4 + ((b // 4) % 2)
                    bank = pb[bi]
                    tt = trans(P, bank[:, (b % 4) * 128:(b % 4) * 128 + 128],
                               t_wqf[:, wsl + b * 128: wsl + (b + 1) * 128], s_identf[:],
                               start=True, stop=False,
                               waits=[("S", twc[c]), ("S", ev_bank.get(bi, 0))])
                    tt = mm(P, bank[:, (b % 4) * 128:(b % 4) * 128 + 128],
                            a2[:16, b * 128:(b + 1) * 128],
                            bt[:16, c * 128:(c + 1) * 128],
                            start=False, stop=True)
                    if b % 4 == 3:
                        ev_tok = P.emit("scalar",
                                        lambda E, b=b, bank=bank, psl=psl:
                                        E.copy(out=t_panel[:, psl + (b - 3) * 128: psl + (b + 1) * 128],
                                               in_=bank[:]),
                                        waits=[("T", tt), ("dwa" if c % 2 == 0 else "dwb", store_tk[c % 2])],
                                        inc="S")
                        ev_bank[bi] = ev_tok
                trans_c[c] = tt
                store_tk[c % 2] = P.emit("gpsimd",
                                         lambda E, c=c, psl=psl:
                                         E.dma_start(out=d_scrT[c * 128:(c + 1) * 128, :],
                                                     in_=t_panel[:, psl:psl + free]),
                                         waits=[("S", ev_tok)],
                                         inc="dwa" if c % 2 == 0 else "dwb", n=16)

            for c in range(nchunk):
                quant_slices(c)
                flush_tw(P)
                if c >= 1:
                    trans_store(c - 1)
            trans_store(nchunk - 1)
            P.fence()

        quant_weight(d_wfc, HC, 1024, s_scfc, s_rsfc, s_hsfc, HC, d_wfcT, s_a2fc, s_btfc, 1024, None)
        quant_weight(d_wpr, DC, 4096, s_scpr, s_rspr, s_hspr, DC, d_wprT, s_a2pr, s_btpr, 4096, None)

        if dbg:
            P.fence()
            P.emit("sync", lambda E: E.nop(),
                   waits=[(k, cnt.get(k, 0)) for k in ("T", "V", "S")])
            pass
            dma(P, "sync", d_dbg["bw"], s_bw[:], "dc")
            dma(P, "sync", d_dbg["bwv"], s_bwv[:], "dc")
            P.emit("vector", lambda E: E.tensor_copy(out=s_am[:, 0:1], in_=s_scfc[:]))
            P.emit("vector", lambda E: E.tensor_copy(out=s_am[:, 1:2], in_=s_rsfc[:]))
            P.emit("vector", lambda E: E.tensor_copy(out=s_am[:, 2:3], in_=s_scpr[:]))
            tvd = P.emit("vector", lambda E: E.tensor_copy(out=s_am[:, 3:4], in_=s_rspr[:]), inc="V")
            dma(P, "sync", d_dbg["scl"], s_am[:, 0:8], "dc")
            dma(P, "sync", d_dbg["btfc"], s_btfc[:], "dc")
            dma(P, "sync", d_dbg["a2fc"], s_a2fc[:], "dc")
            dma(P, "sync", d_dbg["wfcT0"], d_wfcT[0:128, :], "dc")
            dma(P, "sync", d_dbg["wprT0"], d_wprT[0:128, :], "dc")
        for eng in ENGS:
            P.emit(eng, lambda E: E.nop(), inc="BAR")

        with nc.Block("setup") as b1:
            P.replay(b1, sems)

    s_x = e(nc.sbuf_tensor("xsb", [128, 2 * G * D], F32))
    s_xnb = e(nc.sbuf_tensor("xnb", [128, D], F32))
    s_xn1 = e(nc.sbuf_tensor("xn1", [128, DC * TILE], BF16))
    s_xn2 = e(nc.sbuf_tensor("xn2", [128, DC * TILE], BF16))
    s_q = e(nc.sbuf_tensor("qsb", [128, 8 * TILE], BF16))
    s_k = e(nc.sbuf_tensor("ksb", [128, 8 * TILE], BF16))
    s_v = e(nc.sbuf_tensor("vsb", [128, G * D], BF16))
    s_kprev = e(nc.sbuf_tensor("kprev", [128, 8 * 128], BF16))
    s_vprev = e(nc.sbuf_tensor("vprev", [128, D], BF16))
    s_ao = e(nc.sbuf_tensor("aosb", [128, 8 * TILE], BF16))
    s_h = e(nc.sbuf_tensor("hsb", [128, HC * TILE], BF16))
    s_sm = e(nc.sbuf_tensor("smbuf", [128, 7 * 256], F32))
    s_pT = e(nc.sbuf_tensor("pT", [128, 3 * 256], BF16))
    s_stat = e(nc.sbuf_tensor("stat", [128, 80], F32))
    s_ofm = e(nc.sbuf_tensor("ofm", [128, 512], F32))
    s_wfcb = e(nc.sbuf_tensor("wfcb", [128, 3 * 1024], BF16))
    s_wqb = e(nc.sbuf_tensor("wqb", [128, 2 * 2048], BF16))
    s_wvb = e(nc.sbuf_tensor("wvb", [128, 2 * 2048], BF16))
    s_wprb = e(nc.sbuf_tensor("wprb", [128, 3 * 4096], BF16))
    s_woutb = e(nc.sbuf_tensor("woutb", [128, 3 * 1024], BF16))

    NEGMAX, ROWSUM, RECIP = 0, 52, 64
    SUM, SSQ, MU, MSQ, VARP, R1, V2P, ASC, NMA, T0 = 8, 12, 16, 20, 24, 28, 32, 36, 40, 44

    M = Prog(cnt)
    M.fence_snap = {}
    for eng in ENGS:
        M.emit(eng, lambda E: E.nop(), waits=[("BAR", cnt["BAR"])])

    bank_free = [[] for _ in range(8)]

    def bmm(out_bank, cols, lhsT, rhs, start, stop, tp=None, waits=(), prow=None):
        w = list(waits)
        if start:
            w += bank_free[out_bank]
        ap = pb[out_bank][:, cols[0]:cols[1]] if prow is None else pb[out_bank][prow[0]:prow[1], cols[0]:cols[1]]
        tok = mm(M, ap, lhsT, rhs, start=start, stop=stop, tp=tp, waits=w)
        if start:
            bank_free[out_bank] = []
        return tok

    def btrans(out_bank, cols, in_ap, start=True, stop=True, waits=()):
        w = list(waits) + (bank_free[out_bank] if start else [])
        tok = trans(M, pb[out_bank][:, cols[0]:cols[1]], in_ap, s_identf[:],
                    start=start, stop=stop, waits=w)
        if start:
            bank_free[out_bank] = []
        return tok

    def bread(out_bank, sem, tok):
        bank_free[out_bank] = [(sem, tok)]

    wout_cons = [0, 0, 0]
    wfc_cons = [0, 0, 0]
    wpr_cons = [0, 0, 0]
    wq_cons = [0, 0]
    wv_cons = [0, 0]
    xnb_cons = [0, 0]
    xn1_cons = 0
    xn2_cons = 0
    qk_write_guard = [0, 0]
    v_write_guard = 0
    ao_guard = 0
    h_guard = 0
    ofm_trans_last = [0]
    ofm_cons = [0, 0]
    prev_expC = [0] * 7
    prev_trE = [0] * 7
    prev_H = [("V", 0), ("S", 0)]
    store_tok = [0, 0]
    kp_tok = 0
    vp_tok = 0
    st = {}

    def emit_stats(wait_sem, wait_tok):
        tv = ts = None
        for g in range(G):
            xb = st["xbase"]
            tv = M.emit("vector",
                        lambda E, g=g, xb=xb: E.tensor_scalar(out=s_xn1[:, 0:1024],
                                                              in0=s_x[:, xb + g * D: xb + (g + 1) * D],
                                                              scalar1=1.0, scalar2=None,
                                                              op0=ALU.mult, op1=ALU.add,
                                                              accum_out=s_stat[:, SUM + g:SUM + g + 1]),
                        waits=[(wait_sem, wait_tok), ("T", xn1_cons)], inc="V")
            ts = M.emit("scalar",
                        lambda E, g=g, xb=xb: E.activation(out=s_xn1[:, 1024:2048],
                                                           in_=s_x[:, xb + g * D: xb + (g + 1) * D],
                                                           func=AF.Square,
                                                           accum_out=s_stat[:, SSQ + g:SSQ + g + 1]),
                        waits=[(wait_sem, wait_tok), ("T", xn1_cons)], inc="S")
        c = 1.0 / 1024.0
        tv = M.emit("vector", lambda E: E.tensor_scalar_mul(s_stat[:, MU:MU + 4], s_stat[:, SUM:SUM + 4], c),
                    waits=[("V", tv)], inc="V")
        tv = M.emit("vector", lambda E: E.tensor_scalar_mul(s_stat[:, MSQ:MSQ + 4], s_stat[:, SSQ:SSQ + 4], c),
                    waits=[("S", ts)], inc="V")
        tv = M.emit("vector", lambda E: E.tensor_tensor(out=s_stat[:, T0:T0 + 4], in0=s_stat[:, MU:MU + 4],
                                                        in1=s_stat[:, MU:MU + 4], op=ALU.mult), inc="V")
        tv = M.emit("vector", lambda E: E.tensor_tensor(out=s_stat[:, VARP:VARP + 4], in0=s_stat[:, MSQ:MSQ + 4],
                                                        in1=s_stat[:, T0:T0 + 4], op=ALU.subtract), inc="V")
        tv = M.emit("vector", lambda E: E.tensor_scalar_add(s_stat[:, VARP:VARP + 4], s_stat[:, VARP:VARP + 4],
                                                            1e-5), inc="V")
        ts2 = M.emit("scalar", lambda E: E.activation(out=s_stat[:, T0:T0 + 4], in_=s_stat[:, VARP:VARP + 4],
                                                      func=AF.Ln),
                     waits=[("V", tv)], inc="S")
        ts2 = M.emit("scalar", lambda E: E.activation(out=s_stat[:, R1:R1 + 4], in_=s_stat[:, T0:T0 + 4],
                                                      func=AF.Exp, scale=s_cm05[:]), inc="S")
        return tv, ts2

    def emit_norm_transpose(dst, a_col, nma_col, wait_s_tok):
        ev_last = None
        xb = st["xbase"]
        for g in range(G):
            tok_x = M.emit("vector",
                           lambda E, g=g, xb=xb, a_col=a_col, nma_col=nma_col:
                           E.tensor_scalar(out=s_xnb[:, 0:D],
                                           in0=s_x[:, xb + g * D: xb + (g + 1) * D],
                                           scalar1=s_stat[:, a_col + g:a_col + g + 1],
                                           scalar2=s_stat[:, nma_col + g:nma_col + g + 1],
                                           op0=ALU.mult, op1=ALU.add),
                           waits=[("T", xnb_cons[0]), ("S", wait_s_tok)], inc="V")
            tt = None
            for half in range(2):
                bank = half
                for i4 in range(4):
                    dc = half * 4 + i4
                    tt = btrans(bank, (i4 * 128, (i4 + 1) * 128),
                                s_xnb[:, dc * 128:(dc + 1) * 128],
                                start=True, stop=True,
                                waits=[("V", tok_x)])
                dst_ap = bass.AP(dst, half * 4 * TILE + g * 128,
                                 [[DC * TILE, 128], [TILE, 4], [1, 128]])
                tok = M.emit("scalar",
                             lambda E, bank=bank, dst_ap=dst_ap:
                             E.copy(out=dst_ap, in_=pb[bank][:].rearrange("p (a b) -> p a b", b=128)),
                             waits=[("T", tt)], inc="S")
                bread(bank, "S", tok)
                ev_last = tok
            xnb_cons[0] = tt
        return ev_last

    def emit_qkv(t, xn1_ev):
        qk_ev = []
        for op in range(8):
            sl = op % 2
            tok_w = M.emit("gpsimd",
                           lambda E, op=op, sl=sl: E.dma_start(
                               out=s_wqb[:, sl * 2048:(sl + 1) * 2048].rearrange("p (a d) -> p a d", d=1024),
                               in_=d_wqs[op * 256:(op + 1) * 256, :].rearrange("(a p) d -> p a d", p=128)),
                           waits=[("T", wq_cons[sl])], inc="wq%d" % sl, n=16)
            for j in range(2):
                oc = 2 * op + j
                bank = 2 + (oc % 2)
                tt = None
                for dc in range(DC):
                    tt = bmm(bank, (0, 512),
                             s_wqb[:, sl * 2048 + j * 1024 + dc * 128: sl * 2048 + j * 1024 + (dc + 1) * 128],
                             s_xn1[:, dc * TILE:(dc + 1) * TILE],
                             start=(dc == 0), stop=(dc == DC - 1),
                             waits=[("wq%d" % sl, tok_w), ("S", xn1_ev),
                                    ("T", qk_write_guard[0]), ("V", kp_tok)])
                dst = s_q if oc < 8 else s_k
                fc = oc % 8
                tok = M.emit("scalar",
                             lambda E, dst=dst, fc=fc, oc=oc, bank=bank:
                             E.activation(out=dst[:, fc * TILE:(fc + 1) * TILE], in_=pb[bank][:],
                                          func=AF.Identity, bias=s_bw[:, oc:oc + 1]),
                             waits=[("T", tt)], inc="S")
                bread(bank, "S", tok)
                qk_ev.append(tok)
            wq_cons[sl] = tt
        v_ev = [0] * G
        ttl = None
        for dp in range(4):
            sl = dp % 2
            tok_w = M.emit("gpsimd",
                           lambda E, dp=dp, sl=sl: E.dma_start(
                               out=s_wvb[:, sl * 2048:(sl + 1) * 2048].rearrange("p (a d) -> p a d", d=1024),
                               in_=d_wvs[dp * 256:(dp + 1) * 256, :].rearrange("(a p) d -> p a d", p=128)),
                           waits=[("T", wv_cons[sl])], inc="wv%d" % sl, n=16)
            for j in range(2):
                dc = 2 * dp + j
                for g in range(G):
                    for half in range(2):
                        bi = g * 2 + half
                        ttl = bmm(bi, (0, 512),
                                  s_xn1[:, dc * TILE + g * 128: dc * TILE + (g + 1) * 128],
                                  s_wvb[:, sl * 2048 + j * 1024 + half * 512: sl * 2048 + j * 1024 + (half + 1) * 512],
                                  start=(dc == 0), stop=False,
                                  waits=[("wv%d" % sl, tok_w), ("S", xn1_ev),
                                         ("T", v_write_guard), ("V", vp_tok)])
            wv_cons[sl] = ttl
        for g in range(G):
            for half in range(2):
                bi = g * 2 + half
                tt = bmm(bi, (0, 512), s_onesb[:], s_bwv[0:1, half * 512:(half + 1) * 512],
                         start=False, stop=True)
                tok = M.emit("vector",
                             lambda E, g=g, half=half, bi=bi:
                             E.tensor_copy(out=s_v[:, g * D + half * 512: g * D + (half + 1) * 512],
                                           in_=pb[bi][:]),
                             waits=[("T", tt)], inc="V")
                bread(bi, "V", tok)
                v_ev[g] = tok
        return qk_ev, v_ev

    def make_attention_steps(t, qk_ev, v_ev):
        seq_start = (t % SEQ_TILES == 0)
        NI = G * H
        ctx = []
        for w in range(G):
            fw = (seq_start and w == 0)
            for h in range(H):
                ctx.append((w, h, fw))
        tokA = [0] * NI; tokB = [0] * NI; tokC = [0] * NI; tokD = [0] * NI
        tokE = [0] * NI; tokF = [0] * NI; tokG = [0] * NI; tokH = [0] * NI
        engH = [0] * NI
        res = {"tokG": tokG, "tokH": tokH, "engH": engH, "tokA": tokA, "tokB": tokB,
               "tokC": tokC, "tokE": tokE}

        def stA(i):
            w, h, fw = ctx[i]
            po = 64 * (h % 2); fc = h // 2
            qs = s_q[po:po + 64, fc * TILE + w * 128: fc * TILE + (w + 1) * 128]
            bank = i % 2
            wts = [("S", qk_ev[8 + fc]), ("V", tokB[i - 2] if i >= 2 else 0)]
            if fw:
                tokA[i] = bmm(bank, (0, 128), qs, s_k[po:po + 64, fc * TILE: fc * TILE + 128],
                              start=True, stop=True, tp=(po, 0), waits=wts)
            elif w == 0:
                bmm(bank, (0, 128), qs, s_kprev[po:po + 64, fc * 128:(fc + 1) * 128],
                    start=True, stop=True, tp=(po, 0), waits=wts)
                tokA[i] = bmm(bank, (128, 256), qs, s_k[po:po + 64, fc * TILE: fc * TILE + 128],
                              start=True, stop=True, tp=(po, 0))
            else:
                tokA[i] = bmm(bank, (0, 256), qs,
                              s_k[po:po + 64, fc * TILE + (w - 1) * 128: fc * TILE + (w + 1) * 128],
                              start=True, stop=True, tp=(po, 0), waits=wts)

        def stB(i):
            w, h, fw = ctx[i]
            nk = 128 if fw else 256
            wh = i % 7
            bank = i % 2
            mof = 128 if fw else 0
            tok = M.emit("vector",
                         lambda E, wh=wh, nk=nk, mof=mof, bank=bank:
                         E.tensor_tensor(out=s_sm[:, wh * 256: wh * 256 + nk], in0=pb[bank][:, 0:nk],
                                         in1=s_mask[:, mof:mof + nk], op=ALU.add),
                         waits=[("T", tokA[i]),
                                ("S", tokC[i - 7] if i >= 7 else prev_expC[i]),
                                ("T", tokE[i - 7] if i >= 7 else prev_trE[i])], inc="V")
            bread(bank, "V", tok)
            tokB[i] = M.emit("vector",
                             lambda E, wh=wh, nk=nk: E.reduce_max(out=s_stat[:, NEGMAX + wh:NEGMAX + wh + 1],
                                                                  in_=s_sm[:, wh * 256: wh * 256 + nk],
                                                                  axis=AX.X, negate=True),
                             inc="V")

        def stC(i):
            w, h, fw = ctx[i]
            nk = 128 if fw else 256
            wh = i % 7
            tokC[i] = M.emit("scalar",
                             lambda E, wh=wh, nk=nk: E.activation(out=s_sm[:, wh * 256: wh * 256 + nk],
                                                                  in_=s_sm[:, wh * 256: wh * 256 + nk],
                                                                  func=AF.Exp,
                                                                  bias=s_stat[:, NEGMAX + wh:NEGMAX + wh + 1],
                                                                  accum_out=s_stat[:, ROWSUM + wh:ROWSUM + wh + 1]),
                             waits=[("V", tokB[i]),
                                    ("T", tokE[i - 7] if i >= 7 else prev_trE[i])], inc="S")

        def stD(i):
            w, h, fw = ctx[i]
            nk = 128 if fw else 256
            wh = i % 7
            M.emit("vector",
                   lambda E, wh=wh: E.reciprocal(out=s_stat[:, RECIP + wh:RECIP + wh + 1],
                                                 in_=s_stat[:, ROWSUM + wh:ROWSUM + wh + 1]),
                   waits=[("S", tokC[i])], inc="V")
            tokD[i] = M.emit("vector",
                             lambda E, wh=wh, nk=nk: E.tensor_scalar_mul(s_sm[:, wh * 256: wh * 256 + nk],
                                                                         s_sm[:, wh * 256: wh * 256 + nk],
                                                                         s_stat[:, RECIP + wh:RECIP + wh + 1]),
                             inc="V")

        def stE(i):
            w, h, fw = ctx[i]
            nk = 128 if fw else 256
            wh = i % 7
            tt2 = None
            tb = 2 + i % 2
            for kc in range(nk // 128):
                tt2 = btrans(tb, (kc * 128, (kc + 1) * 128),
                             s_sm[:, wh * 256 + kc * 128: wh * 256 + (kc + 1) * 128],
                             start=True, stop=True,
                             waits=[("V", tokD[i])])
            tokE[i] = tt2

        def stF(i):
            w, h, fw = ctx[i]
            nk = 128 if fw else 256
            wh = i % 7
            pslot = i % 3
            tb = 2 + i % 2
            tokF[i] = M.emit("scalar",
                             lambda E, pslot=pslot, nk=nk, tb=tb:
                             E.copy(out=s_pT[:, pslot * 256: pslot * 256 + nk], in_=pb[tb][:, 0:nk]),
                             waits=[("T", tokE[i])], inc="S")
            bread(tb, "S", tokF[i])

        def stG(i):
            w, h, fw = ctx[i]
            wh = i % 3
            po = 64 * (h % 2)
            gb = 4 + i % 2
            pvw = [("S", tokF[i]), ("V", v_ev[w])]
            if fw:
                tokG[i] = bmm(gb, (0, 128), s_v[:, w * D + h * DH: w * D + (h + 1) * DH],
                              s_pT[:, wh * 256: wh * 256 + 128],
                              start=True, stop=True, tp=(0, po), waits=pvw, prow=(po, po + 64))
            else:
                vprev_ap = (s_vprev[:, h * DH:(h + 1) * DH] if w == 0
                            else s_v[:, (w - 1) * D + h * DH: (w - 1) * D + h * DH + DH])
                bmm(gb, (0, 128), vprev_ap, s_pT[:, wh * 256: wh * 256 + 128],
                    start=True, stop=False, tp=(0, po), waits=pvw, prow=(po, po + 64))
                tokG[i] = bmm(gb, (0, 128), s_v[:, w * D + h * DH: w * D + (h + 1) * DH],
                              s_pT[:, wh * 256 + 128: wh * 256 + 256],
                              start=False, stop=True, tp=(0, po), prow=(po, po + 64))

        def stH(i):
            w, h, fw = ctx[i]
            po = 64 * (h % 2); fc = h // 2
            gb = 4 + i % 2
            if i % 2 == 0:
                tokH[i] = M.emit("vector",
                                 lambda E, po=po, fc=fc, w=w, gb=gb:
                                 E.tensor_copy(out=s_ao[po:po + 64, fc * TILE + w * 128: fc * TILE + (w + 1) * 128],
                                               in_=pb[gb][po:po + 64, 0:128]),
                                 waits=[("T", tokG[i]), ("T", ao_guard)], inc="V")
                engH[i] = 0
                bread(gb, "V", tokH[i])
            else:
                tokH[i] = M.emit("scalar",
                                 lambda E, po=po, fc=fc, w=w, gb=gb:
                                 E.copy(out=s_ao[po:po + 64, fc * TILE + w * 128: fc * TILE + (w + 1) * 128],
                                        in_=pb[gb][po:po + 64, 0:128]),
                                 waits=[("T", tokG[i]), ("T", ao_guard)], inc="S")
                engH[i] = 1
                bread(gb, "S", tokH[i])

        stages = (stA, stB, stC, stD, stE, stF, stG, stH)
        offs = (0, 2, 4, 5, 7, 8, 10, 11)
        steps = []
        for j in range(NI + offs[-1]):
            def step(j=j):
                for k in range(len(stages) - 1, -1, -1):
                    i = j - offs[k]
                    if 0 <= i < NI:
                        stages[k](i)
            steps.append(step)
        return steps, res

    def emit_ffn_groups(tp):
        groups = []
        xb = (tp % 2) * G * D
        h_toks = st.setdefault("h_toks", {})
        h_toks.clear()

        def mk_fc(c):
            def f():
                sl = c % 3
                tok_w = M.emit("gpsimd",
                               lambda E, c=c, sl=sl: E.dma_start(out=s_wfcb[:, sl * 1024:(sl + 1) * 1024],
                                                                 in_=d_wfcT[c * 128:(c + 1) * 128, :]),
                               waits=[("T", wfc_cons[sl])], inc="wf%d" % sl, n=16)
                bank = 6 + (c % 2)
                tt = None
                for dc in range(DC):
                    tt = bmm(bank, (0, 512),
                             s_wfcb[:, sl * 1024 + dc * 128: sl * 1024 + (dc + 1) * 128],
                             s_xn2[:, dc * TILE:(dc + 1) * TILE],
                             start=(dc == 0), stop=(dc == DC - 1),
                             waits=[("wf%d" % sl, tok_w), ("S", st["xn2_ev"]),
                                    ("T", h_guard_g[0])])
                wfc_cons[sl] = tt
                tok = M.emit("scalar",
                             lambda E, c=c, bank=bank: E.activation(out=s_h[:, c * TILE:(c + 1) * TILE],
                                                                    in_=pb[bank][:], func=AF.Identity,
                                                                    bias=s_bfc[:, c:c + 1]),
                             waits=[("T", tt)], inc="S")
                bread(bank, "S", tok)
                h_toks[c] = tok
            return f

        def mk_gelu_batch(c0, c1):
            def f():
                tok = None
                for c in range(c0, c1):
                    tok = M.emit("scalar",
                                 lambda E, c=c: E.activation(out=s_h[:, c * TILE:(c + 1) * TILE],
                                                             in_=s_h[:, c * TILE:(c + 1) * TILE],
                                                             func=AF.Gelu),
                                 waits=[("S", h_toks[c1 - 1])] if c == c0 else [], inc="S")
                h_toks[c1 - 1] = tok
            return f

        def mk_proj(do):
            def f():
                sl = do % 3
                tok_w = M.emit("gpsimd",
                               lambda E, do=do, sl=sl: E.dma_start(out=s_wprb[:, sl * 4096:(sl + 1) * 4096],
                                                                   in_=d_wprT[do * 128:(do + 1) * 128, :]),
                               waits=[("T", wpr_cons[sl])], inc="wp%d" % sl, n=16)
                tt = None
                for c in range(HC):
                    tt = bmm(6, (0, 512),
                             s_wprb[:, sl * 4096 + c * 128: sl * 4096 + (c + 1) * 128],
                             s_h[:, c * TILE:(c + 1) * TILE],
                             start=(c == 0), stop=(c == HC - 1),
                             waits=[("wp%d" % sl, tok_w), ("S", h_toks[HC - 1])])
                wpr_cons[sl] = tt
                st["pass2_last"] = tt
                tok_ev = M.emit("scalar",
                                lambda E, do=do: E.activation(out=s_ofm[:, 0:512], in_=pb[6][:],
                                                              func=AF.Identity, bias=s_bpr[:, do:do + 1]),
                                waits=[("T", tt), ("T", ofm_trans_last[0])], inc="S")
                bread(6, "S", tok_ev)
                tt2 = None
                for g in range(G):
                    tt2 = btrans(7, (g * 128, (g + 1) * 128),
                                 s_ofm[:, g * 128:(g + 1) * 128],
                                 start=True, stop=True, waits=[("S", tok_ev)])
                ofm_trans_last[0] = tt2
                tok_res = M.emit("vector",
                                 lambda E, do=do, xb=xb:
                                 E.scalar_tensor_tensor(
                                     out=bass.AP(s_x, xb + do * 128, [[2 * G * D, 128], [D, 4], [1, 128]]),
                                     in0=pb[7][:].rearrange("p (g c) -> p g c", c=128), scalar=1.0,
                                     in1=bass.AP(s_x, xb + do * 128, [[2 * G * D, 128], [D, 4], [1, 128]]),
                                     op0=ALU.mult, op1=ALU.add),
                                 waits=[("T", tt2)], inc="V")
                bread(7, "V", tok_res)
                st["last_res"] = tok_res
            return f

        for c in range(HC // 2):
            groups.append(mk_fc(c))
        groups.append(mk_gelu_batch(0, HC // 2))
        for c in range(HC // 2, HC):
            groups.append(mk_fc(c))
        groups.append(mk_gelu_batch(HC // 2, HC))
        for do in range(DC):
            groups.append(mk_proj(do))

        def mk_store():
            def f():
                par = tp % 2
                store_tok[par] = M.emit("sync",
                                        lambda E, tp=tp, xb=xb: E.dma_start(
                                            out=d_y[tp * TILE:(tp + 1) * TILE, :].rearrange("(g p) d -> p g d", p=128),
                                            in_=s_x[:, xb:xb + G * D].rearrange("p (g d) -> p g d", d=D)),
                                        waits=[("V", st["last_res"])], inc="dy%d" % par, n=16)
            return f
        groups.append(mk_store())
        return groups

    h_guard_g = [0]

    def emit_outproj(t, pv_last_waits):
        tt = None
        for ic in range(DC):
            sl = ic % 3
            tok_w = M.emit("gpsimd",
                           lambda E, ic=ic, sl=sl: E.dma_start(out=s_woutb[:, sl * 1024:(sl + 1) * 1024],
                                                               in_=d_woutb[ic * 128:(ic + 1) * 128, :]),
                           waits=[("T", wout_cons[sl])], inc="wo%d" % sl, n=16)
            for g in range(G):
                for half in range(2):
                    tt = bmm(2 * g + half, (0, 512),
                             s_ao[:, ic * TILE + g * 128: ic * TILE + (g + 1) * 128],
                             s_woutb[:, sl * 1024 + half * 512: sl * 1024 + (half + 1) * 512],
                             start=(ic == 0), stop=(ic == DC - 1),
                             waits=[("wo%d" % sl, tok_w)] + (pv_last_waits if ic == 0 else []))
            wout_cons[sl] = tt
        tok_T = M.emit("tensor", lambda E: E.nop(), inc="T")
        tvlast = None
        xb = st["xbase"]
        for g in range(G):
            for half in range(2):
                bi = 2 * g + half
                tvlast = M.emit("vector",
                                lambda E, g=g, half=half, bi=bi, xb=xb:
                                E.scalar_tensor_tensor(
                                    out=s_x[:, xb + g * D + half * 512: xb + g * D + (half + 1) * 512],
                                    in0=pb[bi][:], scalar=1.0,
                                    in1=s_x[:, xb + g * D + half * 512: xb + g * D + (half + 1) * 512],
                                    op0=ALU.mult, op1=ALU.add),
                                waits=[("T", tok_T)], inc="V")
                bread(bi, "V", tvlast)
        return tok_T, tvlast

    pending_ffn = None
    for t in range(NT):
        st["xbase"] = (t % 2) * G * D
        par = t % 2
        tok_x = M.emit("sync",
                       lambda E, t=t, xb=st["xbase"]: E.dma_start(
                           out=s_x[:, xb:xb + G * D].rearrange("p (g d) -> p g d", d=D),
                           in_=d_x[t * TILE:(t + 1) * TILE, :].rearrange("(g p) d -> p g d", p=128)),
                       waits=[("dy%d" % par, store_tok[par])], inc="dx", n=16)

        tv, ts = emit_stats("dx", tok_x)
        T1 = 48
        tv = M.emit("vector", lambda E: E.reciprocal(out=s_stat[:, T1:T1 + 4], in_=s_stat[:, VARP:VARP + 4]),
                    inc="V")
        tv = M.emit("vector", lambda E: E.tensor_scalar_add(s_stat[:, V2P:V2P + 4], s_stat[:, VARP:VARP + 4],
                                                            -1e-5), inc="V")
        tv = M.emit("vector", lambda E: E.tensor_tensor(out=s_stat[:, V2P:V2P + 4], in0=s_stat[:, V2P:V2P + 4],
                                                        in1=s_stat[:, T1:T1 + 4], op=ALU.mult), inc="V")
        tv = M.emit("vector", lambda E: E.tensor_scalar_add(s_stat[:, V2P:V2P + 4], s_stat[:, V2P:V2P + 4],
                                                            1e-5), inc="V")
        ts = M.emit("scalar", lambda E: E.activation(out=s_stat[:, T1:T1 + 4], in_=s_stat[:, V2P:V2P + 4],
                                                     func=AF.Ln),
                    waits=[("V", tv), ("S", ts)], inc="S")
        ts = M.emit("scalar", lambda E: E.activation(out=s_stat[:, V2P:V2P + 4], in_=s_stat[:, T1:T1 + 4],
                                                     func=AF.Exp, scale=s_cm05[:]), inc="S")
        tv = M.emit("vector", lambda E: E.tensor_tensor(out=s_stat[:, ASC:ASC + 4], in0=s_stat[:, R1:R1 + 4],
                                                        in1=s_stat[:, V2P:V2P + 4], op=ALU.mult),
                    waits=[("S", ts)], inc="V")
        tv = M.emit("vector", lambda E: E.scalar_tensor_tensor(out=s_stat[:, NMA:NMA + 4],
                                                               in0=s_stat[:, MU:MU + 4],
                                                               scalar=-1.0, in1=s_stat[:, ASC:ASC + 4],
                                                               op0=ALU.mult, op1=ALU.mult), inc="V")
        xn1_ev = emit_norm_transpose(s_xn1, ASC, NMA, 0)

        qk_ev, v_ev = emit_qkv(t, xn1_ev)

        att_steps, att = make_attention_steps(t, qk_ev, v_ev)
        if pending_ffn is not None:
            ng, ns = len(pending_ffn), len(att_steps)
            si = 0
            lead = min(ns, 10)
            while si < lead:
                att_steps[si]()
                si += 1
            for gi, grp in enumerate(pending_ffn):
                grp()
                want = min(ns, lead + (gi + 1) * (ns - lead) // ng)
                while si < want:
                    att_steps[si]()
                    si += 1
            while si < ns:
                att_steps[si]()
                si += 1
        else:
            for stp in att_steps:
                stp()
        h_guard_g[0] = st.get("pass2_last", 0)

        NI = G * H
        qk_write_guard[0] = att["tokA"][NI - 1]
        v_write_guard = att["tokG"][NI - 1]
        for k in range(7):
            prev_expC[(NI - 7 + k) % 7] = att["tokC"][NI - 7 + k]
            prev_trE[(NI - 7 + k) % 7] = att["tokE"][NI - 7 + k]

        if (t + 1) % SEQ_TILES != 0:
            kp_tok = M.emit("vector",
                            lambda E: E.tensor_copy(
                                out=s_kprev[:].rearrange("p (f c) -> p f c", c=128),
                                in_=s_k[:].rearrange("p (f t) -> p f t", t=TILE)[:, :, TILE - 128:TILE]),
                            waits=[("T", att["tokG"][15])], inc="V")
            vp_tok = M.emit("vector",
                            lambda E: E.tensor_copy(out=s_vprev[:], in_=s_v[:, (G - 1) * D: G * D]),
                            waits=[("T", att["tokG"][15])], inc="V")

        li = NI - 1
        pvw = [(("V", "S")[att["engH"][li]], att["tokH"][li]),
               (("V", "S")[att["engH"][li - 1]], att["tokH"][li - 1])]
        opj_T, tvlast = emit_outproj(t, pvw)
        ao_guard = opj_T

        tv2, ts2 = emit_stats("V", tvlast)
        tsn = None
        for g in range(G):
            tsn = M.emit("vector",
                         lambda E, g=g: E.scalar_tensor_tensor(out=s_stat[:, NMA + g:NMA + g + 1],
                                                               in0=s_stat[:, MU + g:MU + g + 1], scalar=-1.0,
                                                               in1=s_stat[:, R1 + g:R1 + g + 1],
                                                               op0=ALU.mult, op1=ALU.mult),
                         waits=[("S", ts2)], inc="V")
        st["xn2_ev"] = emit_norm_transpose(s_xn2, R1, NMA, 0)
        xn2_cons = 0
        pending_ffn = emit_ffn_groups(t)

    for grp in pending_ffn:
        grp()

    M.emit("sync", lambda E: E.nop(), waits=[("dy0", cnt.get("dy0", 0)), ("dy1", cnt.get("dy1", 0))])

    with nc.Block("main") as b2:
        M.replay(b2, sems)

    stack.close()
    return nc


_NC_CACHE = None


def kernel(L0_attn_x, L0_attn_ln_gamma, L0_attn_ln_beta, L0_attn_w_qkv, L0_attn_w_out,
           L0_ffn_W_fc, L0_ffn_b_fc, L0_ffn_A_fc, L0_ffn_B_fc,
           L0_ffn_W_proj, L0_ffn_b_proj, L0_ffn_A_proj, L0_ffn_B_proj):
    global _NC_CACHE
    x = np.ascontiguousarray(np.asarray(L0_attn_x, dtype=np.float32))
    B, N, Dm = x.shape
    ncores = 8
    per = B // ncores

    if _NC_CACHE is None:
        _NC_CACHE = build_program(dbg=DBG)
    nc = _NC_CACHE

    ident_f = np.eye(128, dtype=np.float32)
    ii = np.arange(128)[:, None]
    jj = np.arange(256)[None, :]
    prev = jj < 128
    jjm = np.where(prev, jj, jj - 128)
    bad = np.where(prev, jjm < ii, jjm > ii)
    mask_m = np.where(bad, np.float32(NEG), np.float32(0.0)).astype(BFNP)
    ones_f = np.ones((1, 128), dtype=np.float32)

    common = dict(
        gamma=np.asarray(L0_attn_ln_gamma, np.float32),
        beta=np.asarray(L0_attn_ln_beta, np.float32),
        wqkv=np.asarray(L0_attn_w_qkv, np.float32),
        wout=np.asarray(L0_attn_w_out, np.float32),
        wfc=np.asarray(L0_ffn_W_fc, np.float32),
        bfc=np.asarray(L0_ffn_b_fc, np.float32),
        afc=np.asarray(L0_ffn_A_fc, np.float32),
        bfcB=np.asarray(L0_ffn_B_fc, np.float32),
        wpr=np.asarray(L0_ffn_W_proj, np.float32),
        bpr=np.asarray(L0_ffn_b_proj, np.float32),
        apr=np.asarray(L0_ffn_A_proj, np.float32),
        bprB=np.asarray(L0_ffn_B_proj, np.float32),
        identf=ident_f, maskm=mask_m, onesf=ones_f,
        onesb=np.ones((1, 128), dtype=BFNP),
    )
    in_maps = []
    for c in range(ncores):
        m = dict(common)
        m["x"] = np.ascontiguousarray(x[c * per:(c + 1) * per].reshape(TOK, Dm))
        in_maps.append(m)

    res = run_bass_kernel_spmd(nc, in_maps, core_ids=list(range(ncores)))
    out = np.empty((B, N, Dm), np.float32)
    for c in range(ncores):
        out[c * per:(c + 1) * per] = res.results[c]["y"].reshape(per, N, Dm)
    if DBG:
        global DBG_RES
        DBG_RES = res.results
    return out


# version 22
# speedup vs baseline: 1.1508x; 1.0142x over previous
import numpy as np
import ml_dtypes
from contextlib import ExitStack

import concourse.bass as bass
import concourse.mybir as mybir
from concourse.bass_utils import run_bass_kernel_spmd

F32 = mybir.dt.float32
BF16 = mybir.dt.bfloat16
I32 = mybir.dt.int32
AF = mybir.ActivationFunctionType
ALU = mybir.AluOpType
AX = mybir.AxisListType
BFNP = ml_dtypes.bfloat16

D = 1024
TOK = 4096
TILE = 512
NT = TOK // TILE
G = TILE // 128
DC = D // 128
H = 16
DH = 64
FFH = 4096
HC = FFH // 128
NEG = -1.0e30
SEQ_TILES = 2048 // TILE

ENGS = ("sync", "tensor", "vector", "scalar", "gpsimd")


class Prog:

    def __init__(self, cnt):
        self.ops = {e: [] for e in ENGS}
        self.cnt = cnt
        self.last_wait = {e: {} for e in ENGS}
        self.fence_snap = {}

    def emit(self, eng, fn, waits=(), inc=None, n=1):
        own = _OWN.get(eng)
        merged = dict(self.fence_snap)
        if own in ("V", "S"):
            if inc is None:
                inc = own
            merged[own] = self.cnt.get(own, 0)
        for s, v in waits:
            if v > merged.get(s, 0):
                merged[s] = v
        w = []
        for s, v in merged.items():
            if s == "T" and eng == "tensor":
                continue
            if v > self.last_wait[eng].get(s, 0):
                self.last_wait[eng][s] = v
                w.append((s, v))
        self.ops[eng].append((fn, w, inc, n))
        if inc is not None:
            self.cnt[inc] = self.cnt.get(inc, 0) + n
            return self.cnt[inc]
        return None

    def fence(self):
        self.fence_snap = {s: self.cnt.get(s, 0) for s in ("T", "V", "S")}

    def replay(self, block, sems):
        for ename in ENGS:
            ops = self.ops[ename]

            def body(eng, ops=ops):
                for fn, w, inc, n in ops:
                    for s, v in w:
                        eng.wait_ge(sems[s], v)
                    instr = fn(eng)
                    if inc is not None:
                        instr.then_inc(sems[inc], n)
            getattr(block, ename)(body)


_OWN = {"tensor": "T", "vector": "V", "scalar": "S"}


DBG = False


def build_program(dbg=False):
    nc = bass.Bass("TRN2", target_bir_lowering=False, detect_race_conditions=False)

    d_x = nc.dram_tensor("x", [TOK, D], F32, kind="ExternalInput").ap()
    d_gamma = nc.dram_tensor("gamma", [D], F32, kind="ExternalInput").ap()
    d_beta = nc.dram_tensor("beta", [D], F32, kind="ExternalInput").ap()
    d_wqkv = nc.dram_tensor("wqkv", [D, 3 * D], F32, kind="ExternalInput").ap()
    d_wout = nc.dram_tensor("wout", [D, D], F32, kind="ExternalInput").ap()
    d_wfc = nc.dram_tensor("wfc", [FFH, D], F32, kind="ExternalInput").ap()
    d_bfc = nc.dram_tensor("bfc", [FFH], F32, kind="ExternalInput").ap()
    d_afc = nc.dram_tensor("afc", [16, D], F32, kind="ExternalInput").ap()
    d_bfcB = nc.dram_tensor("bfcB", [FFH, 16], F32, kind="ExternalInput").ap()
    d_wpr = nc.dram_tensor("wpr", [D, FFH], F32, kind="ExternalInput").ap()
    d_bpr = nc.dram_tensor("bpr", [D], F32, kind="ExternalInput").ap()
    d_apr = nc.dram_tensor("apr", [16, FFH], F32, kind="ExternalInput").ap()
    d_bprB = nc.dram_tensor("bprB", [D, 16], F32, kind="ExternalInput").ap()
    d_identf = nc.dram_tensor("identf", [128, 128], F32, kind="ExternalInput").ap()
    d_mask = nc.dram_tensor("maskm", [128, 256], BF16, kind="ExternalInput").ap()
    d_onesf = nc.dram_tensor("onesf", [1, 128], F32, kind="ExternalInput").ap()
    d_onesb = nc.dram_tensor("onesb", [1, 128], BF16, kind="ExternalInput").ap()

    d_y = nc.dram_tensor("y", [TOK, D], F32, kind="ExternalOutput").ap()
    d_dbg = {}
    if dbg:
        for name, shape, dt in [("wall", [128, 24576], BF16), ("bw", [128, 16], F32),
                                ("bwv", [1, 1024], F32), ("scl", [128, 8], F32),
                                ("btfc", [16, FFH], BF16), ("a2fc", [16, D], BF16),
                                ("stat", [128, 64], F32), ("xnfm", [128, 4096], BF16),
                                ("qd", [128, 4096], BF16), ("kd", [128, 4096], BF16),
                                ("vd", [128, 4096], BF16), ("ao", [128, 4096], BF16),
                                ("x1", [128, 4096], F32), ("hd", [128, HC * TILE], BF16),
                                ("wfcT0", [128, 1024], BF16), ("wprT0", [128, 4096], BF16)]:
            d_dbg[name] = nc.dram_tensor("dbg_" + name, shape, dt, kind="ExternalOutput").ap()

    d_wfcT = nc.dram_tensor("wfcT_scr", [FFH, D], BF16, kind="Internal").ap()
    d_wprT = nc.dram_tensor("wprT_scr", [D, FFH], BF16, kind="Internal").ap()
    d_woutb = nc.dram_tensor("wout_scr", [D, D], BF16, kind="Internal").ap()
    d_wqs = nc.dram_tensor("wqs_scr", [16 * 128, D], BF16, kind="Internal").ap()
    d_wvs = nc.dram_tensor("wvs_scr", [DC * 128, D], BF16, kind="Internal").ap()

    cnt = {}
    stack = ExitStack()
    e = stack.enter_context

    sems = {}
    for s in ("dc", "da", "db", "dwa", "dwb", "dx", "dy0", "dy1",
              "wo0", "wo1", "wo2", "wf0", "wf1", "wf2",
              "wp0", "wp1", "wp2",
              "wq0", "wq1", "wv0", "wv1", "T", "V", "S", "BAR"):
        sems[s] = e(nc.semaphore(s))

    pb = [e(nc.psum_tensor(f"pb{i}", [128, 512], F32)) for i in range(8)]

    s_identf = e(nc.sbuf_tensor("sbident_f", [128, 128], F32))
    s_identb = e(nc.sbuf_tensor("sbident_b", [128, 128], BF16))
    s_mask = e(nc.sbuf_tensor("sb_maskm", [128, 256], BF16))
    s_maskf = e(nc.sbuf_tensor("sb_maskf", [128, 256], F32))
    s_cm1 = e(nc.sbuf_tensor("cm1", [128, 1], F32))
    s_onesf = e(nc.sbuf_tensor("sb_onesf", [1, 128], F32))
    s_gcol = e(nc.sbuf_tensor("gcol", [128, 8], F32))
    s_g8col = e(nc.sbuf_tensor("g8col", [128, 8], F32))
    s_betab = e(nc.sbuf_tensor("betab", [128, 8], BF16))
    s_bw = e(nc.sbuf_tensor("bw", [128, 16], F32))
    s_bwv = e(nc.sbuf_tensor("bwv", [1, 1024], BF16))
    s_onesb = e(nc.sbuf_tensor("sb_onesb", [1, 128], BF16))
    s_bfc = e(nc.sbuf_tensor("bfccol", [128, 32], F32))
    s_bpr = e(nc.sbuf_tensor("bprcol", [128, 8], F32))
    s_cm05 = e(nc.sbuf_tensor("cm05", [128, 1], F32))
    s_cp05 = e(nc.sbuf_tensor("cp05", [128, 1], F32))
    s_scfc = e(nc.sbuf_tensor("scfc", [128, 1], F32))
    s_rsfc = e(nc.sbuf_tensor("rsfc", [128, 1], F32))
    s_scpr = e(nc.sbuf_tensor("scpr", [128, 1], F32))
    s_rspr = e(nc.sbuf_tensor("rspr", [128, 1], F32))
    s_am = e(nc.sbuf_tensor("amcols", [128, 32], F32))
    s_hsfc = e(nc.sbuf_tensor("hsfc", [128, 1], F32))
    s_hspr = e(nc.sbuf_tensor("hspr", [128, 1], F32))
    s_amc = e(nc.sbuf_tensor("amc", [128, 1], F32))
    s_amr = e(nc.sbuf_tensor("amr", [1, 128], F32))
    s_sc1 = e(nc.sbuf_tensor("sc1", [1, 1], F32))


    s_a2fc = e(nc.sbuf_tensor("a2fc", [16, D], BF16))
    s_btfc = e(nc.sbuf_tensor("btfc", [16, FFH], BF16))
    s_a2pr = e(nc.sbuf_tensor("a2pr", [16, FFH], BF16))
    s_btpr = e(nc.sbuf_tensor("btpr", [16, D], BF16))

    P = Prog(cnt)

    def dma(prog, eng, out_ap, in_ap, sem, slow=False):
        if slow:
            return prog.emit(eng, lambda E, o=out_ap, i=in_ap: E.dma_start(out=o, in_=i, allow_slow_non_contiguous=True), inc=sem, n=16)
        return prog.emit(eng, lambda E, o=out_ap, i=in_ap: E.dma_start(out=o, in_=i), inc=sem, n=16)

    def mm(prog, out_ap, lhsT, rhs, start, stop, tp=None, waits=()):
        def f(E, o=out_ap, l=lhsT, r=rhs):
            return E.matmul(o, l, r, start=start, stop=stop, tile_position=tp,
                            skip_group_check=True)
        return prog.emit("tensor", f, waits=waits, inc="T")

    def trans(prog, out_ap, in_ap, ident, start=True, stop=True, waits=()):
        def f(E, o=out_ap, i=in_ap, idn=ident):
            return E.matmul(o, i, idn, start=start, stop=stop, is_transpose=True,
                            skip_group_check=True)
        return prog.emit("tensor", f, waits=waits, inc="T")

    with ExitStack() as arena:
        a = arena.enter_context
        t_stage = a(nc.sbuf_tensor("stage", [128, 2 * FFH], F32))
        t_q = a(nc.sbuf_tensor("tq", [128, 2048], F32))
        t_k0 = a(nc.sbuf_tensor("tk0", [128, 2048], I32))
        t_d = a(nc.sbuf_tensor("td", [128, 2048], F32))
        t_s1 = a(nc.sbuf_tensor("ts1", [128, 2048], F32))
        t_s2 = a(nc.sbuf_tensor("ts2", [128, 2048], F32))
        t_wqf = a(nc.sbuf_tensor("twqf", [128, 2 * FFH], F32))
        t_panel = a(nc.sbuf_tensor("tpanel", [128, 2 * FFH], BF16))
        t_small = a(nc.sbuf_tensor("tsmall", [128, 16], F32))

        tok_dx = 0
        tok_dx = dma(P, "sync", s_identf[:], d_identf, "dc")
        tok_dx = dma(P, "sync", s_mask[:], d_mask, "dc")
        tok_dx = dma(P, "sync", s_onesf[:], d_onesf, "dc")
        tok_dx = dma(P, "sync", s_onesb[:], d_onesb, "dc")
        tok_dx = dma(P, "sync", s_gcol[:], d_gamma.rearrange("(c p) -> p c", p=128), "dc", slow=True)
        tok_dx = dma(P, "sync", t_small[:, 0:8], d_beta.rearrange("(c p) -> p c", p=128), "dc", slow=True)
        tok_dx = dma(P, "sync", s_bfc[:], d_bfc.rearrange("(c p) -> p c", p=128), "dc", slow=True)
        tok_dx = dma(P, "sync", s_bpr[:], d_bpr.rearrange("(c p) -> p c", p=128), "dc", slow=True)
        tok_consts = tok_dx
        P.emit("vector", lambda E: E.memset(s_cm05[:], -0.5))
        P.emit("vector", lambda E: E.memset(s_cp05[:], 0.5))
        P.emit("vector", lambda E: E.memset(s_cm1[:], -1.0))
        P.emit("vector", lambda E: E.tensor_copy(out=s_identb[:], in_=s_identf[:]),
               waits=[("dc", tok_consts)])
        P.emit("vector", lambda E: E.tensor_copy(out=s_maskf[:], in_=s_mask[:]),
               waits=[("dc", tok_consts)])
        P.emit("vector", lambda E: E.tensor_scalar_mul(s_g8col[:], s_gcol[:], 0.125),
               waits=[("dc", tok_consts)])
        tv = P.emit("vector", lambda E: E.tensor_copy(out=s_betab[:], in_=t_small[:, 0:8]),
                    waits=[("dc", tok_consts)], inc="V")

        t_bwv32 = a(nc.sbuf_tensor("tbwv32", [1, 1024], F32))
        cast_tk2 = [0, 0]
        pstore_tk = [0, 0]
        bw_read = 0
        bwv_read = 0
        for dc in range(DC):
            sem = "da" if dc % 2 == 0 else "db"
            ssem = "dwa" if dc % 2 == 0 else "dwb"
            slot = (dc % 2) * 4096
            tok_l = P.emit("sync",
                           lambda E, dc=dc, slot=slot:
                           E.dma_start(out=t_stage[:, slot:slot + 3072],
                                       in_=d_wqkv[dc * 128:(dc + 1) * 128, :]),
                           waits=[("S", cast_tk2[dc % 2])], inc=sem, n=16)
            cast = P.emit("scalar",
                          lambda E, slot=slot: E.copy(out=t_panel[:, slot:slot + 3072],
                                                      in_=t_stage[:, slot:slot + 3072]),
                          waits=[(sem, tok_l), (ssem, pstore_tk[dc % 2]), ("dc", tok_consts)], inc="S")
            cast_tk2[dc % 2] = cast
            tT = None
            for oc in range(16):
                tT = mm(P, pb[0][:, oc:oc + 1],
                        t_panel[:, slot + oc * 128: slot + (oc + 1) * 128],
                        s_betab[:, dc:dc + 1], start=True, stop=True,
                        waits=[("S", cast), ("V", bw_read)])
            for half in range(2):
                tT = mm(P, pb[1 + half][0:1, 0:512], s_betab[:, dc:dc + 1],
                        t_panel[:, slot + 2048 + half * 512: slot + 2048 + (half + 1) * 512],
                        start=True, stop=True,
                        waits=[("V", bwv_read)])
            tTn = P.emit("tensor", lambda E: E.nop(), inc="T")
            if dc == 0:
                bw_read = P.emit("vector", lambda E: E.tensor_copy(out=s_bw[:], in_=pb[0][:, 0:16]),
                                 waits=[("T", tTn)], inc="V")
                P.emit("vector", lambda E: E.tensor_copy(out=t_bwv32[0:1, 0:512], in_=pb[1][0:1, 0:512]),
                       inc="V")
                bwv_read = P.emit("vector", lambda E: E.tensor_copy(out=t_bwv32[0:1, 512:1024],
                                                                    in_=pb[2][0:1, 0:512]), inc="V")
            else:
                bw_read = P.emit("vector", lambda E: E.tensor_tensor(out=s_bw[:], in0=s_bw[:],
                                                                     in1=pb[0][:, 0:16], op=ALU.add),
                                 waits=[("T", tTn)], inc="V")
                P.emit("vector", lambda E: E.tensor_tensor(out=t_bwv32[0:1, 0:512], in0=t_bwv32[0:1, 0:512],
                                                           in1=pb[1][0:1, 0:512], op=ALU.add), inc="V")
                bwv_read = P.emit("vector", lambda E: E.tensor_tensor(out=t_bwv32[0:1, 512:1024],
                                                                      in0=t_bwv32[0:1, 512:1024],
                                                                      in1=pb[2][0:1, 0:512], op=ALU.add),
                                  inc="V")
            P.emit("scalar",
                   lambda E, dc=dc, slot=slot: E.activation(out=t_panel[:, slot:slot + 1024],
                                                            in_=t_panel[:, slot:slot + 1024],
                                                            func=AF.Copy, scale=s_g8col[:, dc:dc + 1]),
                   waits=[("T", tTn)])
            fold = P.emit("scalar",
                          lambda E, dc=dc, slot=slot: E.activation(out=t_panel[:, slot + 1024:slot + 3072],
                                                                   in_=t_panel[:, slot + 1024:slot + 3072],
                                                                   func=AF.Copy, scale=s_gcol[:, dc:dc + 1]),
                          inc="S")
            P.emit("gpsimd",
                   lambda E, dc=dc, slot=slot: E.dma_start(
                       out=bass.AP(d_wqs.tensor, dc * 128, [[1024, 128], [128 * 1024, 16], [1, 128]]),
                       in_=bass.AP(t_panel, slot, [[8192, 128], [128, 16], [1, 128]])),
                   waits=[("S", fold)], inc=ssem, n=16)
            pstore_tk[dc % 2] = P.emit("gpsimd",
                                       lambda E, dc=dc, slot=slot:
                                       E.dma_start(out=d_wvs[dc * 128:(dc + 1) * 128, :],
                                                   in_=t_panel[:, slot + 2048:slot + 3072]),
                                       inc=ssem, n=16)
        tok_bw = P.emit("vector", lambda E: E.tensor_scalar_mul(s_bw[:, 0:8], s_bw[:, 0:8], 0.125),
                        inc="V")
        tok_bwv = P.emit("vector", lambda E: E.tensor_copy(out=s_bwv[:], in_=t_bwv32[:]), inc="V")

        P.fence()
        st_toks = list(pstore_tk)
        cast_tk = [0, 0]
        for ic in range(DC):
            sem = "da" if ic % 2 == 0 else "db"
            ssem = "dwa" if ic % 2 == 0 else "dwb"
            tok_l = P.emit("sync",
                           lambda E, ic=ic: E.dma_start(out=t_stage[:, (ic % 2) * 4096:(ic % 2) * 4096 + 1024],
                                                        in_=d_wout[ic * 128:(ic + 1) * 128, :]),
                           waits=[("V", cast_tk[ic % 2])], inc=sem, n=16)
            tok_c = P.emit("vector",
                           lambda E, ic=ic: E.tensor_copy(out=t_panel[:, (ic % 2) * 1024:(ic % 2) * 1024 + 1024],
                                                          in_=t_stage[:, (ic % 2) * 4096:(ic % 2) * 4096 + 1024]),
                           waits=[(sem, tok_l), (ssem, st_toks[ic % 2])], inc="V")
            cast_tk[ic % 2] = tok_c
            st_toks[ic % 2] = P.emit(
                "gpsimd",
                lambda E, ic=ic: E.dma_start(out=d_woutb[ic * 128:(ic + 1) * 128, :],
                                             in_=t_panel[:, (ic % 2) * 1024:(ic % 2) * 1024 + 1024]),
                waits=[("V", tok_c)], inc=ssem, n=16)
        P.fence()

        tok_a = dma(P, "sync", t_stage[:16, 0:1024], d_afc, "da")
        tva = P.emit("vector", lambda E: E.tensor_scalar(out=s_a2fc[:], in0=t_stage[:16, 0:1024],
                                                         scalar1=2.0, scalar2=None, op0=ALU.mult),
                     waits=[("da", tok_a)], inc="V")
        tok_a = P.emit("sync", lambda E: E.dma_start(out=t_stage[:16, 0:4096], in_=d_apr),
                       waits=[("V", tva)], inc="da", n=16)
        tva = P.emit("vector", lambda E: E.tensor_scalar(out=s_a2pr[:], in0=t_stage[:16, 0:4096],
                                                         scalar1=2.0, scalar2=None, op0=ALU.mult),
                     waits=[("da", tok_a)], inc="V")
        P.fence()
        btr = [0, 0]
        bev = [0, 0]
        for c in range(HC):
            sem = "da" if c % 2 == 0 else "db"
            tok_l = P.emit("sync",
                           lambda E, c=c: E.dma_start(out=t_stage[:, (c % 2) * 4096:(c % 2) * 4096 + 16],
                                                      in_=d_bfcB[c * 128:(c + 1) * 128, :]),
                           waits=[("T", btr[c % 2])], inc=sem, n=16)
            tt = trans(P, pb[2 + c % 2][0:16, 0:128],
                       t_stage[:, (c % 2) * 4096:(c % 2) * 4096 + 16], s_identf[:],
                       waits=[(sem, tok_l), ("S", bev[c % 2])])
            btr[c % 2] = tt
            bev[c % 2] = P.emit("scalar",
                                lambda E, c=c: E.copy(out=s_btfc[:16, c * 128:(c + 1) * 128],
                                                      in_=pb[2 + c % 2][0:16, 0:128]),
                                waits=[("T", tt)], inc="S")
        P.fence()
        btr = [0, 0]
        bev = [0, 0]
        for c in range(DC):
            sem = "da" if c % 2 == 0 else "db"
            tok_l = P.emit("sync",
                           lambda E, c=c: E.dma_start(out=t_stage[:, (c % 2) * 4096:(c % 2) * 4096 + 16],
                                                      in_=d_bprB[c * 128:(c + 1) * 128, :]),
                           waits=[("T", btr[c % 2])], inc=sem, n=16)
            tt = trans(P, pb[2 + c % 2][0:16, 0:128],
                       t_stage[:, (c % 2) * 4096:(c % 2) * 4096 + 16], s_identf[:],
                       waits=[(sem, tok_l), ("S", bev[c % 2])])
            btr[c % 2] = tt
            bev[c % 2] = P.emit("scalar",
                                lambda E, c=c: E.copy(out=s_btpr[:16, c * 128:(c + 1) * 128],
                                                      in_=pb[2 + c % 2][0:16, 0:128]),
                                waits=[("T", tt)], inc="S")
        P.fence()

        def quant_weight(d_w, nchunk, free, s_scale, s_rscale, s_hscale, am_ncols, d_scrT, a2, bt,
                         panel_cols, tpos_cols):
            P.fence()
            red_tk = [0, 0]
            for c in range(nchunk):
                sl = (c % 2) * 4096
                sem = "da" if c % 2 == 0 else "db"
                tok_l = P.emit("sync",
                               lambda E, c=c, sl=sl: E.dma_start(out=t_stage[:, sl:sl + free],
                                                                 in_=d_w[c * 128:(c + 1) * 128, :]),
                               waits=[("V", red_tk[c % 2])], inc=sem, n=16)
                red_tk[c % 2] = P.emit("vector",
                                       lambda E, c=c, sl=sl: E.reduce_max(out=s_am[:, c:c + 1],
                                                                          in_=t_stage[:, sl:sl + free],
                                                                          axis=AX.X, apply_absolute_value=True),
                                       waits=[(sem, tok_l)], inc="V")
            P.fence()
            tok = P.emit("vector", lambda E: E.reduce_max(out=s_amc[:], in_=s_am[:, 0:am_ncols], axis=AX.X),
                         inc="V")
            tt = trans(P, pb[3][0:1, 0:128], s_amc[:], s_identf[:], waits=[("V", tok)])
            tok = P.emit("vector", lambda E: E.tensor_copy(out=s_amr[:], in_=pb[3][0:1, 0:128]),
                         waits=[("T", tt)], inc="V")
            tok = P.emit("vector", lambda E: E.reduce_max(out=s_sc1[:], in_=s_amr[:], axis=AX.X), inc="V")
            tok = P.emit("vector", lambda E: E.tensor_scalar_mul(s_sc1[:], s_sc1[:], float(np.float32(1.0) / np.float32(127.0))),
                         inc="V")
            tt = mm(P, pb[3][:, 0:1], s_onesf[:], s_sc1[:], start=True, stop=True,
                    waits=[("V", tok)])
            tok = P.emit("vector", lambda E: E.tensor_copy(out=s_scale[:], in_=pb[3][:, 0:1]),
                         waits=[("T", tt)], inc="V")
            tok = P.emit("vector", lambda E: E.reciprocal(out=s_rscale[:], in_=s_scale[:]), inc="V")
            tok = P.emit("vector", lambda E: E.tensor_scalar_mul(s_hscale[:], s_scale[:], 0.5), inc="V")
            P.fence()
            nsl = free // 1024
            tok_store = 0
            ev_bank = {}
            trans_last = 0
            tq_tk = [0, 0]
            gsl = [0]
            stt_tk = [0, 0]
            pend_tw = [None]
            twc = [0] * nchunk
            trans_c = [0] * nchunk
            ev_bank = {}
            store_tk = [0, 0]

            def flush_tw(prog):
                if pend_tw[0] is not None:
                    pf, pc, ptk, ptl = pend_tw[0]
                    tw = prog.emit("scalar", pf, waits=[("V", ptk), ("T", ptl)], inc="S")
                    twc[pc] = tw
                    pend_tw[0] = None

            def quant_slices(c):
                sl = (c % 2) * 4096
                wsl = (c % 2) * 4096
                sem = "da" if c % 2 == 0 else "db"
                tok_l = P.emit("sync",
                               lambda E, c=c, sl=sl: E.dma_start(out=t_stage[:, sl:sl + free],
                                                                 in_=d_w[c * 128:(c + 1) * 128, :]),
                               waits=[("S", tq_tk[c % 2])], inc=sem, n=16)
                tq = 0
                for s in range(nsl):
                    slot = gsl[0] % 2
                    q = slot * 1024
                    gsl[0] += 1
                    src = t_stage[:, sl + s * 1024: sl + (s + 1) * 1024]
                    tq = P.emit("scalar",
                                lambda E, src=src, q=q: E.activation(out=t_q[:, q:q + 1024], in_=src,
                                                                     func=AF.Copy, scale=s_rscale[:]),
                                waits=[(sem, tok_l), ("V", stt_tk[slot])], inc="S")
                    flush_tw(P)
                    tk = P.emit("vector",
                                lambda E, q=q: E.tensor_copy(out=t_k0[:, q:q + 1024], in_=t_q[:, q:q + 1024]),
                                waits=[("S", tq)], inc="V")
                    tk = P.emit("vector",
                                lambda E, q=q: E.tensor_tensor(out=t_d[:, q:q + 1024], in0=t_q[:, q:q + 1024],
                                                               in1=t_k0[:, q:q + 1024], op=ALU.subtract),
                                inc="V")
                    t1 = P.emit("scalar",
                                lambda E, q=q: E.activation(out=t_s1[:, q:q + 1024], in_=t_d[:, q:q + 1024],
                                                            func=AF.Sign, bias=s_cm05[:]),
                                waits=[("V", tk)], inc="S")
                    t1 = P.emit("scalar",
                                lambda E, q=q: E.activation(out=t_s2[:, q:q + 1024], in_=t_d[:, q:q + 1024],
                                                            func=AF.Sign, bias=s_cp05[:]), inc="S")
                    tk = P.emit("vector",
                                lambda E, q=q: E.scalar_tensor_tensor(out=t_s1[:, q:q + 1024],
                                                                      in0=t_k0[:, q:q + 1024], scalar=2.0,
                                                                      in1=t_s1[:, q:q + 1024],
                                                                      op0=ALU.mult, op1=ALU.add),
                                waits=[("S", t1)], inc="V")
                    tk = P.emit("vector",
                                lambda E, q=q: E.tensor_tensor(out=t_d[:, q:q + 1024], in0=t_s1[:, q:q + 1024],
                                                               in1=t_s2[:, q:q + 1024], op=ALU.add),
                                inc="V")
                    stt_tk[slot] = tk
                    def mk_tw(s=s, q=q, wsl=wsl):
                        return lambda E: E.activation(out=t_wqf[:, wsl + s * 1024: wsl + (s + 1) * 1024],
                                                      in_=t_d[:, q:q + 1024],
                                                      func=AF.Copy, scale=s_hscale[:])
                    pend_tw[0] = (mk_tw(), c, tk, trans_c[c - 2] if c >= 2 else 0)
                tq_tk[c % 2] = tq

            def trans_store(c):
                wsl = (c % 2) * 4096
                psl = (c % 2) * free
                nblk = free // 128
                ev_tok = None
                tt = None
                for b in range(nblk):
                    bi = 4 + ((b // 4) % 2)
                    bank = pb[bi]
                    tt = trans(P, bank[:, (b % 4) * 128:(b % 4) * 128 + 128],
                               t_wqf[:, wsl + b * 128: wsl + (b + 1) * 128], s_identf[:],
                               start=True, stop=False,
                               waits=[("S", twc[c]), ("S", ev_bank.get(bi, 0))])
                    tt = mm(P, bank[:, (b % 4) * 128:(b % 4) * 128 + 128],
                            a2[:16, b * 128:(b + 1) * 128],
                            bt[:16, c * 128:(c + 1) * 128],
                            start=False, stop=True)
                    if b % 4 == 3:
                        ev_tok = P.emit("scalar",
                                        lambda E, b=b, bank=bank, psl=psl:
                                        E.copy(out=t_panel[:, psl + (b - 3) * 128: psl + (b + 1) * 128],
                                               in_=bank[:]),
                                        waits=[("T", tt), ("dwa" if c % 2 == 0 else "dwb", store_tk[c % 2])],
                                        inc="S")
                        ev_bank[bi] = ev_tok
                trans_c[c] = tt
                store_tk[c % 2] = P.emit("gpsimd",
                                         lambda E, c=c, psl=psl:
                                         E.dma_start(out=d_scrT[c * 128:(c + 1) * 128, :],
                                                     in_=t_panel[:, psl:psl + free]),
                                         waits=[("S", ev_tok)],
                                         inc="dwa" if c % 2 == 0 else "dwb", n=16)

            for c in range(nchunk):
                quant_slices(c)
                flush_tw(P)
                if c >= 1:
                    trans_store(c - 1)
            trans_store(nchunk - 1)
            P.fence()

        quant_weight(d_wfc, HC, 1024, s_scfc, s_rsfc, s_hsfc, HC, d_wfcT, s_a2fc, s_btfc, 1024, None)
        quant_weight(d_wpr, DC, 4096, s_scpr, s_rspr, s_hspr, DC, d_wprT, s_a2pr, s_btpr, 4096, None)

        if dbg:
            P.fence()
            P.emit("sync", lambda E: E.nop(),
                   waits=[(k, cnt.get(k, 0)) for k in ("T", "V", "S")])
            pass
            dma(P, "sync", d_dbg["bw"], s_bw[:], "dc")
            dma(P, "sync", d_dbg["bwv"], s_bwv[:], "dc")
            P.emit("vector", lambda E: E.tensor_copy(out=s_am[:, 0:1], in_=s_scfc[:]))
            P.emit("vector", lambda E: E.tensor_copy(out=s_am[:, 1:2], in_=s_rsfc[:]))
            P.emit("vector", lambda E: E.tensor_copy(out=s_am[:, 2:3], in_=s_scpr[:]))
            tvd = P.emit("vector", lambda E: E.tensor_copy(out=s_am[:, 3:4], in_=s_rspr[:]), inc="V")
            dma(P, "sync", d_dbg["scl"], s_am[:, 0:8], "dc")
            dma(P, "sync", d_dbg["btfc"], s_btfc[:], "dc")
            dma(P, "sync", d_dbg["a2fc"], s_a2fc[:], "dc")
            dma(P, "sync", d_dbg["wfcT0"], d_wfcT[0:128, :], "dc")
            dma(P, "sync", d_dbg["wprT0"], d_wprT[0:128, :], "dc")
        for eng in ENGS:
            P.emit(eng, lambda E: E.nop(), inc="BAR")

        with nc.Block("setup") as b1:
            P.replay(b1, sems)

    s_x = e(nc.sbuf_tensor("xsb", [128, 2 * G * D], F32))
    s_xnb = e(nc.sbuf_tensor("xnb", [128, D], BF16))
    s_xn1 = e(nc.sbuf_tensor("xn1", [128, DC * TILE], BF16))
    s_xn2 = e(nc.sbuf_tensor("xn2", [128, DC * TILE], BF16))
    s_q = e(nc.sbuf_tensor("qsb", [128, 8 * TILE], BF16))
    s_k = e(nc.sbuf_tensor("ksb", [128, 8 * TILE], BF16))
    s_v = e(nc.sbuf_tensor("vsb", [128, G * D], BF16))
    s_kprev = e(nc.sbuf_tensor("kprev", [128, 8 * 128], BF16))
    s_vprev = e(nc.sbuf_tensor("vprev", [128, D], BF16))
    s_ao = e(nc.sbuf_tensor("aosb", [128, 8 * TILE], BF16))
    s_h = e(nc.sbuf_tensor("hsb", [128, HC * TILE], BF16))
    s_sm = e(nc.sbuf_tensor("smbuf", [128, 4 * 256], F32))
    s_pq = e(nc.sbuf_tensor("pqbuf", [128, 7 * 256], BF16))
    s_pT = e(nc.sbuf_tensor("pT", [128, 3 * 256], BF16))
    s_stat = e(nc.sbuf_tensor("stat", [128, 80], F32))
    s_ofm = e(nc.sbuf_tensor("ofm", [128, 512], F32))
    s_wfcb = e(nc.sbuf_tensor("wfcb", [128, 3 * 1024], BF16))
    s_wqb = e(nc.sbuf_tensor("wqb", [128, 2 * 2048], BF16))
    s_wvb = e(nc.sbuf_tensor("wvb", [128, 2 * 2048], BF16))
    s_wprb = e(nc.sbuf_tensor("wprb", [128, 3 * 4096], BF16))
    s_woutb = e(nc.sbuf_tensor("woutb", [128, 3 * 1024], BF16))

    NEGMAX, ROWSUM, RECIP = 0, 52, 64
    SUM, SSQ, MU, MSQ, VARP, R1, V2P, ASC, NMA, T0 = 8, 12, 16, 20, 24, 28, 32, 36, 40, 44

    M = Prog(cnt)
    M.fence_snap = {}
    for eng in ENGS:
        M.emit(eng, lambda E: E.nop(), waits=[("BAR", cnt["BAR"])])

    bank_free = [[] for _ in range(8)]

    def bmm(out_bank, cols, lhsT, rhs, start, stop, tp=None, waits=(), prow=None):
        w = list(waits)
        if start:
            w += bank_free[out_bank]
        ap = pb[out_bank][:, cols[0]:cols[1]] if prow is None else pb[out_bank][prow[0]:prow[1], cols[0]:cols[1]]
        tok = mm(M, ap, lhsT, rhs, start=start, stop=stop, tp=tp, waits=w)
        if start:
            bank_free[out_bank] = []
        return tok

    def btrans(out_bank, cols, in_ap, start=True, stop=True, waits=(), bf16=False):
        w = list(waits) + (bank_free[out_bank] if start else [])
        if bf16:
            out_ap = pb[out_bank][:, cols[0] // 2: cols[1] // 2].bitcast(BF16)
            ident = s_identb[:]
        else:
            out_ap = pb[out_bank][:, cols[0]:cols[1]]
            ident = s_identf[:]
        tok = trans(M, out_ap, in_ap, ident, start=start, stop=stop, waits=w)
        if start:
            bank_free[out_bank] = []
        return tok

    def bread(out_bank, sem, tok):
        bank_free[out_bank] = [(sem, tok)]

    wout_cons = [0, 0, 0]
    wfc_cons = [0, 0, 0]
    wpr_cons = [0, 0, 0]
    wq_cons = [0, 0]
    wv_cons = [0, 0]
    xnb_cons = [0, 0]
    xn1_cons = 0
    xn2_cons = 0
    qk_write_guard = [0, 0]
    v_write_guard = 0
    ao_guard = 0
    h_guard = 0
    ofm_trans_last = [0]
    ofm_cons = [0, 0]
    prev_expC = [0] * 4
    prev_trE = [0] * 7
    prev_H = [("V", 0), ("S", 0)]
    store_tok = [0, 0]
    kp_tok = 0
    vp_tok = 0
    st = {}

    def emit_stats(wait_sem, wait_tok):
        tv = ts = None
        for g in range(G):
            xb = st["xbase"]
            tv = M.emit("vector",
                        lambda E, g=g, xb=xb: E.tensor_scalar(out=s_xn1[:, 0:1024],
                                                              in0=s_x[:, xb + g * D: xb + (g + 1) * D],
                                                              scalar1=1.0, scalar2=None,
                                                              op0=ALU.mult, op1=ALU.add,
                                                              accum_out=s_stat[:, SUM + g:SUM + g + 1]),
                        waits=[(wait_sem, wait_tok), ("T", xn1_cons)], inc="V")
            ts = M.emit("scalar",
                        lambda E, g=g, xb=xb: E.activation(out=s_xn1[:, 1024:2048],
                                                           in_=s_x[:, xb + g * D: xb + (g + 1) * D],
                                                           func=AF.Square,
                                                           accum_out=s_stat[:, SSQ + g:SSQ + g + 1]),
                        waits=[(wait_sem, wait_tok), ("T", xn1_cons)], inc="S")
        c = 1.0 / 1024.0
        tv = M.emit("vector", lambda E: E.tensor_scalar_mul(s_stat[:, MU:MU + 4], s_stat[:, SUM:SUM + 4], c),
                    waits=[("V", tv)], inc="V")
        tv = M.emit("vector", lambda E: E.tensor_scalar_mul(s_stat[:, MSQ:MSQ + 4], s_stat[:, SSQ:SSQ + 4], c),
                    waits=[("S", ts)], inc="V")
        tv = M.emit("vector", lambda E: E.tensor_tensor(out=s_stat[:, T0:T0 + 4], in0=s_stat[:, MU:MU + 4],
                                                        in1=s_stat[:, MU:MU + 4], op=ALU.mult), inc="V")
        tv = M.emit("vector", lambda E: E.tensor_tensor(out=s_stat[:, VARP:VARP + 4], in0=s_stat[:, MSQ:MSQ + 4],
                                                        in1=s_stat[:, T0:T0 + 4], op=ALU.subtract), inc="V")
        tv = M.emit("vector", lambda E: E.tensor_scalar_add(s_stat[:, VARP:VARP + 4], s_stat[:, VARP:VARP + 4],
                                                            1e-5), inc="V")
        ts2 = M.emit("scalar", lambda E: E.activation(out=s_stat[:, T0:T0 + 4], in_=s_stat[:, VARP:VARP + 4],
                                                      func=AF.Ln),
                     waits=[("V", tv)], inc="S")
        ts2 = M.emit("scalar", lambda E: E.activation(out=s_stat[:, R1:R1 + 4], in_=s_stat[:, T0:T0 + 4],
                                                      func=AF.Exp, scale=s_cm05[:]), inc="S")
        return tv, ts2

    def emit_norm_transpose(dst, a_col, nma_col, wait_s_tok):
        ev_last = None
        xb = st["xbase"]
        for g in range(G):
            tok_x = M.emit("vector",
                           lambda E, g=g, xb=xb, a_col=a_col, nma_col=nma_col:
                           E.tensor_scalar(out=s_xnb[:, 0:D],
                                           in0=s_x[:, xb + g * D: xb + (g + 1) * D],
                                           scalar1=s_stat[:, a_col + g:a_col + g + 1],
                                           scalar2=s_stat[:, nma_col + g:nma_col + g + 1],
                                           op0=ALU.mult, op1=ALU.add),
                           waits=[("T", xnb_cons[0]), ("S", wait_s_tok)], inc="V")
            tt = None
            for half in range(2):
                bank = half
                for i4 in range(4):
                    dc = half * 4 + i4
                    tt = btrans(bank, (i4 * 128, (i4 + 1) * 128),
                                s_xnb[:, dc * 128:(dc + 1) * 128],
                                start=True, stop=True,
                                waits=[("V", tok_x)], bf16=True)
                dst_ap = bass.AP(dst, half * 4 * TILE + g * 128,
                                 [[DC * TILE, 128], [TILE, 4], [1, 128]])
                tok = M.emit("scalar",
                             lambda E, bank=bank, dst_ap=dst_ap:
                             E.copy(out=dst_ap, in_=pb[bank][:, 0:256].bitcast(BF16)
                                    .rearrange("p (a b) -> p a b", b=128)),
                             waits=[("T", tt)], inc="S")
                bread(bank, "S", tok)
                ev_last = tok
            xnb_cons[0] = tt
        return ev_last

    def emit_qkv(t, xn1_ev):
        qk_ev = []
        for op in range(8):
            sl = op % 2
            tok_w = M.emit("gpsimd",
                           lambda E, op=op, sl=sl: E.dma_start(
                               out=s_wqb[:, sl * 2048:(sl + 1) * 2048].rearrange("p (a d) -> p a d", d=1024),
                               in_=d_wqs[op * 256:(op + 1) * 256, :].rearrange("(a p) d -> p a d", p=128)),
                           waits=[("T", wq_cons[sl])], inc="wq%d" % sl, n=16)
            for j in range(2):
                oc = 2 * op + j
                bank = 2 + (oc % 2)
                tt = None
                for dc in range(DC):
                    tt = bmm(bank, (0, 512),
                             s_wqb[:, sl * 2048 + j * 1024 + dc * 128: sl * 2048 + j * 1024 + (dc + 1) * 128],
                             s_xn1[:, dc * TILE:(dc + 1) * TILE],
                             start=(dc == 0), stop=(dc == DC - 1),
                             waits=[("wq%d" % sl, tok_w), ("S", xn1_ev),
                                    ("T", qk_write_guard[0]), ("V", kp_tok)])
                dst = s_q if oc < 8 else s_k
                fc = oc % 8
                tok = M.emit("scalar",
                             lambda E, dst=dst, fc=fc, oc=oc, bank=bank:
                             E.activation(out=dst[:, fc * TILE:(fc + 1) * TILE], in_=pb[bank][:],
                                          func=AF.Identity, bias=s_bw[:, oc:oc + 1]),
                             waits=[("T", tt)], inc="S")
                bread(bank, "S", tok)
                qk_ev.append(tok)
            wq_cons[sl] = tt
        v_ev = [0] * G
        ttl = None
        for dp in range(4):
            sl = dp % 2
            tok_w = M.emit("gpsimd",
                           lambda E, dp=dp, sl=sl: E.dma_start(
                               out=s_wvb[:, sl * 2048:(sl + 1) * 2048].rearrange("p (a d) -> p a d", d=1024),
                               in_=d_wvs[dp * 256:(dp + 1) * 256, :].rearrange("(a p) d -> p a d", p=128)),
                           waits=[("T", wv_cons[sl])], inc="wv%d" % sl, n=16)
            for j in range(2):
                dc = 2 * dp + j
                for g in range(G):
                    for half in range(2):
                        bi = g * 2 + half
                        ttl = bmm(bi, (0, 512),
                                  s_xn1[:, dc * TILE + g * 128: dc * TILE + (g + 1) * 128],
                                  s_wvb[:, sl * 2048 + j * 1024 + half * 512: sl * 2048 + j * 1024 + (half + 1) * 512],
                                  start=(dc == 0), stop=False,
                                  waits=[("wv%d" % sl, tok_w), ("S", xn1_ev),
                                         ("T", v_write_guard), ("V", vp_tok)])
            wv_cons[sl] = ttl
        for g in range(G):
            for half in range(2):
                bi = g * 2 + half
                tt = bmm(bi, (0, 512), s_onesb[:], s_bwv[0:1, half * 512:(half + 1) * 512],
                         start=False, stop=True)
                tok = M.emit("vector",
                             lambda E, g=g, half=half, bi=bi:
                             E.tensor_copy(out=s_v[:, g * D + half * 512: g * D + (half + 1) * 512],
                                           in_=pb[bi][:]),
                             waits=[("T", tt)], inc="V")
                bread(bi, "V", tok)
                v_ev[g] = tok
        return qk_ev, v_ev

    def make_attention_steps(t, qk_ev, v_ev):
        seq_start = (t % SEQ_TILES == 0)
        NI = G * H
        ctx = []
        for w in range(G):
            fw = (seq_start and w == 0)
            for h in range(H):
                ctx.append((w, h, fw))
        tokA = [0] * NI; tokB = [0] * NI; tokC = [0] * NI; tokD = [0] * NI
        tokE = [0] * NI; tokF = [0] * NI; tokG = [0] * NI; tokH = [0] * NI
        engH = [0] * NI; engF = [0] * NI
        res = {"tokG": tokG, "tokH": tokH, "engH": engH, "tokA": tokA, "tokB": tokB,
               "tokC": tokC, "tokE": tokE}

        def stA(i):
            w, h, fw = ctx[i]
            po = 64 * (h % 2); fc = h // 2
            qs = s_q[po:po + 64, fc * TILE + w * 128: fc * TILE + (w + 1) * 128]
            bank = i % 2
            wts = [("S", qk_ev[8 + fc]), ("V", tokB[i - 2] if i >= 2 else 0)]
            if fw:
                tokA[i] = bmm(bank, (0, 128), qs, s_k[po:po + 64, fc * TILE: fc * TILE + 128],
                              start=True, stop=True, tp=(po, 0), waits=wts)
            elif w == 0:
                bmm(bank, (0, 128), qs, s_kprev[po:po + 64, fc * 128:(fc + 1) * 128],
                    start=True, stop=True, tp=(po, 0), waits=wts)
                tokA[i] = bmm(bank, (128, 256), qs, s_k[po:po + 64, fc * TILE: fc * TILE + 128],
                              start=True, stop=True, tp=(po, 0))
            else:
                tokA[i] = bmm(bank, (0, 256), qs,
                              s_k[po:po + 64, fc * TILE + (w - 1) * 128: fc * TILE + (w + 1) * 128],
                              start=True, stop=True, tp=(po, 0), waits=wts)

        def stB(i):
            w, h, fw = ctx[i]
            nk = 128 if fw else 256
            ws = i % 4
            bank = i % 2
            mof = 128 if fw else 0
            tok = M.emit("vector",
                         lambda E, ws=ws, nk=nk, mof=mof, bank=bank:
                         E.tensor_tensor(out=s_sm[:, ws * 256: ws * 256 + nk], in0=pb[bank][:, 0:nk],
                                         in1=s_mask[:, mof:mof + nk], op=ALU.add),
                         waits=[("T", tokA[i]),
                                ("S", tokC[i - 4] if i >= 4 else prev_expC[i % 4])], inc="V")
            bread(bank, "V", tok)
            tokB[i] = M.emit("vector",
                             lambda E, ws=ws, nk=nk: E.reduce_max(out=s_stat[:, NEGMAX + ws:NEGMAX + ws + 1],
                                                                  in_=s_sm[:, ws * 256: ws * 256 + nk],
                                                                  axis=AX.X, negate=True),
                             inc="V")

        def stC(i):
            w, h, fw = ctx[i]
            nk = 128 if fw else 256
            ws = i % 4
            tokC[i] = M.emit("scalar",
                             lambda E, ws=ws, nk=nk: E.activation(out=s_sm[:, ws * 256: ws * 256 + nk],
                                                                  in_=s_sm[:, ws * 256: ws * 256 + nk],
                                                                  func=AF.Exp,
                                                                  bias=s_stat[:, NEGMAX + ws:NEGMAX + ws + 1],
                                                                  accum_out=s_stat[:, ROWSUM + ws:ROWSUM + ws + 1]),
                             waits=[("V", tokB[i])], inc="S")

        def stD(i):
            w, h, fw = ctx[i]
            nk = 128 if fw else 256
            ws = i % 4
            wh = i % 7
            M.emit("vector",
                   lambda E, ws=ws: E.reciprocal(out=s_stat[:, RECIP + ws:RECIP + ws + 1],
                                                 in_=s_stat[:, ROWSUM + ws:ROWSUM + ws + 1]),
                   waits=[("S", tokC[i])], inc="V")
            tokD[i] = M.emit("vector",
                             lambda E, ws=ws, wh=wh, nk=nk:
                             E.tensor_scalar_mul(s_pq[:, wh * 256: wh * 256 + nk],
                                                 s_sm[:, ws * 256: ws * 256 + nk],
                                                 s_stat[:, RECIP + ws:RECIP + ws + 1]),
                             waits=[("T", tokE[i - 7] if i >= 7 else prev_trE[i])], inc="V")

        def stE(i):
            w, h, fw = ctx[i]
            nk = 128 if fw else 256
            wh = i % 7
            tt2 = None
            tb = 2 + i % 2
            for kc in range(nk // 128):
                tt2 = btrans(tb, (kc * 128, (kc + 1) * 128),
                             s_pq[:, wh * 256 + kc * 128: wh * 256 + (kc + 1) * 128],
                             start=True, stop=True,
                             waits=[("V", tokD[i])], bf16=True)
            tokE[i] = tt2

        def stF(i):
            w, h, fw = ctx[i]
            nk = 128 if fw else 256
            wh = i % 7
            pslot = i % 3
            tb = 2 + i % 2
            if i % 2 == 0:
                tokF[i] = M.emit("vector",
                                 lambda E, pslot=pslot, nk=nk, tb=tb:
                                 E.tensor_copy(out=s_pT[:, pslot * 256: pslot * 256 + nk],
                                               in_=pb[tb][:, 0:nk // 2].bitcast(BF16)),
                                 waits=[("T", tokE[i])], inc="V")
                engF[i] = 0
            else:
                tokF[i] = M.emit("scalar",
                                 lambda E, pslot=pslot, nk=nk, tb=tb:
                                 E.copy(out=s_pT[:, pslot * 256: pslot * 256 + nk],
                                        in_=pb[tb][:, 0:nk // 2].bitcast(BF16)),
                                 waits=[("T", tokE[i])], inc="S")
                engF[i] = 1
            bread(tb, ("V", "S")[engF[i]], tokF[i])

        def stG(i):
            w, h, fw = ctx[i]
            wh = i % 3
            po = 64 * (h % 2)
            gb = 4 + i % 2
            pvw = [(("V", "S")[engF[i]], tokF[i]), ("V", v_ev[w])]
            if fw:
                tokG[i] = bmm(gb, (0, 128), s_v[:, w * D + h * DH: w * D + (h + 1) * DH],
                              s_pT[:, wh * 256: wh * 256 + 128],
                              start=True, stop=True, tp=(0, po), waits=pvw, prow=(po, po + 64))
            else:
                vprev_ap = (s_vprev[:, h * DH:(h + 1) * DH] if w == 0
                            else s_v[:, (w - 1) * D + h * DH: (w - 1) * D + h * DH + DH])
                bmm(gb, (0, 128), vprev_ap, s_pT[:, wh * 256: wh * 256 + 128],
                    start=True, stop=False, tp=(0, po), waits=pvw, prow=(po, po + 64))
                tokG[i] = bmm(gb, (0, 128), s_v[:, w * D + h * DH: w * D + (h + 1) * DH],
                              s_pT[:, wh * 256 + 128: wh * 256 + 256],
                              start=False, stop=True, tp=(0, po), prow=(po, po + 64))

        def stH(i):
            w, h, fw = ctx[i]
            po = 64 * (h % 2); fc = h // 2
            gb = 4 + i % 2
            if i % 2 == 0:
                tokH[i] = M.emit("vector",
                                 lambda E, po=po, fc=fc, w=w, gb=gb:
                                 E.tensor_copy(out=s_ao[po:po + 64, fc * TILE + w * 128: fc * TILE + (w + 1) * 128],
                                               in_=pb[gb][po:po + 64, 0:128]),
                                 waits=[("T", tokG[i]), ("T", ao_guard)], inc="V")
                engH[i] = 0
                bread(gb, "V", tokH[i])
            else:
                tokH[i] = M.emit("scalar",
                                 lambda E, po=po, fc=fc, w=w, gb=gb:
                                 E.copy(out=s_ao[po:po + 64, fc * TILE + w * 128: fc * TILE + (w + 1) * 128],
                                        in_=pb[gb][po:po + 64, 0:128]),
                                 waits=[("T", tokG[i]), ("T", ao_guard)], inc="S")
                engH[i] = 1
                bread(gb, "S", tokH[i])

        stages = (stA, stB, stC, stD, stE, stF, stG, stH)
        offs = (0, 2, 4, 5, 7, 8, 10, 11)
        steps = []
        for j in range(NI + offs[-1]):
            def step(j=j):
                for k in range(len(stages) - 1, -1, -1):
                    i = j - offs[k]
                    if 0 <= i < NI:
                        stages[k](i)
            steps.append(step)
        return steps, res

    def emit_ffn_groups(tp):
        groups = []
        xb = (tp % 2) * G * D
        h_toks = st.setdefault("h_toks", {})
        h_toks.clear()

        def mk_fc(c):
            def f():
                sl = c % 3
                tok_w = M.emit("gpsimd",
                               lambda E, c=c, sl=sl: E.dma_start(out=s_wfcb[:, sl * 1024:(sl + 1) * 1024],
                                                                 in_=d_wfcT[c * 128:(c + 1) * 128, :]),
                               waits=[("T", wfc_cons[sl])], inc="wf%d" % sl, n=16)
                bank = 6 + (c % 2)
                tt = None
                for dc in range(DC):
                    tt = bmm(bank, (0, 512),
                             s_wfcb[:, sl * 1024 + dc * 128: sl * 1024 + (dc + 1) * 128],
                             s_xn2[:, dc * TILE:(dc + 1) * TILE],
                             start=(dc == 0), stop=(dc == DC - 1),
                             waits=[("wf%d" % sl, tok_w), ("S", st["xn2_ev"]),
                                    ("T", h_guard_g[0])])
                wfc_cons[sl] = tt
                tok = M.emit("scalar",
                             lambda E, c=c, bank=bank: E.activation(out=s_h[:, c * TILE:(c + 1) * TILE],
                                                                    in_=pb[bank][:], func=AF.Identity,
                                                                    bias=s_bfc[:, c:c + 1]),
                             waits=[("T", tt)], inc="S")
                bread(bank, "S", tok)
                h_toks[c] = tok
            return f

        def mk_gelu_batch(c0, c1):
            def f():
                tok = None
                for c in range(c0, c1):
                    tok = M.emit("scalar",
                                 lambda E, c=c: E.activation(out=s_h[:, c * TILE:(c + 1) * TILE],
                                                             in_=s_h[:, c * TILE:(c + 1) * TILE],
                                                             func=AF.Gelu),
                                 waits=[("S", h_toks[c1 - 1])] if c == c0 else [], inc="S")
                h_toks[c1 - 1] = tok
            return f

        def mk_proj(do):
            def f():
                sl = do % 3
                tok_w = M.emit("gpsimd",
                               lambda E, do=do, sl=sl: E.dma_start(out=s_wprb[:, sl * 4096:(sl + 1) * 4096],
                                                                   in_=d_wprT[do * 128:(do + 1) * 128, :]),
                               waits=[("T", wpr_cons[sl])], inc="wp%d" % sl, n=16)
                tt = None
                for c in range(HC):
                    tt = bmm(6, (0, 512),
                             s_wprb[:, sl * 4096 + c * 128: sl * 4096 + (c + 1) * 128],
                             s_h[:, c * TILE:(c + 1) * TILE],
                             start=(c == 0), stop=(c == HC - 1),
                             waits=[("wp%d" % sl, tok_w), ("S", h_toks[HC - 1])])
                wpr_cons[sl] = tt
                st["pass2_last"] = tt
                tok_ev = M.emit("scalar",
                                lambda E, do=do: E.activation(out=s_ofm[:, 0:512], in_=pb[6][:],
                                                              func=AF.Identity, bias=s_bpr[:, do:do + 1]),
                                waits=[("T", tt), ("T", ofm_trans_last[0])], inc="S")
                bread(6, "S", tok_ev)
                tt2 = None
                for g in range(G):
                    tt2 = btrans(7, (g * 128, (g + 1) * 128),
                                 s_ofm[:, g * 128:(g + 1) * 128],
                                 start=True, stop=True, waits=[("S", tok_ev)])
                ofm_trans_last[0] = tt2
                tok_res = M.emit("vector",
                                 lambda E, do=do, xb=xb:
                                 E.scalar_tensor_tensor(
                                     out=bass.AP(s_x, xb + do * 128, [[2 * G * D, 128], [D, 4], [1, 128]]),
                                     in0=pb[7][:].rearrange("p (g c) -> p g c", c=128), scalar=1.0,
                                     in1=bass.AP(s_x, xb + do * 128, [[2 * G * D, 128], [D, 4], [1, 128]]),
                                     op0=ALU.mult, op1=ALU.add),
                                 waits=[("T", tt2)], inc="V")
                bread(7, "V", tok_res)
                st["last_res"] = tok_res
            return f

        for c in range(HC // 2):
            groups.append(mk_fc(c))
        groups.append(mk_gelu_batch(0, HC // 2))
        for c in range(HC // 2, HC):
            groups.append(mk_fc(c))
        groups.append(mk_gelu_batch(HC // 2, HC))
        for do in range(DC):
            groups.append(mk_proj(do))

        def mk_store():
            def f():
                par = tp % 2
                store_tok[par] = M.emit("sync",
                                        lambda E, tp=tp, xb=xb: E.dma_start(
                                            out=d_y[tp * TILE:(tp + 1) * TILE, :].rearrange("(g p) d -> p g d", p=128),
                                            in_=s_x[:, xb:xb + G * D].rearrange("p (g d) -> p g d", d=D)),
                                        waits=[("V", st["last_res"])], inc="dy%d" % par, n=16)
            return f
        groups.append(mk_store())
        return groups

    h_guard_g = [0]

    def emit_outproj(t, pv_last_waits):
        tt = None
        for ic in range(DC):
            sl = ic % 3
            tok_w = M.emit("gpsimd",
                           lambda E, ic=ic, sl=sl: E.dma_start(out=s_woutb[:, sl * 1024:(sl + 1) * 1024],
                                                               in_=d_woutb[ic * 128:(ic + 1) * 128, :]),
                           waits=[("T", wout_cons[sl])], inc="wo%d" % sl, n=16)
            for g in range(G):
                for half in range(2):
                    tt = bmm(2 * g + half, (0, 512),
                             s_ao[:, ic * TILE + g * 128: ic * TILE + (g + 1) * 128],
                             s_woutb[:, sl * 1024 + half * 512: sl * 1024 + (half + 1) * 512],
                             start=(ic == 0), stop=(ic == DC - 1),
                             waits=[("wo%d" % sl, tok_w)] + (pv_last_waits if ic == 0 else []))
            wout_cons[sl] = tt
        tok_T = M.emit("tensor", lambda E: E.nop(), inc="T")
        tvlast = None
        xb = st["xbase"]
        for g in range(G):
            for half in range(2):
                bi = 2 * g + half
                tvlast = M.emit("vector",
                                lambda E, g=g, half=half, bi=bi, xb=xb:
                                E.scalar_tensor_tensor(
                                    out=s_x[:, xb + g * D + half * 512: xb + g * D + (half + 1) * 512],
                                    in0=pb[bi][:], scalar=1.0,
                                    in1=s_x[:, xb + g * D + half * 512: xb + g * D + (half + 1) * 512],
                                    op0=ALU.mult, op1=ALU.add),
                                waits=[("T", tok_T)], inc="V")
                bread(bi, "V", tvlast)
        return tok_T, tvlast

    pending_ffn = None
    for t in range(NT):
        st["xbase"] = (t % 2) * G * D
        par = t % 2
        tok_x = M.emit("sync",
                       lambda E, t=t, xb=st["xbase"]: E.dma_start(
                           out=s_x[:, xb:xb + G * D].rearrange("p (g d) -> p g d", d=D),
                           in_=d_x[t * TILE:(t + 1) * TILE, :].rearrange("(g p) d -> p g d", p=128)),
                       waits=[("dy%d" % par, store_tok[par])], inc="dx", n=16)

        tv, ts = emit_stats("dx", tok_x)
        T1 = 48
        tv = M.emit("vector", lambda E: E.reciprocal(out=s_stat[:, T1:T1 + 4], in_=s_stat[:, VARP:VARP + 4]),
                    inc="V")
        tv = M.emit("vector", lambda E: E.tensor_scalar_add(s_stat[:, V2P:V2P + 4], s_stat[:, VARP:VARP + 4],
                                                            -1e-5), inc="V")
        tv = M.emit("vector", lambda E: E.tensor_tensor(out=s_stat[:, V2P:V2P + 4], in0=s_stat[:, V2P:V2P + 4],
                                                        in1=s_stat[:, T1:T1 + 4], op=ALU.mult), inc="V")
        tv = M.emit("vector", lambda E: E.tensor_scalar_add(s_stat[:, V2P:V2P + 4], s_stat[:, V2P:V2P + 4],
                                                            1e-5), inc="V")
        ts = M.emit("scalar", lambda E: E.activation(out=s_stat[:, T1:T1 + 4], in_=s_stat[:, V2P:V2P + 4],
                                                     func=AF.Ln),
                    waits=[("V", tv), ("S", ts)], inc="S")
        ts = M.emit("scalar", lambda E: E.activation(out=s_stat[:, V2P:V2P + 4], in_=s_stat[:, T1:T1 + 4],
                                                     func=AF.Exp, scale=s_cm05[:]), inc="S")
        tv = M.emit("vector", lambda E: E.tensor_tensor(out=s_stat[:, ASC:ASC + 4], in0=s_stat[:, R1:R1 + 4],
                                                        in1=s_stat[:, V2P:V2P + 4], op=ALU.mult),
                    waits=[("S", ts)], inc="V")
        tv = M.emit("vector", lambda E: E.scalar_tensor_tensor(out=s_stat[:, NMA:NMA + 4],
                                                               in0=s_stat[:, MU:MU + 4],
                                                               scalar=-1.0, in1=s_stat[:, ASC:ASC + 4],
                                                               op0=ALU.mult, op1=ALU.mult), inc="V")
        xn1_ev = emit_norm_transpose(s_xn1, ASC, NMA, 0)

        qk_ev, v_ev = emit_qkv(t, xn1_ev)

        att_steps, att = make_attention_steps(t, qk_ev, v_ev)
        if pending_ffn is not None:
            ng, ns = len(pending_ffn), len(att_steps)
            si = 0
            lead = min(ns, 10)
            while si < lead:
                att_steps[si]()
                si += 1
            for gi, grp in enumerate(pending_ffn):
                grp()
                want = min(ns, lead + (gi + 1) * (ns - lead) // ng)
                while si < want:
                    att_steps[si]()
                    si += 1
            while si < ns:
                att_steps[si]()
                si += 1
        else:
            for stp in att_steps:
                stp()
        h_guard_g[0] = st.get("pass2_last", 0)

        NI = G * H
        qk_write_guard[0] = att["tokA"][NI - 1]
        v_write_guard = att["tokG"][NI - 1]
        for k in range(7):
            prev_trE[(NI - 7 + k) % 7] = att["tokE"][NI - 7 + k]
        for k in range(4):
            prev_expC[(NI - 4 + k) % 4] = att["tokC"][NI - 4 + k]

        if (t + 1) % SEQ_TILES != 0:
            kp_tok = M.emit("vector",
                            lambda E: E.tensor_copy(
                                out=s_kprev[:].rearrange("p (f c) -> p f c", c=128),
                                in_=s_k[:].rearrange("p (f t) -> p f t", t=TILE)[:, :, TILE - 128:TILE]),
                            waits=[("T", att["tokG"][15])], inc="V")
            vp_tok = M.emit("vector",
                            lambda E: E.tensor_copy(out=s_vprev[:], in_=s_v[:, (G - 1) * D: G * D]),
                            waits=[("T", att["tokG"][15])], inc="V")

        li = NI - 1
        pvw = [(("V", "S")[att["engH"][li]], att["tokH"][li]),
               (("V", "S")[att["engH"][li - 1]], att["tokH"][li - 1])]
        opj_T, tvlast = emit_outproj(t, pvw)
        ao_guard = opj_T

        tv2, ts2 = emit_stats("V", tvlast)
        tsn = None
        for g in range(G):
            tsn = M.emit("vector",
                         lambda E, g=g: E.scalar_tensor_tensor(out=s_stat[:, NMA + g:NMA + g + 1],
                                                               in0=s_stat[:, MU + g:MU + g + 1], scalar=-1.0,
                                                               in1=s_stat[:, R1 + g:R1 + g + 1],
                                                               op0=ALU.mult, op1=ALU.mult),
                         waits=[("S", ts2)], inc="V")
        st["xn2_ev"] = emit_norm_transpose(s_xn2, R1, NMA, 0)
        xn2_cons = 0
        pending_ffn = emit_ffn_groups(t)

    for grp in pending_ffn:
        grp()

    M.emit("sync", lambda E: E.nop(), waits=[("dy0", cnt.get("dy0", 0)), ("dy1", cnt.get("dy1", 0))])

    with nc.Block("main") as b2:
        M.replay(b2, sems)

    stack.close()
    return nc


_NC_CACHE = None


def kernel(L0_attn_x, L0_attn_ln_gamma, L0_attn_ln_beta, L0_attn_w_qkv, L0_attn_w_out,
           L0_ffn_W_fc, L0_ffn_b_fc, L0_ffn_A_fc, L0_ffn_B_fc,
           L0_ffn_W_proj, L0_ffn_b_proj, L0_ffn_A_proj, L0_ffn_B_proj):
    global _NC_CACHE
    x = np.ascontiguousarray(np.asarray(L0_attn_x, dtype=np.float32))
    B, N, Dm = x.shape
    ncores = 8
    per = B // ncores

    if _NC_CACHE is None:
        _NC_CACHE = build_program(dbg=DBG)
    nc = _NC_CACHE

    ident_f = np.eye(128, dtype=np.float32)
    ii = np.arange(128)[:, None]
    jj = np.arange(256)[None, :]
    prev = jj < 128
    jjm = np.where(prev, jj, jj - 128)
    bad = np.where(prev, jjm < ii, jjm > ii)
    mask_m = np.where(bad, np.float32(NEG), np.float32(0.0)).astype(BFNP)
    ones_f = np.ones((1, 128), dtype=np.float32)

    common = dict(
        gamma=np.asarray(L0_attn_ln_gamma, np.float32),
        beta=np.asarray(L0_attn_ln_beta, np.float32),
        wqkv=np.asarray(L0_attn_w_qkv, np.float32),
        wout=np.asarray(L0_attn_w_out, np.float32),
        wfc=np.asarray(L0_ffn_W_fc, np.float32),
        bfc=np.asarray(L0_ffn_b_fc, np.float32),
        afc=np.asarray(L0_ffn_A_fc, np.float32),
        bfcB=np.asarray(L0_ffn_B_fc, np.float32),
        wpr=np.asarray(L0_ffn_W_proj, np.float32),
        bpr=np.asarray(L0_ffn_b_proj, np.float32),
        apr=np.asarray(L0_ffn_A_proj, np.float32),
        bprB=np.asarray(L0_ffn_B_proj, np.float32),
        identf=ident_f, maskm=mask_m, onesf=ones_f,
        onesb=np.ones((1, 128), dtype=BFNP),
    )
    in_maps = []
    for c in range(ncores):
        m = dict(common)
        m["x"] = np.ascontiguousarray(x[c * per:(c + 1) * per].reshape(TOK, Dm))
        in_maps.append(m)

    res = run_bass_kernel_spmd(nc, in_maps, core_ids=list(range(ncores)))
    out = np.empty((B, N, Dm), np.float32)
    for c in range(ncores):
        out[c * per:(c + 1) * per] = res.results[c]["y"].reshape(per, N, Dm)
    if DBG:
        global DBG_RES
        DBG_RES = res.results
    return out


# version 24 (final)
# speedup vs baseline: 1.2058x; 1.0478x over previous
import numpy as np
import ml_dtypes
from contextlib import ExitStack

import concourse.bass as bass
import concourse.mybir as mybir
from concourse.bass_utils import run_bass_kernel_spmd

F32 = mybir.dt.float32
BF16 = mybir.dt.bfloat16
I32 = mybir.dt.int32
AF = mybir.ActivationFunctionType
ALU = mybir.AluOpType
AX = mybir.AxisListType
BFNP = ml_dtypes.bfloat16

D = 1024
TOK = 4096
TILE = 512
NT = TOK // TILE
G = TILE // 128
DC = D // 128
H = 16
DH = 64
FFH = 4096
HC = FFH // 128
NEG = -1.0e30
SEQ_TILES = 2048 // TILE

ENGS = ("sync", "tensor", "vector", "scalar", "gpsimd")


class Prog:

    def __init__(self, cnt):
        self.ops = {e: [] for e in ENGS}
        self.cnt = cnt
        self.last_wait = {e: {} for e in ENGS}
        self.fence_snap = {}

    def emit(self, eng, fn, waits=(), inc=None, n=1):
        own = _OWN.get(eng)
        merged = dict(self.fence_snap)
        if own in ("V", "S"):
            if inc is None:
                inc = own
            merged[own] = self.cnt.get(own, 0)
        for s, v in waits:
            if v > merged.get(s, 0):
                merged[s] = v
        w = []
        for s, v in merged.items():
            if s == "T" and eng == "tensor":
                continue
            if v > self.last_wait[eng].get(s, 0):
                self.last_wait[eng][s] = v
                w.append((s, v))
        self.ops[eng].append((fn, w, inc, n))
        if inc is not None:
            self.cnt[inc] = self.cnt.get(inc, 0) + n
            return self.cnt[inc]
        return None

    def fence(self):
        self.fence_snap = {s: self.cnt.get(s, 0) for s in ("T", "V", "S")}

    def replay(self, block, sems):
        for ename in ENGS:
            ops = self.ops[ename]

            def body(eng, ops=ops):
                for fn, w, inc, n in ops:
                    for s, v in w:
                        eng.wait_ge(sems[s], v)
                    instr = fn(eng)
                    if inc is not None:
                        instr.then_inc(sems[inc], n)
            getattr(block, ename)(body)


_OWN = {"tensor": "T", "vector": "V", "scalar": "S"}


DBG = False


def build_program(dbg=False):
    nc = bass.Bass("TRN2", target_bir_lowering=False, detect_race_conditions=False)

    d_x = nc.dram_tensor("x", [TOK, D], F32, kind="ExternalInput").ap()
    d_gamma = nc.dram_tensor("gamma", [D], F32, kind="ExternalInput").ap()
    d_beta = nc.dram_tensor("beta", [D], F32, kind="ExternalInput").ap()
    d_wqkv = nc.dram_tensor("wqkv", [D, 3 * D], F32, kind="ExternalInput").ap()
    d_wout = nc.dram_tensor("wout", [D, D], F32, kind="ExternalInput").ap()
    d_wfc = nc.dram_tensor("wfc", [FFH, D], F32, kind="ExternalInput").ap()
    d_bfc = nc.dram_tensor("bfc", [FFH], F32, kind="ExternalInput").ap()
    d_afc = nc.dram_tensor("afc", [16, D], F32, kind="ExternalInput").ap()
    d_bfcB = nc.dram_tensor("bfcB", [FFH, 16], F32, kind="ExternalInput").ap()
    d_wpr = nc.dram_tensor("wpr", [D, FFH], F32, kind="ExternalInput").ap()
    d_bpr = nc.dram_tensor("bpr", [D], F32, kind="ExternalInput").ap()
    d_apr = nc.dram_tensor("apr", [16, FFH], F32, kind="ExternalInput").ap()
    d_bprB = nc.dram_tensor("bprB", [D, 16], F32, kind="ExternalInput").ap()
    d_identf = nc.dram_tensor("identf", [128, 128], F32, kind="ExternalInput").ap()
    d_mask = nc.dram_tensor("maskm", [128, 256], BF16, kind="ExternalInput").ap()
    d_onesf = nc.dram_tensor("onesf", [1, 128], F32, kind="ExternalInput").ap()
    d_onesb = nc.dram_tensor("onesb", [1, 128], BF16, kind="ExternalInput").ap()

    d_y = nc.dram_tensor("y", [TOK, D], F32, kind="ExternalOutput").ap()
    d_dbg = {}
    if dbg:
        for name, shape, dt in [("wall", [128, 24576], BF16), ("bw", [128, 16], F32),
                                ("bwv", [1, 1024], F32), ("scl", [128, 8], F32),
                                ("btfc", [16, FFH], BF16), ("a2fc", [16, D], BF16),
                                ("stat", [128, 64], F32), ("xnfm", [128, 4096], BF16),
                                ("qd", [128, 4096], BF16), ("kd", [128, 4096], BF16),
                                ("vd", [128, 4096], BF16), ("ao", [128, 4096], BF16),
                                ("x1", [128, 4096], F32), ("hd", [128, HC * TILE], BF16),
                                ("wfcT0", [128, 1024], BF16), ("wprT0", [128, 4096], BF16)]:
            d_dbg[name] = nc.dram_tensor("dbg_" + name, shape, dt, kind="ExternalOutput").ap()

    d_wfcT = nc.dram_tensor("wfcT_scr", [FFH, D], BF16, kind="Internal").ap()
    d_wprT = nc.dram_tensor("wprT_scr", [D, FFH], BF16, kind="Internal").ap()
    d_woutb = nc.dram_tensor("wout_scr", [D, D], BF16, kind="Internal").ap()
    d_wqs = nc.dram_tensor("wqs_scr", [16 * 128, D], BF16, kind="Internal").ap()
    d_wvs = nc.dram_tensor("wvs_scr", [DC * 128, D], BF16, kind="Internal").ap()

    cnt = {}
    stack = ExitStack()
    e = stack.enter_context

    sems = {}
    for s in ("dc", "da", "db", "dwa", "dwb", "dx", "dy0", "dy1",
              "wo0", "wo1", "wo2", "wf0", "wf1", "wf2",
              "wp0", "wp1", "wp2",
              "wq0", "wq1", "wv0", "wv1", "T", "V", "S", "BAR"):
        sems[s] = e(nc.semaphore(s))

    pb = [e(nc.psum_tensor(f"pb{i}", [128, 512], F32)) for i in range(8)]

    s_identf = e(nc.sbuf_tensor("sbident_f", [128, 128], F32))
    s_identb = e(nc.sbuf_tensor("sbident_b", [128, 128], BF16))
    s_mask = e(nc.sbuf_tensor("sb_maskm", [128, 256], BF16))
    s_maskf = e(nc.sbuf_tensor("sb_maskf", [128, 256], F32))
    s_cm1 = e(nc.sbuf_tensor("cm1", [128, 1], F32))
    s_onesf = e(nc.sbuf_tensor("sb_onesf", [1, 128], F32))
    s_gcol = e(nc.sbuf_tensor("gcol", [128, 8], F32))
    s_g8col = e(nc.sbuf_tensor("g8col", [128, 8], F32))
    s_betab = e(nc.sbuf_tensor("betab", [128, 8], BF16))
    s_bw = e(nc.sbuf_tensor("bw", [128, 16], F32))
    s_bwv = e(nc.sbuf_tensor("bwv", [1, 1024], BF16))
    s_onesb = e(nc.sbuf_tensor("sb_onesb", [1, 128], BF16))
    s_bfc = e(nc.sbuf_tensor("bfccol", [128, 32], F32))
    s_bpr = e(nc.sbuf_tensor("bprcol", [128, 8], F32))
    s_cm05 = e(nc.sbuf_tensor("cm05", [128, 1], F32))
    s_cp05 = e(nc.sbuf_tensor("cp05", [128, 1], F32))
    s_scfc = e(nc.sbuf_tensor("scfc", [128, 1], F32))
    s_rsfc = e(nc.sbuf_tensor("rsfc", [128, 1], F32))
    s_scpr = e(nc.sbuf_tensor("scpr", [128, 1], F32))
    s_rspr = e(nc.sbuf_tensor("rspr", [128, 1], F32))
    s_am = e(nc.sbuf_tensor("amcols", [128, 32], F32))
    s_hsfc = e(nc.sbuf_tensor("hsfc", [128, 1], F32))
    s_hspr = e(nc.sbuf_tensor("hspr", [128, 1], F32))
    s_amc = e(nc.sbuf_tensor("amc", [128, 1], F32))
    s_amr = e(nc.sbuf_tensor("amr", [1, 128], F32))
    s_sc1 = e(nc.sbuf_tensor("sc1", [1, 1], F32))


    s_a2fc = e(nc.sbuf_tensor("a2fc", [16, D], BF16))
    s_btfc = e(nc.sbuf_tensor("btfc", [16, FFH], BF16))
    s_a2pr = e(nc.sbuf_tensor("a2pr", [16, FFH], BF16))
    s_btpr = e(nc.sbuf_tensor("btpr", [16, D], BF16))

    P = Prog(cnt)

    def dma(prog, eng, out_ap, in_ap, sem, slow=False):
        if slow:
            return prog.emit(eng, lambda E, o=out_ap, i=in_ap: E.dma_start(out=o, in_=i, allow_slow_non_contiguous=True), inc=sem, n=16)
        return prog.emit(eng, lambda E, o=out_ap, i=in_ap: E.dma_start(out=o, in_=i), inc=sem, n=16)

    def mm(prog, out_ap, lhsT, rhs, start, stop, tp=None, waits=()):
        def f(E, o=out_ap, l=lhsT, r=rhs):
            return E.matmul(o, l, r, start=start, stop=stop, tile_position=tp,
                            skip_group_check=True)
        return prog.emit("tensor", f, waits=waits, inc="T")

    def trans(prog, out_ap, in_ap, ident, start=True, stop=True, waits=()):
        def f(E, o=out_ap, i=in_ap, idn=ident):
            return E.matmul(o, i, idn, start=start, stop=stop, is_transpose=True,
                            skip_group_check=True)
        return prog.emit("tensor", f, waits=waits, inc="T")

    with ExitStack() as arena:
        a = arena.enter_context
        t_stage = a(nc.sbuf_tensor("stage", [128, 2 * FFH], F32))
        t_q = a(nc.sbuf_tensor("tq", [128, 2048], F32))
        t_k0 = a(nc.sbuf_tensor("tk0", [128, 2048], I32))
        t_d = a(nc.sbuf_tensor("td", [128, 2048], F32))
        t_s1 = a(nc.sbuf_tensor("ts1", [128, 2048], F32))
        t_s2 = a(nc.sbuf_tensor("ts2", [128, 2048], F32))
        t_wqf = a(nc.sbuf_tensor("twqf", [128, 2 * FFH], F32))
        t_panel = a(nc.sbuf_tensor("tpanel", [128, 2 * FFH], BF16))
        t_small = a(nc.sbuf_tensor("tsmall", [128, 16], F32))

        tok_dx = 0
        tok_dx = dma(P, "sync", s_identf[:], d_identf, "dc")
        tok_dx = dma(P, "sync", s_mask[:], d_mask, "dc")
        tok_dx = dma(P, "sync", s_onesf[:], d_onesf, "dc")
        tok_dx = dma(P, "sync", s_onesb[:], d_onesb, "dc")
        tok_dx = dma(P, "sync", s_gcol[:], d_gamma.rearrange("(c p) -> p c", p=128), "dc", slow=True)
        tok_dx = dma(P, "sync", t_small[:, 0:8], d_beta.rearrange("(c p) -> p c", p=128), "dc", slow=True)
        tok_dx = dma(P, "sync", s_bfc[:], d_bfc.rearrange("(c p) -> p c", p=128), "dc", slow=True)
        tok_dx = dma(P, "sync", s_bpr[:], d_bpr.rearrange("(c p) -> p c", p=128), "dc", slow=True)
        tok_consts = tok_dx
        P.emit("vector", lambda E: E.memset(s_cm05[:], -0.5))
        P.emit("vector", lambda E: E.memset(s_cp05[:], 0.5))
        P.emit("vector", lambda E: E.memset(s_cm1[:], -1.0))
        P.emit("vector", lambda E: E.tensor_copy(out=s_identb[:], in_=s_identf[:]),
               waits=[("dc", tok_consts)])
        P.emit("vector", lambda E: E.tensor_copy(out=s_maskf[:], in_=s_mask[:]),
               waits=[("dc", tok_consts)])
        P.emit("vector", lambda E: E.tensor_scalar_mul(s_g8col[:], s_gcol[:], 0.125),
               waits=[("dc", tok_consts)])
        tv = P.emit("vector", lambda E: E.tensor_copy(out=s_betab[:], in_=t_small[:, 0:8]),
                    waits=[("dc", tok_consts)], inc="V")

        t_bwv32 = a(nc.sbuf_tensor("tbwv32", [1, 1024], F32))
        cast_tk2 = [0, 0]
        pstore_tk = [0, 0]
        bw_read = 0
        bwv_read = 0
        for dc in range(DC):
            sem = "da" if dc % 2 == 0 else "db"
            ssem = "dwa" if dc % 2 == 0 else "dwb"
            slot = (dc % 2) * 4096
            tok_l = P.emit("sync",
                           lambda E, dc=dc, slot=slot:
                           E.dma_start(out=t_stage[:, slot:slot + 3072],
                                       in_=d_wqkv[dc * 128:(dc + 1) * 128, :]),
                           waits=[("S", cast_tk2[dc % 2])], inc=sem, n=16)
            cast = P.emit("scalar",
                          lambda E, slot=slot: E.copy(out=t_panel[:, slot:slot + 3072],
                                                      in_=t_stage[:, slot:slot + 3072]),
                          waits=[(sem, tok_l), (ssem, pstore_tk[dc % 2]), ("dc", tok_consts)], inc="S")
            cast_tk2[dc % 2] = cast
            tT = None
            for oc in range(16):
                tT = mm(P, pb[0][:, oc:oc + 1],
                        t_panel[:, slot + oc * 128: slot + (oc + 1) * 128],
                        s_betab[:, dc:dc + 1], start=True, stop=True,
                        waits=[("S", cast), ("V", bw_read)])
            for half in range(2):
                tT = mm(P, pb[1 + half][0:1, 0:512], s_betab[:, dc:dc + 1],
                        t_panel[:, slot + 2048 + half * 512: slot + 2048 + (half + 1) * 512],
                        start=True, stop=True,
                        waits=[("V", bwv_read)])
            tTn = P.emit("tensor", lambda E: E.nop(), inc="T")
            if dc == 0:
                bw_read = P.emit("vector", lambda E: E.tensor_copy(out=s_bw[:], in_=pb[0][:, 0:16]),
                                 waits=[("T", tTn)], inc="V")
                P.emit("vector", lambda E: E.tensor_copy(out=t_bwv32[0:1, 0:512], in_=pb[1][0:1, 0:512]),
                       inc="V")
                bwv_read = P.emit("vector", lambda E: E.tensor_copy(out=t_bwv32[0:1, 512:1024],
                                                                    in_=pb[2][0:1, 0:512]), inc="V")
            else:
                bw_read = P.emit("vector", lambda E: E.tensor_tensor(out=s_bw[:], in0=s_bw[:],
                                                                     in1=pb[0][:, 0:16], op=ALU.add),
                                 waits=[("T", tTn)], inc="V")
                P.emit("vector", lambda E: E.tensor_tensor(out=t_bwv32[0:1, 0:512], in0=t_bwv32[0:1, 0:512],
                                                           in1=pb[1][0:1, 0:512], op=ALU.add), inc="V")
                bwv_read = P.emit("vector", lambda E: E.tensor_tensor(out=t_bwv32[0:1, 512:1024],
                                                                      in0=t_bwv32[0:1, 512:1024],
                                                                      in1=pb[2][0:1, 0:512], op=ALU.add),
                                  inc="V")
            P.emit("scalar",
                   lambda E, dc=dc, slot=slot: E.activation(out=t_panel[:, slot:slot + 1024],
                                                            in_=t_panel[:, slot:slot + 1024],
                                                            func=AF.Copy, scale=s_g8col[:, dc:dc + 1]),
                   waits=[("T", tTn)])
            fold = P.emit("scalar",
                          lambda E, dc=dc, slot=slot: E.activation(out=t_panel[:, slot + 1024:slot + 3072],
                                                                   in_=t_panel[:, slot + 1024:slot + 3072],
                                                                   func=AF.Copy, scale=s_gcol[:, dc:dc + 1]),
                          inc="S")
            P.emit("gpsimd",
                   lambda E, dc=dc, slot=slot: E.dma_start(
                       out=bass.AP(d_wqs.tensor, dc * 128, [[1024, 128], [128 * 1024, 16], [1, 128]]),
                       in_=bass.AP(t_panel, slot, [[8192, 128], [128, 16], [1, 128]])),
                   waits=[("S", fold)], inc=ssem, n=16)
            pstore_tk[dc % 2] = P.emit("gpsimd",
                                       lambda E, dc=dc, slot=slot:
                                       E.dma_start(out=d_wvs[dc * 128:(dc + 1) * 128, :],
                                                   in_=t_panel[:, slot + 2048:slot + 3072]),
                                       inc=ssem, n=16)
        tok_bw = P.emit("vector", lambda E: E.tensor_scalar_mul(s_bw[:, 0:8], s_bw[:, 0:8], 0.125),
                        inc="V")
        tok_bwv = P.emit("vector", lambda E: E.tensor_copy(out=s_bwv[:], in_=t_bwv32[:]), inc="V")

        P.fence()
        st_toks = list(pstore_tk)
        cast_tk = [0, 0]
        for ic in range(DC):
            sem = "da" if ic % 2 == 0 else "db"
            ssem = "dwa" if ic % 2 == 0 else "dwb"
            tok_l = P.emit("sync",
                           lambda E, ic=ic: E.dma_start(out=t_stage[:, (ic % 2) * 4096:(ic % 2) * 4096 + 1024],
                                                        in_=d_wout[ic * 128:(ic + 1) * 128, :]),
                           waits=[("V", cast_tk[ic % 2])], inc=sem, n=16)
            tok_c = P.emit("vector",
                           lambda E, ic=ic: E.tensor_copy(out=t_panel[:, (ic % 2) * 1024:(ic % 2) * 1024 + 1024],
                                                          in_=t_stage[:, (ic % 2) * 4096:(ic % 2) * 4096 + 1024]),
                           waits=[(sem, tok_l), (ssem, st_toks[ic % 2])], inc="V")
            cast_tk[ic % 2] = tok_c
            st_toks[ic % 2] = P.emit(
                "gpsimd",
                lambda E, ic=ic: E.dma_start(out=d_woutb[ic * 128:(ic + 1) * 128, :],
                                             in_=t_panel[:, (ic % 2) * 1024:(ic % 2) * 1024 + 1024]),
                waits=[("V", tok_c)], inc=ssem, n=16)
        P.fence()

        tok_a = dma(P, "sync", t_stage[:16, 0:1024], d_afc, "da")
        tva = P.emit("vector", lambda E: E.tensor_scalar(out=s_a2fc[:], in0=t_stage[:16, 0:1024],
                                                         scalar1=2.0, scalar2=None, op0=ALU.mult),
                     waits=[("da", tok_a)], inc="V")
        tok_a = P.emit("sync", lambda E: E.dma_start(out=t_stage[:16, 0:4096], in_=d_apr),
                       waits=[("V", tva)], inc="da", n=16)
        tva = P.emit("vector", lambda E: E.tensor_scalar(out=s_a2pr[:], in0=t_stage[:16, 0:4096],
                                                         scalar1=2.0, scalar2=None, op0=ALU.mult),
                     waits=[("da", tok_a)], inc="V")
        P.fence()
        btr = [0, 0]
        bev = [0, 0]
        for c in range(HC):
            sem = "da" if c % 2 == 0 else "db"
            tok_l = P.emit("sync",
                           lambda E, c=c: E.dma_start(out=t_stage[:, (c % 2) * 4096:(c % 2) * 4096 + 16],
                                                      in_=d_bfcB[c * 128:(c + 1) * 128, :]),
                           waits=[("T", btr[c % 2])], inc=sem, n=16)
            tt = trans(P, pb[2 + c % 2][0:16, 0:128],
                       t_stage[:, (c % 2) * 4096:(c % 2) * 4096 + 16], s_identf[:],
                       waits=[(sem, tok_l), ("S", bev[c % 2])])
            btr[c % 2] = tt
            bev[c % 2] = P.emit("scalar",
                                lambda E, c=c: E.copy(out=s_btfc[:16, c * 128:(c + 1) * 128],
                                                      in_=pb[2 + c % 2][0:16, 0:128]),
                                waits=[("T", tt)], inc="S")
        P.fence()
        btr = [0, 0]
        bev = [0, 0]
        for c in range(DC):
            sem = "da" if c % 2 == 0 else "db"
            tok_l = P.emit("sync",
                           lambda E, c=c: E.dma_start(out=t_stage[:, (c % 2) * 4096:(c % 2) * 4096 + 16],
                                                      in_=d_bprB[c * 128:(c + 1) * 128, :]),
                           waits=[("T", btr[c % 2])], inc=sem, n=16)
            tt = trans(P, pb[2 + c % 2][0:16, 0:128],
                       t_stage[:, (c % 2) * 4096:(c % 2) * 4096 + 16], s_identf[:],
                       waits=[(sem, tok_l), ("S", bev[c % 2])])
            btr[c % 2] = tt
            bev[c % 2] = P.emit("scalar",
                                lambda E, c=c: E.copy(out=s_btpr[:16, c * 128:(c + 1) * 128],
                                                      in_=pb[2 + c % 2][0:16, 0:128]),
                                waits=[("T", tt)], inc="S")
        P.fence()

        def quant_weight(d_w, nchunk, free, s_scale, s_rscale, s_hscale, am_ncols, d_scrT, a2, bt,
                         panel_cols, tpos_cols):
            P.fence()
            red_tk = [0, 0]
            for c in range(nchunk):
                sl = (c % 2) * 4096
                sem = "da" if c % 2 == 0 else "db"
                tok_l = P.emit("sync",
                               lambda E, c=c, sl=sl: E.dma_start(out=t_stage[:, sl:sl + free],
                                                                 in_=d_w[c * 128:(c + 1) * 128, :]),
                               waits=[("V", red_tk[c % 2])], inc=sem, n=16)
                red_tk[c % 2] = P.emit("vector",
                                       lambda E, c=c, sl=sl: E.reduce_max(out=s_am[:, c:c + 1],
                                                                          in_=t_stage[:, sl:sl + free],
                                                                          axis=AX.X, apply_absolute_value=True),
                                       waits=[(sem, tok_l)], inc="V")
            P.fence()
            tok = P.emit("vector", lambda E: E.reduce_max(out=s_amc[:], in_=s_am[:, 0:am_ncols], axis=AX.X),
                         inc="V")
            tt = trans(P, pb[3][0:1, 0:128], s_amc[:], s_identf[:], waits=[("V", tok)])
            tok = P.emit("vector", lambda E: E.tensor_copy(out=s_amr[:], in_=pb[3][0:1, 0:128]),
                         waits=[("T", tt)], inc="V")
            tok = P.emit("vector", lambda E: E.reduce_max(out=s_sc1[:], in_=s_amr[:], axis=AX.X), inc="V")
            tok = P.emit("vector", lambda E: E.tensor_scalar_mul(s_sc1[:], s_sc1[:], float(np.float32(1.0) / np.float32(127.0))),
                         inc="V")
            tt = mm(P, pb[3][:, 0:1], s_onesf[:], s_sc1[:], start=True, stop=True,
                    waits=[("V", tok)])
            tok = P.emit("vector", lambda E: E.tensor_copy(out=s_scale[:], in_=pb[3][:, 0:1]),
                         waits=[("T", tt)], inc="V")
            tok = P.emit("vector", lambda E: E.reciprocal(out=s_rscale[:], in_=s_scale[:]), inc="V")
            tok = P.emit("vector", lambda E: E.tensor_scalar_mul(s_hscale[:], s_scale[:], 0.5), inc="V")
            P.fence()
            nsl = free // 1024
            tok_store = 0
            ev_bank = {}
            trans_last = 0
            tq_tk = [0, 0]
            gsl = [0]
            stt_tk = [0, 0]
            pend_tw = [None]
            twc = [0] * nchunk
            trans_c = [0] * nchunk
            ev_bank = {}
            store_tk = [0, 0]

            def flush_tw(prog):
                if pend_tw[0] is not None:
                    pf, pc, ptk, ptl = pend_tw[0]
                    tw = prog.emit("scalar", pf, waits=[("V", ptk), ("T", ptl)], inc="S")
                    twc[pc] = tw
                    pend_tw[0] = None

            def quant_slices(c):
                sl = (c % 2) * 4096
                wsl = (c % 2) * 4096
                sem = "da" if c % 2 == 0 else "db"
                tok_l = P.emit("sync",
                               lambda E, c=c, sl=sl: E.dma_start(out=t_stage[:, sl:sl + free],
                                                                 in_=d_w[c * 128:(c + 1) * 128, :]),
                               waits=[("S", tq_tk[c % 2])], inc=sem, n=16)
                tq = 0
                for s in range(nsl):
                    slot = gsl[0] % 2
                    q = slot * 1024
                    gsl[0] += 1
                    src = t_stage[:, sl + s * 1024: sl + (s + 1) * 1024]
                    tq = P.emit("scalar",
                                lambda E, src=src, q=q: E.activation(out=t_q[:, q:q + 1024], in_=src,
                                                                     func=AF.Copy, scale=s_rscale[:]),
                                waits=[(sem, tok_l), ("V", stt_tk[slot])], inc="S")
                    flush_tw(P)
                    tk = P.emit("vector",
                                lambda E, q=q: E.tensor_copy(out=t_k0[:, q:q + 1024], in_=t_q[:, q:q + 1024]),
                                waits=[("S", tq)], inc="V")
                    tk = P.emit("vector",
                                lambda E, q=q: E.tensor_tensor(out=t_d[:, q:q + 1024], in0=t_q[:, q:q + 1024],
                                                               in1=t_k0[:, q:q + 1024], op=ALU.subtract),
                                inc="V")
                    t1 = P.emit("scalar",
                                lambda E, q=q: E.activation(out=t_s1[:, q:q + 1024], in_=t_d[:, q:q + 1024],
                                                            func=AF.Sign, bias=s_cm05[:]),
                                waits=[("V", tk)], inc="S")
                    t1 = P.emit("scalar",
                                lambda E, q=q: E.activation(out=t_s2[:, q:q + 1024], in_=t_d[:, q:q + 1024],
                                                            func=AF.Sign, bias=s_cp05[:]), inc="S")
                    tk = P.emit("vector",
                                lambda E, q=q: E.scalar_tensor_tensor(out=t_s1[:, q:q + 1024],
                                                                      in0=t_k0[:, q:q + 1024], scalar=2.0,
                                                                      in1=t_s1[:, q:q + 1024],
                                                                      op0=ALU.mult, op1=ALU.add),
                                waits=[("S", t1)], inc="V")
                    tk = P.emit("vector",
                                lambda E, q=q: E.tensor_tensor(out=t_d[:, q:q + 1024], in0=t_s1[:, q:q + 1024],
                                                               in1=t_s2[:, q:q + 1024], op=ALU.add),
                                inc="V")
                    stt_tk[slot] = tk
                    def mk_tw(s=s, q=q, wsl=wsl):
                        return lambda E: E.activation(out=t_wqf[:, wsl + s * 1024: wsl + (s + 1) * 1024],
                                                      in_=t_d[:, q:q + 1024],
                                                      func=AF.Copy, scale=s_hscale[:])
                    pend_tw[0] = (mk_tw(), c, tk, trans_c[c - 2] if c >= 2 else 0)
                tq_tk[c % 2] = tq

            def trans_store(c):
                wsl = (c % 2) * 4096
                psl = (c % 2) * free
                nblk = free // 128
                ev_tok = None
                tt = None
                for b in range(nblk):
                    bi = 4 + ((b // 4) % 2)
                    bank = pb[bi]
                    tt = trans(P, bank[:, (b % 4) * 128:(b % 4) * 128 + 128],
                               t_wqf[:, wsl + b * 128: wsl + (b + 1) * 128], s_identf[:],
                               start=True, stop=False,
                               waits=[("S", twc[c]), ("S", ev_bank.get(bi, 0))])
                    tt = mm(P, bank[:, (b % 4) * 128:(b % 4) * 128 + 128],
                            a2[:16, b * 128:(b + 1) * 128],
                            bt[:16, c * 128:(c + 1) * 128],
                            start=False, stop=True)
                    if b % 4 == 3:
                        ev_tok = P.emit("scalar",
                                        lambda E, b=b, bank=bank, psl=psl:
                                        E.copy(out=t_panel[:, psl + (b - 3) * 128: psl + (b + 1) * 128],
                                               in_=bank[:]),
                                        waits=[("T", tt), ("dwa" if c % 2 == 0 else "dwb", store_tk[c % 2])],
                                        inc="S")
                        ev_bank[bi] = ev_tok
                trans_c[c] = tt
                store_tk[c % 2] = P.emit("gpsimd",
                                         lambda E, c=c, psl=psl:
                                         E.dma_start(out=d_scrT[c * 128:(c + 1) * 128, :],
                                                     in_=t_panel[:, psl:psl + free]),
                                         waits=[("S", ev_tok)],
                                         inc="dwa" if c % 2 == 0 else "dwb", n=16)

            for c in range(nchunk):
                quant_slices(c)
                flush_tw(P)
                if c >= 1:
                    trans_store(c - 1)
            trans_store(nchunk - 1)
            P.fence()

        quant_weight(d_wfc, HC, 1024, s_scfc, s_rsfc, s_hsfc, HC, d_wfcT, s_a2fc, s_btfc, 1024, None)
        quant_weight(d_wpr, DC, 4096, s_scpr, s_rspr, s_hspr, DC, d_wprT, s_a2pr, s_btpr, 4096, None)

        if dbg:
            P.fence()
            P.emit("sync", lambda E: E.nop(),
                   waits=[(k, cnt.get(k, 0)) for k in ("T", "V", "S")])
            pass
            dma(P, "sync", d_dbg["bw"], s_bw[:], "dc")
            dma(P, "sync", d_dbg["bwv"], s_bwv[:], "dc")
            P.emit("vector", lambda E: E.tensor_copy(out=s_am[:, 0:1], in_=s_scfc[:]))
            P.emit("vector", lambda E: E.tensor_copy(out=s_am[:, 1:2], in_=s_rsfc[:]))
            P.emit("vector", lambda E: E.tensor_copy(out=s_am[:, 2:3], in_=s_scpr[:]))
            tvd = P.emit("vector", lambda E: E.tensor_copy(out=s_am[:, 3:4], in_=s_rspr[:]), inc="V")
            dma(P, "sync", d_dbg["scl"], s_am[:, 0:8], "dc")
            dma(P, "sync", d_dbg["btfc"], s_btfc[:], "dc")
            dma(P, "sync", d_dbg["a2fc"], s_a2fc[:], "dc")
            dma(P, "sync", d_dbg["wfcT0"], d_wfcT[0:128, :], "dc")
            dma(P, "sync", d_dbg["wprT0"], d_wprT[0:128, :], "dc")
        for eng in ENGS:
            P.emit(eng, lambda E: E.nop(), inc="BAR")

        with nc.Block("setup") as b1:
            P.replay(b1, sems)

    s_x = e(nc.sbuf_tensor("xsb", [128, 2 * G * D], F32))
    s_xnb = e(nc.sbuf_tensor("xnb", [128, D], BF16))
    s_xn1 = e(nc.sbuf_tensor("xn1", [128, DC * TILE], BF16))
    s_xn2 = e(nc.sbuf_tensor("xn2", [128, DC * TILE], BF16))
    s_q = e(nc.sbuf_tensor("qsb", [128, 8 * TILE], BF16))
    s_k = e(nc.sbuf_tensor("ksb", [128, 8 * TILE], BF16))
    s_v = e(nc.sbuf_tensor("vsb", [128, G * D], BF16))
    s_kprev = e(nc.sbuf_tensor("kprev", [128, 8 * 128], BF16))
    s_vprev = e(nc.sbuf_tensor("vprev", [128, D], BF16))
    s_ao = e(nc.sbuf_tensor("aosb", [128, 8 * TILE], BF16))
    s_h = e(nc.sbuf_tensor("hsb", [128, HC * TILE], BF16))
    s_sm = e(nc.sbuf_tensor("smbuf", [128, 4 * 256], F32))
    s_pq = e(nc.sbuf_tensor("pqbuf", [128, 7 * 256], BF16))
    s_pT = e(nc.sbuf_tensor("pT", [128, 3 * 256], BF16))
    s_stat = e(nc.sbuf_tensor("stat", [128, 80], F32))
    s_ofm = e(nc.sbuf_tensor("ofm", [128, 512], F32))
    s_wfcb = e(nc.sbuf_tensor("wfcb", [128, 3 * 1024], BF16))
    s_wqb = e(nc.sbuf_tensor("wqb", [128, 2 * 2048], BF16))
    s_wvb = e(nc.sbuf_tensor("wvb", [128, 2 * 2048], BF16))
    s_wprb = e(nc.sbuf_tensor("wprb", [128, 3 * 4096], BF16))
    s_woutb = e(nc.sbuf_tensor("woutb", [128, 3 * 1024], BF16))

    NEGMAX, ROWSUM, RECIP = 0, 52, 64
    SUM, SSQ, MU, MSQ, VARP, R1, V2P, ASC, NMA, T0 = 8, 12, 16, 20, 24, 28, 32, 36, 40, 44

    M = Prog(cnt)
    M.fence_snap = {}
    for eng in ENGS:
        M.emit(eng, lambda E: E.nop(), waits=[("BAR", cnt["BAR"])])

    bank_free = [[] for _ in range(8)]

    def bmm(out_bank, cols, lhsT, rhs, start, stop, tp=None, waits=(), prow=None):
        w = list(waits)
        if start:
            w += bank_free[out_bank]
        ap = pb[out_bank][:, cols[0]:cols[1]] if prow is None else pb[out_bank][prow[0]:prow[1], cols[0]:cols[1]]
        tok = mm(M, ap, lhsT, rhs, start=start, stop=stop, tp=tp, waits=w)
        if start:
            bank_free[out_bank] = []
        return tok

    def btrans(out_bank, cols, in_ap, start=True, stop=True, waits=(), bf16=False):
        w = list(waits) + (bank_free[out_bank] if start else [])
        if bf16:
            out_ap = pb[out_bank][:, cols[0] // 2: cols[1] // 2].bitcast(BF16)
            ident = s_identb[:]
        else:
            out_ap = pb[out_bank][:, cols[0]:cols[1]]
            ident = s_identf[:]
        tok = trans(M, out_ap, in_ap, ident, start=start, stop=stop, waits=w)
        if start:
            bank_free[out_bank] = []
        return tok

    def bread(out_bank, sem, tok):
        bank_free[out_bank] = [(sem, tok)]

    wout_cons = [0, 0, 0]
    wfc_cons = [0, 0, 0]
    wpr_cons = [0, 0, 0]
    wq_cons = [0, 0]
    wv_cons = [0, 0]
    xnb_cons = [0, 0]
    xn1_cons = 0
    xn2_cons = 0
    qk_write_guard = [0, 0]
    v_write_guard = 0
    ao_guard = 0
    h_guard = 0
    ofm_trans_last = [0]
    ofm_cons = [0, 0]
    prev_expC = [0] * 4
    prev_trE = [0] * 7
    prev_H = [("V", 0), ("S", 0)]
    store_tok = [0, 0]
    kp_tok = 0
    vp_tok = 0
    st = {}

    def emit_stats(wait_sem, wait_tok):
        tv = ts = None
        for g in range(G):
            xb = st["xbase"]
            tv = M.emit("vector",
                        lambda E, g=g, xb=xb: E.tensor_scalar(out=s_xn1[:, 0:1024],
                                                              in0=s_x[:, xb + g * D: xb + (g + 1) * D],
                                                              scalar1=1.0, scalar2=None,
                                                              op0=ALU.mult, op1=ALU.add,
                                                              accum_out=s_stat[:, SUM + g:SUM + g + 1]),
                        waits=[(wait_sem, wait_tok), ("T", xn1_cons)], inc="V")
            ts = M.emit("scalar",
                        lambda E, g=g, xb=xb: E.activation(out=s_xn1[:, 1024:2048],
                                                           in_=s_x[:, xb + g * D: xb + (g + 1) * D],
                                                           func=AF.Square,
                                                           accum_out=s_stat[:, SSQ + g:SSQ + g + 1]),
                        waits=[(wait_sem, wait_tok), ("T", xn1_cons)], inc="S")
        c = 1.0 / 1024.0
        tv = M.emit("vector", lambda E: E.tensor_scalar_mul(s_stat[:, MU:MU + 4], s_stat[:, SUM:SUM + 4], c),
                    waits=[("V", tv)], inc="V")
        tv = M.emit("vector", lambda E: E.tensor_scalar_mul(s_stat[:, MSQ:MSQ + 4], s_stat[:, SSQ:SSQ + 4], c),
                    waits=[("S", ts)], inc="V")
        tv = M.emit("vector", lambda E: E.tensor_tensor(out=s_stat[:, T0:T0 + 4], in0=s_stat[:, MU:MU + 4],
                                                        in1=s_stat[:, MU:MU + 4], op=ALU.mult), inc="V")
        tv = M.emit("vector", lambda E: E.tensor_tensor(out=s_stat[:, VARP:VARP + 4], in0=s_stat[:, MSQ:MSQ + 4],
                                                        in1=s_stat[:, T0:T0 + 4], op=ALU.subtract), inc="V")
        tv = M.emit("vector", lambda E: E.tensor_scalar_add(s_stat[:, VARP:VARP + 4], s_stat[:, VARP:VARP + 4],
                                                            1e-5), inc="V")
        ts2 = M.emit("scalar", lambda E: E.activation(out=s_stat[:, T0:T0 + 4], in_=s_stat[:, VARP:VARP + 4],
                                                      func=AF.Ln),
                     waits=[("V", tv)], inc="S")
        ts2 = M.emit("scalar", lambda E: E.activation(out=s_stat[:, R1:R1 + 4], in_=s_stat[:, T0:T0 + 4],
                                                      func=AF.Exp, scale=s_cm05[:]), inc="S")
        return tv, ts2

    def emit_norm_transpose(dst, a_col, nma_col, wait_s_tok):
        ev_last = None
        xb = st["xbase"]
        for g in range(G):
            tok_x = M.emit("vector",
                           lambda E, g=g, xb=xb, a_col=a_col, nma_col=nma_col:
                           E.tensor_scalar(out=s_xnb[:, 0:D],
                                           in0=s_x[:, xb + g * D: xb + (g + 1) * D],
                                           scalar1=s_stat[:, a_col + g:a_col + g + 1],
                                           scalar2=s_stat[:, nma_col + g:nma_col + g + 1],
                                           op0=ALU.mult, op1=ALU.add),
                           waits=[("T", xnb_cons[0]), ("S", wait_s_tok)], inc="V")
            tt = None
            for half in range(2):
                bank = half
                for i4 in range(4):
                    dc = half * 4 + i4
                    tt = btrans(bank, (i4 * 128, (i4 + 1) * 128),
                                s_xnb[:, dc * 128:(dc + 1) * 128],
                                start=True, stop=True,
                                waits=[("V", tok_x)], bf16=True)
                dst_ap = bass.AP(dst, half * 4 * TILE + g * 128,
                                 [[DC * TILE, 128], [TILE, 4], [1, 128]])
                tok = M.emit("scalar",
                             lambda E, bank=bank, dst_ap=dst_ap:
                             E.copy(out=dst_ap, in_=pb[bank][:, 0:256].bitcast(BF16)
                                    .rearrange("p (a b) -> p a b", b=128)),
                             waits=[("T", tt)], inc="S")
                bread(bank, "S", tok)
                ev_last = tok
            xnb_cons[0] = tt
        return ev_last

    def emit_qkv(t, xn1_ev):
        qk_ev = []
        for op in range(8):
            sl = op % 2
            tok_w = M.emit("gpsimd",
                           lambda E, op=op, sl=sl: E.dma_start(
                               out=s_wqb[:, sl * 2048:(sl + 1) * 2048].rearrange("p (a d) -> p a d", d=1024),
                               in_=d_wqs[op * 256:(op + 1) * 256, :].rearrange("(a p) d -> p a d", p=128)),
                           waits=[("T", wq_cons[sl])], inc="wq%d" % sl, n=16)
            for j in range(2):
                oc = 2 * op + j
                bank = 2 + (oc % 2)
                tt = None
                for dc in range(DC):
                    tt = bmm(bank, (0, 512),
                             s_wqb[:, sl * 2048 + j * 1024 + dc * 128: sl * 2048 + j * 1024 + (dc + 1) * 128],
                             s_xn1[:, dc * TILE:(dc + 1) * TILE],
                             start=(dc == 0), stop=(dc == DC - 1),
                             waits=[("wq%d" % sl, tok_w), ("S", xn1_ev),
                                    ("T", qk_write_guard[0]), ("V", kp_tok)])
                dst = s_q if oc < 8 else s_k
                fc = oc % 8
                tok = M.emit("scalar",
                             lambda E, dst=dst, fc=fc, oc=oc, bank=bank:
                             E.activation(out=dst[:, fc * TILE:(fc + 1) * TILE], in_=pb[bank][:],
                                          func=AF.Identity, bias=s_bw[:, oc:oc + 1]),
                             waits=[("T", tt)], inc="S")
                bread(bank, "S", tok)
                qk_ev.append(tok)
            wq_cons[sl] = tt
        v_ev = [0] * G
        ttl = None
        for dp in range(4):
            sl = dp % 2
            tok_w = M.emit("gpsimd",
                           lambda E, dp=dp, sl=sl: E.dma_start(
                               out=s_wvb[:, sl * 2048:(sl + 1) * 2048].rearrange("p (a d) -> p a d", d=1024),
                               in_=d_wvs[dp * 256:(dp + 1) * 256, :].rearrange("(a p) d -> p a d", p=128)),
                           waits=[("T", wv_cons[sl])], inc="wv%d" % sl, n=16)
            for j in range(2):
                dc = 2 * dp + j
                for g in range(G):
                    for half in range(2):
                        bi = g * 2 + half
                        ttl = bmm(bi, (0, 512),
                                  s_xn1[:, dc * TILE + g * 128: dc * TILE + (g + 1) * 128],
                                  s_wvb[:, sl * 2048 + j * 1024 + half * 512: sl * 2048 + j * 1024 + (half + 1) * 512],
                                  start=(dc == 0), stop=False,
                                  waits=[("wv%d" % sl, tok_w), ("S", xn1_ev),
                                         ("T", v_write_guard), ("V", vp_tok)])
            wv_cons[sl] = ttl
        for g in range(G):
            for half in range(2):
                bi = g * 2 + half
                tt = bmm(bi, (0, 512), s_onesb[:], s_bwv[0:1, half * 512:(half + 1) * 512],
                         start=False, stop=True)
                tok = M.emit("vector",
                             lambda E, g=g, half=half, bi=bi:
                             E.tensor_copy(out=s_v[:, g * D + half * 512: g * D + (half + 1) * 512],
                                           in_=pb[bi][:]),
                             waits=[("T", tt)], inc="V")
                bread(bi, "V", tok)
                v_ev[g] = tok
        return qk_ev, v_ev

    def make_attention_steps(t, qk_ev, v_ev):
        seq_start = (t % SEQ_TILES == 0)
        NI = G * H
        ctx = []
        for w in range(G):
            fw = (seq_start and w == 0)
            for h in range(H):
                ctx.append((w, h, fw))
        tokA = [0] * NI; tokB = [0] * NI; tokC = [0] * NI; tokD = [0] * NI
        tokE = [0] * NI; tokF = [0] * NI; tokG = [0] * NI; tokH = [0] * NI
        engH = [0] * NI; engF = [0] * NI
        res = {"tokG": tokG, "tokH": tokH, "engH": engH, "tokA": tokA, "tokB": tokB,
               "tokC": tokC, "tokE": tokE}

        def stA(i):
            w, h, fw = ctx[i]
            po = 64 * (h % 2); fc = h // 2
            qs = s_q[po:po + 64, fc * TILE + w * 128: fc * TILE + (w + 1) * 128]
            bank = i % 2
            wts = [("S", qk_ev[8 + fc]), ("V", tokB[i - 2] if i >= 2 else 0)]
            if fw:
                tokA[i] = bmm(bank, (0, 128), qs, s_k[po:po + 64, fc * TILE: fc * TILE + 128],
                              start=True, stop=True, tp=(po, 0), waits=wts)
            elif w == 0:
                bmm(bank, (0, 128), qs, s_kprev[po:po + 64, fc * 128:(fc + 1) * 128],
                    start=True, stop=True, tp=(po, 0), waits=wts)
                tokA[i] = bmm(bank, (128, 256), qs, s_k[po:po + 64, fc * TILE: fc * TILE + 128],
                              start=True, stop=True, tp=(po, 0))
            else:
                tokA[i] = bmm(bank, (0, 256), qs,
                              s_k[po:po + 64, fc * TILE + (w - 1) * 128: fc * TILE + (w + 1) * 128],
                              start=True, stop=True, tp=(po, 0), waits=wts)

        def stB(i):
            w, h, fw = ctx[i]
            nk = 128 if fw else 256
            ws = i % 4
            bank = i % 2
            mof = 128 if fw else 0
            tok = M.emit("vector",
                         lambda E, ws=ws, nk=nk, mof=mof, bank=bank:
                         E.tensor_tensor(out=s_sm[:, ws * 256: ws * 256 + nk], in0=pb[bank][:, 0:nk],
                                         in1=s_mask[:, mof:mof + nk], op=ALU.add),
                         waits=[("T", tokA[i]),
                                ("S", tokC[i - 4] if i >= 4 else prev_expC[i % 4])], inc="V")
            bread(bank, "V", tok)
            tokB[i] = M.emit("vector",
                             lambda E, ws=ws, nk=nk: E.reduce_max(out=s_stat[:, NEGMAX + ws:NEGMAX + ws + 1],
                                                                  in_=s_sm[:, ws * 256: ws * 256 + nk],
                                                                  axis=AX.X, negate=True),
                             inc="V")

        def stC(i):
            w, h, fw = ctx[i]
            nk = 128 if fw else 256
            ws = i % 4
            tokC[i] = M.emit("scalar",
                             lambda E, ws=ws, nk=nk: E.activation(out=s_sm[:, ws * 256: ws * 256 + nk],
                                                                  in_=s_sm[:, ws * 256: ws * 256 + nk],
                                                                  func=AF.Exp,
                                                                  bias=s_stat[:, NEGMAX + ws:NEGMAX + ws + 1],
                                                                  accum_out=s_stat[:, ROWSUM + ws:ROWSUM + ws + 1]),
                             waits=[("V", tokB[i])], inc="S")

        def stD(i):
            w, h, fw = ctx[i]
            nk = 128 if fw else 256
            ws = i % 4
            wh = i % 7
            M.emit("vector",
                   lambda E, ws=ws: E.reciprocal(out=s_stat[:, RECIP + ws:RECIP + ws + 1],
                                                 in_=s_stat[:, ROWSUM + ws:ROWSUM + ws + 1]),
                   waits=[("S", tokC[i])], inc="V")
            tokD[i] = M.emit("vector",
                             lambda E, ws=ws, wh=wh, nk=nk:
                             E.tensor_scalar_mul(s_pq[:, wh * 256: wh * 256 + nk],
                                                 s_sm[:, ws * 256: ws * 256 + nk],
                                                 s_stat[:, RECIP + ws:RECIP + ws + 1]),
                             waits=[("T", tokE[i - 7] if i >= 7 else prev_trE[i])], inc="V")

        def stE(i):
            w, h, fw = ctx[i]
            nk = 128 if fw else 256
            wh = i % 7
            tt2 = None
            tb = 2 + i % 2
            for kc in range(nk // 128):
                tt2 = btrans(tb, (kc * 128, (kc + 1) * 128),
                             s_pq[:, wh * 256 + kc * 128: wh * 256 + (kc + 1) * 128],
                             start=True, stop=True,
                             waits=[("V", tokD[i])], bf16=True)
            tokE[i] = tt2

        def stF(i):
            w, h, fw = ctx[i]
            nk = 128 if fw else 256
            wh = i % 7
            pslot = i % 3
            tb = 2 + i % 2
            fw_w = [("T", tokE[i]), ("T", tokG[i - 3] if i >= 3 else 0)]
            if i % 2 == 0:
                tokF[i] = M.emit("vector",
                                 lambda E, pslot=pslot, nk=nk, tb=tb:
                                 E.tensor_copy(out=s_pT[:, pslot * 256: pslot * 256 + nk],
                                               in_=pb[tb][:, 0:nk // 2].bitcast(BF16)),
                                 waits=fw_w, inc="V")
                engF[i] = 0
            else:
                tokF[i] = M.emit("scalar",
                                 lambda E, pslot=pslot, nk=nk, tb=tb:
                                 E.copy(out=s_pT[:, pslot * 256: pslot * 256 + nk],
                                        in_=pb[tb][:, 0:nk // 2].bitcast(BF16)),
                                 waits=fw_w, inc="S")
                engF[i] = 1
            bread(tb, ("V", "S")[engF[i]], tokF[i])

        def stG(i):
            w, h, fw = ctx[i]
            wh = i % 3
            hp = h - h % 2
            gb = 4 + i % 2
            pvw = [(("V", "S")[engF[i]], tokF[i]), ("V", v_ev[w])]
            if fw:
                tokG[i] = bmm(gb, (0, 128), s_v[:, w * D + hp * DH: w * D + hp * DH + 128],
                              s_pT[:, wh * 256: wh * 256 + 128],
                              start=True, stop=True, waits=pvw)
            else:
                vprev_ap = (s_vprev[:, hp * DH: hp * DH + 128] if w == 0
                            else s_v[:, (w - 1) * D + hp * DH: (w - 1) * D + hp * DH + 128])
                bmm(gb, (0, 128), vprev_ap, s_pT[:, wh * 256: wh * 256 + 128],
                    start=True, stop=False, waits=pvw)
                tokG[i] = bmm(gb, (0, 128), s_v[:, w * D + hp * DH: w * D + hp * DH + 128],
                              s_pT[:, wh * 256 + 128: wh * 256 + 256],
                              start=False, stop=True)

        def stH(i):
            w, h, fw = ctx[i]
            po = 64 * (h % 2); fc = h // 2
            gb = 4 + i % 2
            if i % 2 == 0:
                tokH[i] = M.emit("vector",
                                 lambda E, po=po, fc=fc, w=w, gb=gb:
                                 E.tensor_copy(out=s_ao[po:po + 64, fc * TILE + w * 128: fc * TILE + (w + 1) * 128],
                                               in_=pb[gb][po:po + 64, 0:128]),
                                 waits=[("T", tokG[i]), ("T", ao_guard)], inc="V")
                engH[i] = 0
                bread(gb, "V", tokH[i])
            else:
                tokH[i] = M.emit("scalar",
                                 lambda E, po=po, fc=fc, w=w, gb=gb:
                                 E.copy(out=s_ao[po:po + 64, fc * TILE + w * 128: fc * TILE + (w + 1) * 128],
                                        in_=pb[gb][po:po + 64, 0:128]),
                                 waits=[("T", tokG[i]), ("T", ao_guard)], inc="S")
                engH[i] = 1
                bread(gb, "S", tokH[i])

        stages = (stA, stB, stC, stD, stE, stF, stG, stH)
        poffs = (0, 1, 2, 3, 4, 5, 6, 7)
        NP = NI // 2
        steps = []
        for jp in range(NP + poffs[-1]):
            def step(jp=jp):
                for k in range(len(stages) - 1, -1, -1):
                    ip = jp - poffs[k]
                    if 0 <= ip < NP:
                        stages[k](2 * ip)
                        stages[k](2 * ip + 1)
            steps.append(step)
        return steps, res

    def emit_ffn_groups(tp):
        groups = []
        xb = (tp % 2) * G * D
        h_toks = st.setdefault("h_toks", {})
        h_toks.clear()

        def mk_fc(c):
            def f():
                sl = c % 3
                tok_w = M.emit("gpsimd",
                               lambda E, c=c, sl=sl: E.dma_start(out=s_wfcb[:, sl * 1024:(sl + 1) * 1024],
                                                                 in_=d_wfcT[c * 128:(c + 1) * 128, :]),
                               waits=[("T", wfc_cons[sl])], inc="wf%d" % sl, n=16)
                bank = 6 + (c % 2)
                tt = None
                for dc in range(DC):
                    tt = bmm(bank, (0, 512),
                             s_wfcb[:, sl * 1024 + dc * 128: sl * 1024 + (dc + 1) * 128],
                             s_xn2[:, dc * TILE:(dc + 1) * TILE],
                             start=(dc == 0), stop=(dc == DC - 1),
                             waits=[("wf%d" % sl, tok_w), ("S", st["xn2_ev"]),
                                    ("T", h_guard_g[0])])
                wfc_cons[sl] = tt
                tok = M.emit("scalar",
                             lambda E, c=c, bank=bank: E.activation(out=s_h[:, c * TILE:(c + 1) * TILE],
                                                                    in_=pb[bank][:], func=AF.Identity,
                                                                    bias=s_bfc[:, c:c + 1]),
                             waits=[("T", tt)], inc="S")
                bread(bank, "S", tok)
                h_toks[c] = tok
            return f

        def mk_gelu_batch(c0, c1):
            def f():
                tok = None
                for c in range(c0, c1):
                    tok = M.emit("scalar",
                                 lambda E, c=c: E.activation(out=s_h[:, c * TILE:(c + 1) * TILE],
                                                             in_=s_h[:, c * TILE:(c + 1) * TILE],
                                                             func=AF.Gelu),
                                 waits=[("S", h_toks[c1 - 1])] if c == c0 else [], inc="S")
                h_toks[c1 - 1] = tok
            return f

        def mk_proj(do):
            def f():
                sl = do % 3
                tok_w = M.emit("gpsimd",
                               lambda E, do=do, sl=sl: E.dma_start(out=s_wprb[:, sl * 4096:(sl + 1) * 4096],
                                                                   in_=d_wprT[do * 128:(do + 1) * 128, :]),
                               waits=[("T", wpr_cons[sl])], inc="wp%d" % sl, n=16)
                tt = None
                for c in range(HC):
                    tt = bmm(6, (0, 512),
                             s_wprb[:, sl * 4096 + c * 128: sl * 4096 + (c + 1) * 128],
                             s_h[:, c * TILE:(c + 1) * TILE],
                             start=(c == 0), stop=(c == HC - 1),
                             waits=[("wp%d" % sl, tok_w), ("S", h_toks[HC - 1])])
                wpr_cons[sl] = tt
                st["pass2_last"] = tt
                tok_ev = M.emit("scalar",
                                lambda E, do=do: E.activation(out=s_ofm[:, 0:512], in_=pb[6][:],
                                                              func=AF.Identity, bias=s_bpr[:, do:do + 1]),
                                waits=[("T", tt), ("T", ofm_trans_last[0])], inc="S")
                bread(6, "S", tok_ev)
                tt2 = None
                for g in range(G):
                    tt2 = btrans(7, (g * 128, (g + 1) * 128),
                                 s_ofm[:, g * 128:(g + 1) * 128],
                                 start=True, stop=True, waits=[("S", tok_ev)])
                ofm_trans_last[0] = tt2
                tok_res = M.emit("vector",
                                 lambda E, do=do, xb=xb:
                                 E.scalar_tensor_tensor(
                                     out=bass.AP(s_x, xb + do * 128, [[2 * G * D, 128], [D, 4], [1, 128]]),
                                     in0=pb[7][:].rearrange("p (g c) -> p g c", c=128), scalar=1.0,
                                     in1=bass.AP(s_x, xb + do * 128, [[2 * G * D, 128], [D, 4], [1, 128]]),
                                     op0=ALU.mult, op1=ALU.add),
                                 waits=[("T", tt2)], inc="V")
                bread(7, "V", tok_res)
                st["last_res"] = tok_res
            return f

        for c in range(HC // 2):
            groups.append(mk_fc(c))
        groups.append(mk_gelu_batch(0, HC // 2))
        for c in range(HC // 2, HC):
            groups.append(mk_fc(c))
        groups.append(mk_gelu_batch(HC // 2, HC))
        for do in range(DC):
            groups.append(mk_proj(do))

        def mk_store():
            def f():
                par = tp % 2
                store_tok[par] = M.emit("sync",
                                        lambda E, tp=tp, xb=xb: E.dma_start(
                                            out=d_y[tp * TILE:(tp + 1) * TILE, :].rearrange("(g p) d -> p g d", p=128),
                                            in_=s_x[:, xb:xb + G * D].rearrange("p (g d) -> p g d", d=D)),
                                        waits=[("V", st["last_res"])], inc="dy%d" % par, n=16)
            return f
        groups.append(mk_store())
        return groups

    h_guard_g = [0]

    def emit_outproj(t, pv_last_waits):
        tt = None
        for ic in range(DC):
            sl = ic % 3
            tok_w = M.emit("gpsimd",
                           lambda E, ic=ic, sl=sl: E.dma_start(out=s_woutb[:, sl * 1024:(sl + 1) * 1024],
                                                               in_=d_woutb[ic * 128:(ic + 1) * 128, :]),
                           waits=[("T", wout_cons[sl])], inc="wo%d" % sl, n=16)
            for g in range(G):
                for half in range(2):
                    tt = bmm(2 * g + half, (0, 512),
                             s_ao[:, ic * TILE + g * 128: ic * TILE + (g + 1) * 128],
                             s_woutb[:, sl * 1024 + half * 512: sl * 1024 + (half + 1) * 512],
                             start=(ic == 0), stop=(ic == DC - 1),
                             waits=[("wo%d" % sl, tok_w)] + (pv_last_waits if ic == 0 else []))
            wout_cons[sl] = tt
        tok_T = M.emit("tensor", lambda E: E.nop(), inc="T")
        tvlast = None
        xb = st["xbase"]
        for g in range(G):
            for half in range(2):
                bi = 2 * g + half
                tvlast = M.emit("vector",
                                lambda E, g=g, half=half, bi=bi, xb=xb:
                                E.scalar_tensor_tensor(
                                    out=s_x[:, xb + g * D + half * 512: xb + g * D + (half + 1) * 512],
                                    in0=pb[bi][:], scalar=1.0,
                                    in1=s_x[:, xb + g * D + half * 512: xb + g * D + (half + 1) * 512],
                                    op0=ALU.mult, op1=ALU.add),
                                waits=[("T", tok_T)], inc="V")
                bread(bi, "V", tvlast)
        return tok_T, tvlast

    pending_ffn = None
    for t in range(NT):
        st["xbase"] = (t % 2) * G * D
        par = t % 2
        tok_x = M.emit("sync",
                       lambda E, t=t, xb=st["xbase"]: E.dma_start(
                           out=s_x[:, xb:xb + G * D].rearrange("p (g d) -> p g d", d=D),
                           in_=d_x[t * TILE:(t + 1) * TILE, :].rearrange("(g p) d -> p g d", p=128)),
                       waits=[("dy%d" % par, store_tok[par])], inc="dx", n=16)

        tv, ts = emit_stats("dx", tok_x)
        T1 = 48
        tv = M.emit("vector", lambda E: E.reciprocal(out=s_stat[:, T1:T1 + 4], in_=s_stat[:, VARP:VARP + 4]),
                    inc="V")
        tv = M.emit("vector", lambda E: E.tensor_scalar_add(s_stat[:, V2P:V2P + 4], s_stat[:, VARP:VARP + 4],
                                                            -1e-5), inc="V")
        tv = M.emit("vector", lambda E: E.tensor_tensor(out=s_stat[:, V2P:V2P + 4], in0=s_stat[:, V2P:V2P + 4],
                                                        in1=s_stat[:, T1:T1 + 4], op=ALU.mult), inc="V")
        tv = M.emit("vector", lambda E: E.tensor_scalar_add(s_stat[:, V2P:V2P + 4], s_stat[:, V2P:V2P + 4],
                                                            1e-5), inc="V")
        ts = M.emit("scalar", lambda E: E.activation(out=s_stat[:, T1:T1 + 4], in_=s_stat[:, V2P:V2P + 4],
                                                     func=AF.Ln),
                    waits=[("V", tv), ("S", ts)], inc="S")
        ts = M.emit("scalar", lambda E: E.activation(out=s_stat[:, V2P:V2P + 4], in_=s_stat[:, T1:T1 + 4],
                                                     func=AF.Exp, scale=s_cm05[:]), inc="S")
        tv = M.emit("vector", lambda E: E.tensor_tensor(out=s_stat[:, ASC:ASC + 4], in0=s_stat[:, R1:R1 + 4],
                                                        in1=s_stat[:, V2P:V2P + 4], op=ALU.mult),
                    waits=[("S", ts)], inc="V")
        tv = M.emit("vector", lambda E: E.scalar_tensor_tensor(out=s_stat[:, NMA:NMA + 4],
                                                               in0=s_stat[:, MU:MU + 4],
                                                               scalar=-1.0, in1=s_stat[:, ASC:ASC + 4],
                                                               op0=ALU.mult, op1=ALU.mult), inc="V")
        xn1_ev = emit_norm_transpose(s_xn1, ASC, NMA, 0)

        qk_ev, v_ev = emit_qkv(t, xn1_ev)

        att_steps, att = make_attention_steps(t, qk_ev, v_ev)
        if pending_ffn is not None:
            ng, ns = len(pending_ffn), len(att_steps)
            si = 0
            lead = min(ns, 10)
            while si < lead:
                att_steps[si]()
                si += 1
            for gi, grp in enumerate(pending_ffn):
                grp()
                want = min(ns, lead + (gi + 1) * (ns - lead) // ng)
                while si < want:
                    att_steps[si]()
                    si += 1
            while si < ns:
                att_steps[si]()
                si += 1
        else:
            for stp in att_steps:
                stp()
        h_guard_g[0] = st.get("pass2_last", 0)

        NI = G * H
        qk_write_guard[0] = att["tokA"][NI - 1]
        v_write_guard = att["tokG"][NI - 1]
        for k in range(7):
            prev_trE[(NI - 7 + k) % 7] = att["tokE"][NI - 7 + k]
        for k in range(4):
            prev_expC[(NI - 4 + k) % 4] = att["tokC"][NI - 4 + k]

        if (t + 1) % SEQ_TILES != 0:
            kp_tok = M.emit("vector",
                            lambda E: E.tensor_copy(
                                out=s_kprev[:].rearrange("p (f c) -> p f c", c=128),
                                in_=s_k[:].rearrange("p (f t) -> p f t", t=TILE)[:, :, TILE - 128:TILE]),
                            waits=[("T", att["tokG"][15])], inc="V")
            vp_tok = M.emit("vector",
                            lambda E: E.tensor_copy(out=s_vprev[:], in_=s_v[:, (G - 1) * D: G * D]),
                            waits=[("T", att["tokG"][15])], inc="V")

        li = NI - 1
        pvw = [(("V", "S")[att["engH"][li]], att["tokH"][li]),
               (("V", "S")[att["engH"][li - 1]], att["tokH"][li - 1])]
        opj_T, tvlast = emit_outproj(t, pvw)
        ao_guard = opj_T

        tv2, ts2 = emit_stats("V", tvlast)
        tsn = None
        for g in range(G):
            tsn = M.emit("vector",
                         lambda E, g=g: E.scalar_tensor_tensor(out=s_stat[:, NMA + g:NMA + g + 1],
                                                               in0=s_stat[:, MU + g:MU + g + 1], scalar=-1.0,
                                                               in1=s_stat[:, R1 + g:R1 + g + 1],
                                                               op0=ALU.mult, op1=ALU.mult),
                         waits=[("S", ts2)], inc="V")
        st["xn2_ev"] = emit_norm_transpose(s_xn2, R1, NMA, 0)
        xn2_cons = 0
        pending_ffn = emit_ffn_groups(t)

    for grp in pending_ffn:
        grp()

    M.emit("sync", lambda E: E.nop(), waits=[("dy0", cnt.get("dy0", 0)), ("dy1", cnt.get("dy1", 0))])

    with nc.Block("main") as b2:
        M.replay(b2, sems)

    stack.close()
    return nc


_NC_CACHE = None


def kernel(L0_attn_x, L0_attn_ln_gamma, L0_attn_ln_beta, L0_attn_w_qkv, L0_attn_w_out,
           L0_ffn_W_fc, L0_ffn_b_fc, L0_ffn_A_fc, L0_ffn_B_fc,
           L0_ffn_W_proj, L0_ffn_b_proj, L0_ffn_A_proj, L0_ffn_B_proj):
    global _NC_CACHE
    x = np.ascontiguousarray(np.asarray(L0_attn_x, dtype=np.float32))
    B, N, Dm = x.shape
    ncores = 8
    per = B // ncores

    if _NC_CACHE is None:
        _NC_CACHE = build_program(dbg=DBG)
    nc = _NC_CACHE

    ident_f = np.eye(128, dtype=np.float32)
    ii = np.arange(128)[:, None]
    jj = np.arange(256)[None, :]
    prev = jj < 128
    jjm = np.where(prev, jj, jj - 128)
    bad = np.where(prev, jjm < ii, jjm > ii)
    mask_m = np.where(bad, np.float32(NEG), np.float32(0.0)).astype(BFNP)
    ones_f = np.ones((1, 128), dtype=np.float32)

    common = dict(
        gamma=np.asarray(L0_attn_ln_gamma, np.float32),
        beta=np.asarray(L0_attn_ln_beta, np.float32),
        wqkv=np.asarray(L0_attn_w_qkv, np.float32),
        wout=np.asarray(L0_attn_w_out, np.float32),
        wfc=np.asarray(L0_ffn_W_fc, np.float32),
        bfc=np.asarray(L0_ffn_b_fc, np.float32),
        afc=np.asarray(L0_ffn_A_fc, np.float32),
        bfcB=np.asarray(L0_ffn_B_fc, np.float32),
        wpr=np.asarray(L0_ffn_W_proj, np.float32),
        bpr=np.asarray(L0_ffn_b_proj, np.float32),
        apr=np.asarray(L0_ffn_A_proj, np.float32),
        bprB=np.asarray(L0_ffn_B_proj, np.float32),
        identf=ident_f, maskm=mask_m, onesf=ones_f,
        onesb=np.ones((1, 128), dtype=BFNP),
    )
    in_maps = []
    for c in range(ncores):
        m = dict(common)
        m["x"] = np.ascontiguousarray(x[c * per:(c + 1) * per].reshape(TOK, Dm))
        in_maps.append(m)

    res = run_bass_kernel_spmd(nc, in_maps, core_ids=list(range(ncores)))
    out = np.empty((B, N, Dm), np.float32)
    for c in range(ncores):
        out[c * per:(c + 1) * per] = res.results[c]["y"].reshape(per, N, Dm)
    if DBG:
        global DBG_RES
        DBG_RES = res.results
    return out
